# version 19
# baseline (speedup 1.0000x reference)
import numpy as np
import ml_dtypes
import concourse.bass as bass
import concourse.mybir as mybir
from concourse.bass_utils import run_bass_kernel_spmd

F32 = mybir.dt.float32
BF16 = mybir.dt.bfloat16
AF = mybir.ActivationFunctionType
ALU = mybir.AluOpType
AX = mybir.AxisListType

B, L, D = 2, 8192, 2048
DIN = 4096
H = 64
PH = 64
NST = 128
DIP = 8512
CONVD = DIN + 2 * NST
NCORES = 8
GROUP = 4
TCORE = (B * L) // NCORES
SS = 1024
NSS = TCORE // SS
CH = 128
NCH = SS // CH
NT = SS // 128
KD = D // 128
EPS_LN = 1e-5
EPS_RMS = 1e-5
EPS_L1 = 1e-9
SCL_OUT1 = float(128 ** -0.5)

ENGINES = ("tensor", "vector", "scalar", "gpsimd", "sync")


class Ctx:

    NSLOT = 12

    def __init__(self, nc):
        self.nc = nc
        self.plan = []
        self.sems = {}
        names = ["te", "ve", "se", "ge", "cc"]
        names += ["dy%d" % i for i in range(self.NSLOT)]
        names += ["dg%d" % i for i in range(self.NSLOT)]
        names += ["ds%d" % i for i in range(self.NSLOT)]
        self.semv = {n: 0 for n in names}
        for n in self.semv:
            self.sems[n] = nc.semaphore(n).__enter__()
        self.waited = {e: {n: 0 for n in self.semv} for e in ENGINES}
        self.engine_sem = {"tensor": "te", "vector": "ve", "scalar": "se",
                           "gpsimd": "ge", "sync": None}
        self.slot = {"sync": 0, "gpsimd": 0, "scalar": 0}

    def do(self, eng, fn, waits=None, inc=None, inck=1):
        wd = {}
        if waits:
            it = waits.items() if isinstance(waits, dict) else waits
            for s, v in it:
                if v > wd.get(s, 0):
                    wd[s] = v
        weff = {}
        for s, v in wd.items():
            if v > self.waited[eng][s]:
                weff[s] = v
                self.waited[eng][s] = v
        tok = None
        if inc:
            self.semv[inc] += inck
            tok = (inc, self.semv[inc])
        sems = self.sems

        def run(e, fn=fn, weff=weff, inc=inc, inck=inck):
            for s, v in weff.items():
                e.wait_ge(sems[s], v)
            ins = fn(e)
            if inc is not None:
                assert ins is not None
                ins.then_inc(sems[inc], inck)
        self.plan.append((eng, run))
        return tok

    def tok(self, sem):
        return (sem, self.semv[sem])

    def finalize(self):
        nc = self.nc
        with nc.Block() as block:
            for name in ENGINES:
                def make(name):
                    def body(e):
                        for eng, fn in self.plan:
                            if eng == name:
                                fn(e)
                    return body
                getattr(block, name)(make(name))


class Buf:

    def __init__(self, ap, base=None):
        self.ap_ = ap
        self._st = base._st if base is not None else {"w": None, "r": []}

    def sub(self, key):
        return Buf(self.ap_[key], base=self)

    def __getitem__(self, key):
        return self.ap_[key]

    @property
    def ap(self):
        return self.ap_


def rd(bufs):
    w = []
    for b in bufs:
        if b._st["w"]:
            w.append(b._st["w"])
    return w


def wr(bufs):
    w = []
    for b in bufs:
        if b._st["w"]:
            w.append(b._st["w"])
        w.extend(b._st["r"])
    return w


def mark(tok, reads=(), writes=()):
    for b in reads:
        b._st["r"].append(tok)
    for b in writes:
        b._st["w"] = tok
        b._st["r"] = []


def build(debug_names=(), dump_scratch=False):
    nc = bass.Bass("TRN2", target_bir_lowering=False)
    cx = Ctx(nc)

    def din(name, shape, dt=F32):
        return nc.dram_tensor(name, shape, dt, kind="ExternalInput").ap()

    u_d = din("u", [128 + TCORE, D])
    winT = din("winT", [128, 66 * 2048 + 1024])
    woutT = din("woutT", [128, KD * 4096])
    wqkvT = din("wqkvT", [128, 48 * 2048])
    wout1T = din("wout1T", [128, KD * 2048])
    wf0 = din("wf0", [D, D])
    wf1 = din("wf1", [D, D])
    wconv_d = din("wconv", [128, 34 * 4])
    bconv_d = din("bconv", [128, 34])
    dtb_d = din("dtb", [H, 1])
    alog_d = din("alog", [H, 1])
    dpcol_d = din("dpcol", [128, H])
    normw_d = din("normw", [128, 32])
    gamma_d = din("gamma", [128, KD])
    beta_d = din("beta", [128, KD])
    onesf_d = din("onesf", [1, 128])
    f0b_d = din("f0b", [128, KD])
    f0a_d = din("f0a", [128, KD])
    f1b_d = din("f1b", [128, KD])
    ident_d = din("ident", [128, 128])
    triu_d = din("triu", [128, 128])
    negm_d = din("negm", [128, 128])
    mask8_d = din("mask8", [128, 8])
    ones_d = din("ones", [1, 128])

    out_d = nc.dram_tensor("out", [TCORE, D], F32, kind="ExternalOutput").ap()
    dbg = {}
    for nm, shp in debug_names:
        dbg[nm] = nc.dram_tensor("dbg_" + nm, shp, F32, kind="ExternalOutput").ap()

    _scr_kind = "ExternalOutput" if dump_scratch else "Internal"

    def dsc(name, shape, dt):
        return nc.dram_tensor(name, shape, dt, kind=_scr_kind).ap()

    sc_ypart = dsc("sc_ypart", [TCORE, DIN], BF16)
    sc_z = dsc("sc_z", [TCORE, DIN], BF16)
    sc_c = dsc("sc_c", [NST, TCORE], BF16)
    sc_g0 = dsc("sc_g0", [D, TCORE], BF16)
    sc_csg = dsc("sc_csg", [H, TCORE], F32)
    sc_q = dsc("sc_q", [D, TCORE], BF16)
    sc_kvc = dsc("sc_kvc", [D, TCORE], BF16)
    sc_kc = dsc("sc_kc", [D, TCORE], BF16)
    sc_g1 = dsc("sc_g1", [D, TCORE], BF16)
    sc_x1 = dsc("sc_x1", [TCORE, D], BF16)
    sc_qwT = dsc("sc_qwT", [128, KD * 2048], BF16)
    sc_sw0T = dsc("sc_sw0T", [128, KD * 2048], BF16)
    sc_sg = dsc("sc_sg", [D, TCORE], BF16)
    dbg_sin = dsc("dbg_sin", [NST, DIN], F32)
    dbg_y = dsc("dbg_y", [128, DIN], F32)
    dbg_t2 = dsc("dbg_t2", [128, DIN], F32)
    dbg_sz = dsc("dbg_sz", [128, DIN], F32)
    dbg_yn = dsc("dbg_yn", [128, DIN], F32)
    dbg_sg = dsc("dbg_sg", [128, SS], F32)
    m_woutT = nc.dram_tensor("m_woutT", [128, KD * 4096], BF16).ap()
    m_wqkvT = nc.dram_tensor("m_wqkvT", [128, 48 * 2048], BF16).ap()
    m_wout1T = nc.dram_tensor("m_wout1T", [128, KD * 2048], BF16).ap()
    ag1s_i = nc.dram_tensor("ag1s_i", [NST, DIN], BF16).ap()
    ag1s_o = nc.dram_tensor("ag1s_o", [GROUP * NST, DIN], BF16).ap()
    ag1d_i = nc.dram_tensor("ag1d_i", [1, H], F32).ap()
    ag1d_o = nc.dram_tensor("ag1d_o", [GROUP, H], F32).ap()
    ag2_i = nc.dram_tensor("ag2_i", [2, D], F32).ap()
    ag2_o = nc.dram_tensor("ag2_o", [GROUP * 2, D], F32).ap()

    def sb(name, shape, dt):
        return Buf(nc.alloc_sbuf_tensor(name, shape, dt).ap())

    arA = sb("arA", [128, 32768], BF16)
    arB = sb("arB", [128, 18432], BF16)
    st_bf = sb("state", [128, 4096], BF16)
    c_id_bf = sb("c_id_bf", [128, 128], BF16)
    c_id_f = sb("c_id_f", [64, 64], F32)
    c_triu = sb("c_triu", [128, 128], BF16)
    c_ones = sb("c_ones", [34, 128], BF16)
    c_dp = sb("c_dp", [128, H], BF16)
    c_f0b = sb("c_f0b", [128, KD], F32)
    c_f0a = sb("c_f0a", [128, KD], F32)
    c_f1b = sb("c_f1b", [128, KD], F32)
    c_wcv = sb("c_wcv", [128, 34 * 4], F32)
    c_bcv = sb("c_bcv", [128, 34], F32)
    c_dtb = sb("c_dtb", [H, 1], F32)
    c_A = sb("c_A", [H, 1], F32)
    c_m8 = sb("c_m8", [128, 8], F32)
    c_nwc = sb("c_nwc", [128, 32], F32)
    c_gc = sb("c_gc", [128, KD], F32)
    c_bc = sb("c_bc", [128, KD], F32)
    c_epsln = sb("c_epsln", [128, 1], F32)
    c_epsl1 = sb("c_epsl1", [128, 1], F32)
    c_one = sb("c_one", [128, 1], F32)
    b_u = sb("b_u", [128, D], F32)
    b_x1 = sb("b_x1", [128, D], F32)
    b_xn = sb("b_xn", [128, D], BF16)
    b_v2 = sb("b_v2", [128, D], BF16)
    b_st1 = sb("b_st1", [128, 1], F32)
    b_st2 = sb("b_st2", [128, 1], F32)
    b_m = sb("b_m", [128, 1], F32)
    b_r = sb("b_r", [128, 1], F32)
    b_m2 = sb("b_m2", [128, 1], F32)
    b_r2 = sb("b_r2", [128, 1], F32)
    b_sc1 = sb("b_sc1", [128, 1], F32)
    b_sc2 = sb("b_sc2", [128, 1], F32)
    b_st1B = sb("b_st1B", [128, 1], F32)
    b_st2B = sb("b_st2B", [128, 1], F32)
    b_mB = sb("b_mB", [128, 1], F32)
    b_rB = sb("b_rB", [128, 1], F32)
    b_sc1B = sb("b_sc1B", [128, 1], F32)
    b_sc2B = sb("b_sc2B", [128, 1], F32)
    b_w0 = sb("b_w0", [128, 2048], BF16)
    b_w1 = sb("b_w1", [128, 2048], BF16)
    Q1 = sb("Q1", [128, 4096], BF16)
    Q2 = sb("Q2", [128, 4096], BF16)
    Q3 = sb("Q3", [128, 4096], BF16)
    Q4 = sb("Q4", [128, 4096], BF16)
    b_dx = Q1
    b_ypl = Q1
    b_yn = Q1
    b_dpx = Q2
    b_zl = Q2
    b_y = Q2
    b_yp = Q3
    b_sq = Q3
    b_dump = Q3.sub((slice(None), slice(0, 2048)))
    b_x1c = Buf(Q3.ap[:, 2048:4096])
    b_t2 = Q4
    b_mt = Q4.sub((slice(None), slice(0, 2048)))
    b_qw = Q4.sub((slice(None), slice(0, 2048)))
    E1 = sb("E1", [128, 1152], BF16)
    E2 = sb("E2", [128, 1024], BF16)
    E3 = sb("E3", [128, 1024], BF16)
    E4 = sb("E4", [128, 1024], BF16)
    _k1 = (slice(None), slice(0, 1024))
    b_xbct = E1
    b_sg1 = E1.sub(_k1)
    b_xt = E2
    b_e = E2.sub(_k1)
    b_kv = E2.sub(_k1)
    b_s1 = E2.sub(_k1)
    b_ecl = E3.sub(_k1)
    b_kphi = E3.sub(_k1)
    b_g0b = E3.sub(_k1)
    b_zt = E3.sub((slice(None), slice(0, 512)))
    b_qphi = E4.sub(_k1)
    b_exps = E4.sub(_k1)
    F1 = sb("F1", [128, 1024], F32)
    F2 = sb("F2", [128, 1024], F32)
    b_ca0 = F1
    b_g0t = F1
    b_n1 = F1
    b_ca1 = F2
    b_kvc = F2
    b_d1 = F2
    b_dtraw = Buf(Q1.ap.bitcast(F32)[0:H, 0:1152], base=Q1)
    b_adt = Buf(Q2.ap.bitcast(F32)[0:H, 0:1024], base=Q2)
    b_csg128 = sb("b_csg128", [128, 1024], F32)
    b_csg = Buf(b_csg128.ap[0:H, :], base=b_csg128)
    b_csgc = sb("b_csgc", [H, 1], F32)
    b_cs = sb("b_cs", [H, 128], F32)
    b_e1 = sb("b_e1", [H, 128], F32)
    b_e2 = sb("b_e2", [H, 128], F32)
    b_dd = sb("b_dd", [H, 128], F32)
    b_ds = sb("b_ds", [H, 128], F32)
    b_nc0 = sb("b_nc0", [H, 1], F32)
    b_dec4 = sb("b_dec4", [128, 128], F32)
    b_dtot = sb("b_dtot", [128, H], F32)
    b_smb = sb("b_smb", [128, 128], BF16)
    b_cshi = sb("b_cshi", [H, 128], BF16)
    b_decb = sb("b_decb", [128, 128], BF16)
    b_csB = sb("b_csB", [H, 128], F32)
    b_cshiB = sb("b_cshiB", [H, 128], BF16)
    b_csloB = sb("b_csloB", [H, 128], BF16)
    b_ddB = sb("b_ddB", [H, 128], F32)
    b_dsB = sb("b_dsB", [H, 128], F32)
    b_dec4B = sb("b_dec4B", [128, 128], F32)
    b_decbB = sb("b_decbB", [128, 128], BF16)
    b_dtotB = sb("b_dtotB", [128, H], F32)
    b_smbB = Buf(Q4.ap[:, 2048:2176], base=Q4)
    b_BtmB = Buf(Q4.ap[:, 2176:2304], base=Q4)
    b_e1B = sb("b_e1B", [H, 128], F32)
    b_cslo = sb("b_cslo", [H, 128], BF16)
    b_fhl0 = Buf(b_u.ap.bitcast(BF16)[0:34, :], base=b_u)
    b_fhl1 = Buf(b_x1.ap.bitcast(BF16)[0:34, :], base=b_x1)
    c_nones = sb("c_nones", [34, 128], BF16)
    c_negm = sb("c_negm", [128, 128], BF16)
    b_u1 = Buf(F1.ap[:, 0:512], base=F1)
    b_u2 = Buf(F2.ap[:, 0:512], base=F2)
    b_u3 = b_u1
    b_su = Buf(E4.ap[:, 0:512], base=E4)
    b_Bfm = sb("b_Bfm", [128, 1024], BF16)
    b_Cfm = sb("b_Cfm", [128, 1024], BF16)
    b_Btm = sb("b_Btm", [128, 128], BF16)
    b_cfm2 = b_Bfm
    b_thr = sb("b_thr", [128, 1], F32)
    b_wq = b_u
    b_wcl = b_x1
    b_gst = Buf(F1.ap[0:4, 0:64], base=F1)
    b_dtr = Buf(b_csg128.ap[64:65, 128:192], base=b_csg128)
    b_dmod = Buf(F1.ap[:, 128:192], base=F1)
    b_offk = Buf(E1.ap.bitcast(F32)[:, 512:528], base=E1)
    b_offkv = Buf(E1.ap.bitcast(F32)[:, 528:544], base=E1)
    b_ckv = sb("b_ckv", [128, KD], F32)
    b_ckc = sb("b_ckc", [128, KD], F32)
    b_x2 = b_x1
    b_dtg = b_cs
    b_dcg = Buf(b_dtot.ap[:, 0:H], base=b_dtot)
    b_gk = Buf(b_ckv.ap[:, :], base=b_ckv)
    c_onesf = Buf(b_csg128.ap[64:65, 0:128], base=b_csg128)

    def ps(name, shape, dt):
        return Buf(nc.alloc_psum_tensor(name, shape, dt))

    PBIG = ps("PBIG", [128, 2048], F32)
    PYD = ps("PYD", [128, 512], F32)
    PYO = ps("PYO", [128, 512], F32)
    PTRa = ps("PTRa", [128, 512], F32)
    PTRb = ps("PTRb", [128, 512], F32)

    for _b in (PBIG, PYD, PYO, PTRa, PTRb):
        _b.ap_ = _b.ap_.ap()
    BA = Buf(PBIG.ap[:, 0:1024])
    BB = Buf(PBIG.ap[:, 1024:2048])
    BPD = PYD.sub((slice(None), slice(0, 512)))
    PD = BPD

    def pjoin():
        tok = cx.do(VE, lambda e: e.engine_nop(),
                    waits=wr([BA, BB, BPD, PTRa, PTRb]), inc="ve")
        mark(tok, writes=[BA, BB, BPD, PTRa, PTRb])

    TN, VE, SC, GP, SY = "tensor", "vector", "scalar", "gpsimd", "sync"
    ESEM = {TN: "te", VE: "ve", SC: "se", GP: "ge"}

    def step(eng, fn, reads=(), writes=(), inc=None, inck=1):
        waits = rd(reads) + wr(writes)
        if inc is None:
            inc, inck = ESEM[eng], 1
        tok = cx.do(eng, fn, waits=waits, inc=inc, inck=inck)
        mark(tok, reads=reads, writes=writes)
        return tok

    def dma(q, dst_ap, src_ap, reads=(), writes=()):
        i = cx.slot[q]
        cx.slot[q] = (i + 1) % Ctx.NSLOT
        sem = {"sync": "dy%d", "gpsimd": "dg%d", "scalar": "ds%d"}[q] % i
        extra = {sem: cx.semv[sem]} if cx.semv[sem] else None
        waits = rd(reads) + wr(writes) + (list(extra.items()) if extra else [])
        tok = cx.do(q, lambda e, d=dst_ap, s=src_ap: e.dma_start(out=d, in_=s),
                    waits=waits, inc=sem, inck=16)
        mark(tok, reads=reads, writes=writes)
        return tok

    def brep(ap2d, rep):
        return ap2d.unsqueeze(2).broadcast_to([ap2d.shape[0], ap2d.shape[1], rep])

    def r3(ap2d, inner):
        return ap2d.rearrange("p (a b) -> p a b", b=inner)

    vA_x = arA.ap.rearrange("p (t c) -> p t c", c=DIN)
    vA_sgn = arA.ap[:, 0:16 * SS].rearrange("p (k c) -> p k c", c=SS)
    vA_wz = arA.ap.rearrange("p (k c) -> p k c", c=2048)
    vA_ynfm = arA.ap.rearrange("p (k c) -> p k c", c=SS)
    vA_sgtm = arA.ap[:, 0:8 * D].rearrange("p (t c) -> p t c", c=D)
    vB_xnfm = arB.ap.rearrange("p (k c) -> p k c", c=1152)
    vB_xn1 = arB.ap[:, 0:16 * SS].rearrange("p (k c) -> p k c", c=SS)
    vB_ao = vB_xn1
    vA_ao = arA.ap[:, 16384:32768].rearrange("p (k c) -> p k c", c=SS)
    arA_hi = Buf(None)

    scr_ypart = Buf(sc_ypart)
    scr_z = Buf(sc_z)
    scr_c = Buf(sc_c)
    scr_g0 = Buf(sc_g0)
    scr_csg = Buf(sc_csg)
    scr_q = Buf(sc_q)
    scr_kvc = Buf(sc_kvc)
    scr_kc = Buf(sc_kc)
    scr_g1 = Buf(sc_g1)
    scr_x1 = Buf(sc_x1)
    qwms = [Buf(None) for _ in range(KD)]
    sw0ms = [Buf(None) for _ in range(KD)]
    scr_sg = Buf(sc_sg)
    scr_ag = Buf(None)

    dma(GP, c_id_bf[:, :], ident_d[:, :], writes=[c_id_bf])
    dma(SY, c_id_f[:, :], ident_d[0:64, 0:64], writes=[c_id_f])
    dma(GP, c_triu[:, :], triu_d[:, :], writes=[c_triu])
    dma(GP, c_negm[:, :], negm_d[:, :], writes=[c_negm])
    step(GP, lambda e: e.memset(c_ones[:, :], 1.0), writes=[c_ones])
    dma(GP, c_dp[:, :], dpcol_d[:, :], writes=[c_dp])
    dma(SY, c_f0b[:, :], f0b_d[:, :], writes=[c_f0b])
    dma(SY, c_f0a[:, :], f0a_d[:, :], writes=[c_f0a])
    dma(SY, c_f1b[:, :], f1b_d[:, :], writes=[c_f1b])
    dma(SY, c_wcv[:, :], wconv_d[:, :], writes=[c_wcv])
    dma(SY, c_bcv[:, :], bconv_d[:, :], writes=[c_bcv])
    dma(SY, c_dtb[:, :], dtb_d[:, :], writes=[c_dtb])
    dma(SY, c_m8[:, :], mask8_d[:, :], writes=[c_m8])
    dma(SY, c_nwc[:, :], normw_d[:, :], writes=[c_nwc])
    dma(SY, c_gc[:, :], gamma_d[:, :], writes=[c_gc])
    dma(SY, c_bc[:, :], beta_d[:, :], writes=[c_bc])
    dma(SY, c_onesf[:, :], onesf_d[:, :], writes=[c_onesf])
    dma(SY, c_A[:, :], alog_d[:, :], writes=[c_A])
    step(GP, lambda e: e.memset(c_epsln[:, :], EPS_LN), writes=[c_epsln])
    step(GP, lambda e: e.memset(c_epsl1[:, :], EPS_L1), writes=[c_epsl1])
    step(GP, lambda e: e.memset(c_one[:, :], 1.0), writes=[c_one])
    step(GP, lambda e: e.memset(c_nones[:, :], -1.0), writes=[c_nones])
    step(GP, lambda e: e.memset(b_csgc[:, :], 0.0), writes=[b_csgc])
    step(GP, lambda e: e.memset(b_ckv[:, :], 0.0), writes=[b_ckv])
    step(GP, lambda e: e.memset(b_ckc[:, :], 0.0), writes=[b_ckc])
    step(GP, lambda e: e.memset(st_bf[:, :], 0.0), writes=[st_bf])
    step(SC, lambda e: e.activation(c_A[:, :], c_A[:, :], AF.Exp),
         reads=[c_A], writes=[c_A])
    step(VE, lambda e: e.tensor_scalar(c_A[:, :], c_A[:, :], -1.0, None, ALU.mult),
         reads=[c_A], writes=[c_A])
    WCSTEP = 4096
    w2cs = [Buf(None) for _ in range((KD * 4096) // WCSTEP)]
    w3cs = [Buf(None) for _ in range((48 * 2048) // WCSTEP)]
    w4cs = [Buf(None) for _ in range((KD * 2048) // WCSTEP)]
    _drip_pend = []
    for dst, srcw, cols, cbs in ((m_woutT, woutT, KD * 4096, w2cs),
                                 (m_wqkvT, wqkvT, 48 * 2048, w3cs),
                                 (m_wout1T, wout1T, KD * 2048, w4cs)):
        for ci, c0 in enumerate(range(0, cols, WCSTEP)):
            _drip_pend.append((dst, srcw, c0, cbs[ci]))

    def drip(n):
        for _ in range(min(n, len(_drip_pend))):
            dst, srcw, c0, bb = _drip_pend.pop(0)
            dma(GP, dst[:, c0:c0 + WCSTEP], srcw[:, c0:c0 + WCSTEP],
                writes=[bb])

    for m in range(KD):
        wq = b_u if m % 2 == 0 else b_x1
        sgn = Buf(Q1.ap[:, 0:2048], base=Q1) if m % 2 == 0 else \
            Buf(Q2.ap[:, 0:2048], base=Q2)
        st = b_st1 if m % 2 == 0 else b_st2
        mm_ = b_m if m % 2 == 0 else b_m2
        dma(SY, wq[:, :], wf0[m * 128:(m + 1) * 128, :], writes=[wq])
        step(SC, lambda e, wq=wq, sgn=sgn, st=st: e.activation(sgn[:, :],
             wq[:, :], AF.Copy, accum_out=st[:, :]), reads=[wq],
             writes=[sgn, st])
        step(VE, lambda e, st=st, mm_=mm_: e.tensor_scalar(mm_[:, :], st[:, :],
             -1.0 / D, None, ALU.mult), reads=[st], writes=[mm_])
        step(SC, lambda e, wq=wq, sgn=sgn, mm_=mm_: e.activation(sgn[:, :],
             wq[:, :], AF.Sign, bias=mm_[:, :]), reads=[wq, mm_], writes=[sgn])
        for g in range(4):
            ptr = PTRa if g % 2 == 0 else PTRb
            eb = b_e if g % 2 == 0 else b_ecl
            def tpw(e, g=g, ptr=ptr, sgn=sgn):
                for i in range(4):
                    j = g * 4 + i
                    ins = e.matmul(ptr[:, i * 128:(i + 1) * 128],
                                   sgn[:, j * 128:(j + 1) * 128],
                                   c_id_bf[:, :], start=(i == 0), stop=(i == 3))
                return ins
            step(TN, tpw, reads=[sgn, c_id_bf], writes=[ptr])
            step(SC if g % 2 == 0 else VE,
                 lambda e, ptr=ptr, eb=eb: (e.copy if ptr is PTRa else e.tensor_copy)(
                 eb[:, 0:512], r3(ptr.ap, 128)), reads=[ptr], writes=[eb])
            dma(SC, sc_sw0T[:, m * 2048 + g * 512: m * 2048 + (g + 1) * 512],
                eb[:, 0:512], reads=[eb], writes=[sw0ms[m]])

    def pass1_ss(ss):
        for tt in range(-1, NT):
            par = (tt + 1) % 2
            ub = b_u if par == 0 else b_x1
            xnb = b_xn if par == 0 else Buf(Q1.ap[:, 0:2048], base=Q1)
            v2b = b_v2 if par == 0 else Buf(Q2.ap[:, 0:2048], base=Q2)
            dmp = b_dump if par == 0 else Buf(Q4.ap[:, 0:2048], base=Q4)
            st1 = b_st1 if par == 0 else b_st1B
            st2 = b_st2 if par == 0 else b_st2B
            mc = b_m if par == 0 else b_mB
            rc = b_r if par == 0 else b_rB
            s1c = b_sc1 if par == 0 else b_sc1B
            s2c = b_sc2 if par == 0 else b_sc2B
            r0 = ss * SS + (tt + 1) * 128
            dma(SY, ub[:, :], u_d[r0:r0 + 128, :], writes=[ub])
            step(SC, lambda e, ub=ub, dmp=dmp, st1=st1: e.activation(dmp[:, :],
                 ub[:, :], AF.Copy, accum_out=st1[:, :]), reads=[ub],
                 writes=[dmp, st1])
            step(SC, lambda e, ub=ub, dmp=dmp, st2=st2: e.activation(dmp[:, :],
                 ub[:, :], AF.Square, accum_out=st2[:, :]), reads=[ub],
                 writes=[dmp, st2])
            step(VE, lambda e, st1=st1, mc=mc: e.tensor_scalar(mc[:, :],
                 st1[:, :], 1.0 / D, None, ALU.mult), reads=[st1], writes=[mc])
            step(VE, lambda e, st2=st2, s1c=s1c: e.tensor_scalar(s1c[:, :],
                 st2[:, :], 1.0 / D, None, ALU.mult), reads=[st2], writes=[s1c])
            step(VE, lambda e, mc=mc, s2c=s2c: e.tensor_mul(s2c[:, :], mc[:, :],
                 mc[:, :]), reads=[mc], writes=[s2c])
            step(VE, lambda e, s1c=s1c, s2c=s2c: e.tensor_sub(s1c[:, :],
                 s1c[:, :], s2c[:, :]), reads=[s1c, s2c], writes=[s1c])
            step(SC, lambda e, s1c=s1c, s2c=s2c: e.activation(s2c[:, :],
                 s1c[:, :], AF.Sqrt, bias=c_epsln[:, :]),
                 reads=[s1c, c_epsln], writes=[s2c])
            step(VE, lambda e, s2c=s2c, rc=rc: e.reciprocal(rc[:, :], s2c[:, :]),
                 reads=[s2c], writes=[rc])
            step(VE, lambda e, ub=ub, mc=mc, rc=rc, xnb=xnb: e.tensor_scalar(
                 xnb[:, :], ub[:, :], mc[:, :], rc[:, :], ALU.subtract,
                 ALU.mult), reads=[ub, mc, rc], writes=[xnb])
            for g in range(4):
                ptr = PTRa if g % 2 == 0 else PTRb
                def tp(e, g=g, ptr=ptr, xnb=xnb):
                    for i in range(4):
                        ins = e.matmul(ptr[:, i * 128:(i + 1) * 128],
                                       xnb[:, (g * 4 + i) * 128:(g * 4 + i + 1) * 128],
                                       c_id_bf[:, :], start=(i == 0), stop=(i == 3))
                    return ins
                step(TN, tp, reads=[xnb, c_id_bf], writes=[ptr])
                col = (tt + 1) * 128
                step(SC if g % 2 == 0 else VE,
                     lambda e, g=g, col=col, ptr=ptr:
                     (e.copy if ptr is PTRa else e.tensor_copy)(
                     vB_xnfm[:, g * 4:(g + 1) * 4, col:col + 128],
                     r3(ptr.ap, 128)), reads=[ptr], writes=[arB])
            if tt < 0:
                continue
            step(SC, lambda e, xnb=xnb, dmp=dmp, st1=st1: e.activation(dmp[:, :],
                 xnb[:, :], AF.Copy, accum_out=st1[:, :]), reads=[xnb],
                 writes=[dmp, st1])
            step(SC, lambda e, xnb=xnb, dmp=dmp, st2=st2: e.activation(dmp[:, :],
                 xnb[:, :], AF.Square, accum_out=st2[:, :]), reads=[xnb],
                 writes=[dmp, st2])
            step(VE, lambda e, st1=st1, mc=mc: e.tensor_scalar(mc[:, :],
                 st1[:, :], 1.0 / D, None, ALU.mult), reads=[st1], writes=[mc])
            step(VE, lambda e, st2=st2, s1c=s1c: e.tensor_scalar(s1c[:, :],
                 st2[:, :], 1.0 / D, None, ALU.mult), reads=[st2], writes=[s1c])
            step(VE, lambda e, mc=mc, s2c=s2c: e.tensor_mul(s2c[:, :], mc[:, :],
                 mc[:, :]), reads=[mc], writes=[s2c])
            step(VE, lambda e, s1c=s1c, s2c=s2c: e.tensor_sub(s1c[:, :],
                 s1c[:, :], s2c[:, :]), reads=[s1c, s2c], writes=[s1c])
            step(SC, lambda e, s1c=s1c, s2c=s2c: e.activation(s2c[:, :],
                 s1c[:, :], AF.Sqrt, bias=c_epsln[:, :]),
                 reads=[s1c, c_epsln], writes=[s2c])
            step(VE, lambda e, s2c=s2c, rc=rc: e.reciprocal(rc[:, :], s2c[:, :]),
                 reads=[s2c], writes=[rc])
            step(VE, lambda e, xnb=xnb, mc=mc, rc=rc, v2b=v2b: e.tensor_scalar(
                 v2b[:, :], xnb[:, :], mc[:, :], rc[:, :], ALU.subtract,
                 ALU.mult), reads=[xnb, mc, rc], writes=[v2b])
            for g in range(4):
                ptr = PTRa if g % 2 == 0 else PTRb
                def tps(e, g=g, ptr=ptr, v2b=v2b):
                    for i in range(4):
                        ins = e.matmul(ptr[:, i * 128:(i + 1) * 128],
                                       v2b[:, (g * 4 + i) * 128:(g * 4 + i + 1) * 128],
                                       c_id_bf[:, :], start=(i == 0), stop=(i == 3))
                    return ins
                step(TN, tps, reads=[v2b, c_id_bf], writes=[ptr])
                step(SC if g % 2 == 0 else VE,
                     lambda e, g=g, tt=tt, ptr=ptr:
                     (e.copy if ptr is PTRa else e.tensor_copy)(
                     vA_sgn[:, g * 4:(g + 1) * 4, tt * 128:(tt + 1) * 128],
                     r3(ptr.ap, 128)), reads=[ptr], writes=[arA])
        for k in range(KD):
            step(VE, lambda e, k=k: e.tensor_scalar(vA_sgn[:, k, :],
                 vA_sgn[:, k, :], c_gc[:, k:k + 1], c_bc[:, k:k + 1],
                 ALU.mult, ALU.add), reads=[arA, c_gc, c_bc], writes=[arA])
            step(SC, lambda e, k=k: e.activation(vA_sgn[:, k, :], vA_sgn[:, k, :],
                 AF.Sign), reads=[arA], writes=[arA])

        for m in range(KD):
            wb = b_w0 if m % 2 == 0 else b_w1
            pp = BA if m % 2 == 0 else BB
            g0t = b_g0t if m % 2 == 0 else b_ca1
            g0b = b_g0b if m % 2 == 0 else b_e
            dma(SY, wb[:, 0:2048],
                sc_sw0T[:, m * 2048:(m + 1) * 2048],
                reads=[sw0ms[m]], writes=[wb])
            def mmf(e, wb=wb, pp=pp):
                for k in range(KD):
                    for n0 in (0, 512):
                        ins = e.matmul(pp[:, n0:n0 + 512],
                                       wb[:, k * 128:(k + 1) * 128],
                                       vA_sgn[:, k, n0:n0 + 512],
                                       start=(k == 0), stop=(k == KD - 1))
                return ins
            step(TN, mmf, reads=[wb, arA], writes=[pp])
            step(VE, lambda e, m=m, pp=pp, g0t=g0t: e.tensor_scalar(
                g0t[:, :], pp[:, 0:1024], c_f0b[:, m:m + 1], c_f0a[:, m:m + 1],
                ALU.add, ALU.mult), reads=[pp, c_f0b, c_f0a], writes=[g0t])
            step(SC, lambda e, g0t=g0t, g0b=g0b: e.activation(g0b[:, :],
                 g0t[:, :], AF.Silu), reads=[g0t], writes=[g0b])
            dma(SC, sc_g0[m * 128:(m + 1) * 128, ss * SS:(ss + 1) * SS], g0b[:, :],
                reads=[g0b], writes=[scr_g0])
        pjoin()

        zps = [BA[:, 0:512], BA[:, 512:1024], BB[:, 0:512], BB[:, 512:1024]]
        vA_wz4 = arA.ap.rearrange("p (s k c) -> p s k c", k=KD, c=128)
        for rnd in range(2):
            for h in range(2):
                c0 = rnd * 32768 + h * 16384
                dma(GP, arA.ap[:, h * 16384:(h + 1) * 16384],
                    winT[:, c0:c0 + 16384], writes=[arA])
            for tt in range(NT):
                def mmz(e, rnd=rnd, tt=tt):
                    for k in range(KD):
                        for zq in range(4):
                            ins = e.matmul(zps[zq],
                                           vB_xnfm[:, k, (tt + 1) * 128:(tt + 2) * 128],
                                           vA_wz4[:, zq * 4:(zq + 1) * 4, k, :],
                                           start=(k == 0), stop=(k == KD - 1))
                    return ins
                step(TN, mmz, reads=[arA, arB], writes=[BA, BB])
                for zq in range(4):
                    step(VE, lambda e, zq=zq: e.tensor_copy(b_zt[:, :], zps[zq]),
                         reads=[BA, BB], writes=[b_zt])
                    dma(SC, sc_z[ss * SS + tt * 128: ss * SS + (tt + 1) * 128,
                                 rnd * 2048 + zq * 512: rnd * 2048 + (zq + 1) * 512],
                        b_zt[:, :], reads=[b_zt], writes=[scr_z])


        for m in range(35):
            wb = b_w0 if m % 2 == 0 else b_w1
            mw = 64 if m == 34 else 128
            dma(GP, wb[:, 0:KD * mw],
                winT[:, (32 + m) * 2048: (32 + m) * 2048 + KD * mw],
                writes=[wb])
            def mmc(e, wb=wb, mw=mw):
                for k in range(KD):
                    for dst, n0, nl in ((BA, 0, 512), (BA, 512, 512), (BB, 1024, 128)):
                        ins = e.matmul(dst.ap[0:mw, n0 - (1024 if dst is BB else 0):
                                              n0 - (1024 if dst is BB else 0) + nl],
                                       wb[:, k * mw:(k + 1) * mw],
                                       vB_xnfm[:, k, n0:n0 + nl],
                                       start=(k == 0), stop=(k == KD - 1))
                return ins
            step(TN, mmc, reads=[wb, arB], writes=[BA, BB])
            if m == 34:
                step(SC, lambda e: e.activation(b_dtraw[:, 0:1024], BA[0:H, 0:1024],
                     AF.Copy), reads=[BA], writes=[b_dtraw])
                step(SC, lambda e: e.activation(b_dtraw[:, 1024:1152], BB[0:H, 0:128],
                     AF.Copy), reads=[BB], writes=[b_dtraw])
                step(SC, lambda e: e.activation(b_dtraw[:, :], b_dtraw[:, :], AF.Exp,
                     bias=c_dtb[:, :]), reads=[b_dtraw, c_dtb], writes=[b_dtraw])
                step(SC, lambda e: e.activation(b_dtraw[:, :], b_dtraw[:, :], AF.Ln,
                     bias=c_one[0:H, :]), reads=[b_dtraw, c_one], writes=[b_dtraw])
                step(VE, lambda e: e.tensor_scalar(b_adt[:, :], b_dtraw[:, 128:1152],
                     c_A[:, :], None, ALU.mult), reads=[b_dtraw, c_A], writes=[b_adt])
                step(VE, lambda e: e.tensor_tensor_scan(b_csg[:, :], b_adt[:, :],
                     b_adt[:, :], b_csgc[:, 0:1], ALU.add, ALU.bypass),
                     reads=[b_adt, b_csgc], writes=[b_csg])
                dma(SC, sc_csg[:, ss * SS:(ss + 1) * SS], b_csg[:, :],
                    reads=[b_csg], writes=[scr_csg])
                continue
            step(SC, lambda e: e.activation(b_xbct[:, 0:1024], BA[:, 0:1024],
                 AF.Copy), reads=[BA], writes=[b_xbct])
            step(SC, lambda e: e.activation(b_xbct[:, 1024:1152], BB[:, 0:128],
                 AF.Copy), reads=[BB], writes=[b_xbct])
            step(VE, lambda e, m=m: e.tensor_scalar(b_ca0[:, :], b_xbct[:, 125:1149],
                 c_wcv[:, m * 4:m * 4 + 1], None, ALU.mult),
                 reads=[b_xbct, c_wcv], writes=[b_ca0])
            step(VE, lambda e, m=m: e.scalar_tensor_tensor(b_ca1[:, :],
                 b_xbct[:, 126:1150], c_wcv[:, m * 4 + 1:m * 4 + 2], b_ca0[:, :],
                 ALU.mult, ALU.add), reads=[b_xbct, c_wcv, b_ca0], writes=[b_ca1])
            step(VE, lambda e, m=m: e.scalar_tensor_tensor(b_ca0[:, :],
                 b_xbct[:, 127:1151], c_wcv[:, m * 4 + 2:m * 4 + 3], b_ca1[:, :],
                 ALU.mult, ALU.add), reads=[b_xbct, c_wcv, b_ca1], writes=[b_ca0])
            step(VE, lambda e, m=m: e.scalar_tensor_tensor(b_ca1[:, :],
                 b_xbct[:, 128:1152], c_wcv[:, m * 4 + 3:m * 4 + 4], b_ca0[:, :],
                 ALU.mult, ALU.add), reads=[b_xbct, c_wcv, b_ca0], writes=[b_ca1])
            if m < 32:
                step(SC, lambda e, m=m: e.activation(b_xt[:, 0:1024], b_ca1[:, :],
                     AF.Silu, bias=c_bcv[:, m:m + 1]),
                     reads=[b_ca1, c_bcv], writes=[b_xt])
                for g in range(2):
                    ptr = PTRa if g % 2 == 0 else PTRb
                    def tpx(e, g=g, ptr=ptr):
                        for i in range(4):
                            ins = e.matmul(ptr[:, i * 128:(i + 1) * 128],
                                           b_xt[:, (g * 4 + i) * 128:(g * 4 + i + 1) * 128],
                                           c_id_bf[:, :], start=(i == 0), stop=(i == 3))
                        return ins
                    step(TN, tpx, reads=[b_xt, c_id_bf], writes=[ptr])
                    step(SC if ptr is PTRa else VE, lambda e, m=m, g=g, ptr=ptr: (e.copy if ptr is PTRa else e.tensor_copy)(
                        vA_x[:, g * 4:(g + 1) * 4, m * 128:(m + 1) * 128],
                        r3(ptr.ap, 128)), reads=[ptr], writes=[arA])
            else:
                dstb = b_Bfm if m == 32 else b_Cfm
                step(SC, lambda e, m=m, dstb=dstb: e.activation(
                    dstb[:, 0:1024], b_ca1[:, :], AF.Silu, bias=c_bcv[:, m:m + 1]),
                    reads=[b_ca1, c_bcv], writes=[dstb])
                if m == 33:
                    dma(SC, sc_c[:, ss * SS:(ss + 1) * SS], b_Cfm[:, :],
                        reads=[b_Cfm], writes=[scr_c])

        pjoin()
        for c in range(NCH):
            drip(3)
            par = c % 2
            cs = b_cs if par == 0 else b_csB
            cshi = b_cshi if par == 0 else b_cshiB
            cslo = b_cslo if par == 0 else b_csloB
            loF = b_e1 if par == 0 else b_e1B
            dd = b_dd if par == 0 else b_ddB
            ds = b_ds if par == 0 else b_dsB
            dec4 = b_dec4 if par == 0 else b_dec4B
            decb = b_decb if par == 0 else b_decbB
            dtot = b_dtot if par == 0 else b_dtotB
            dtr = b_dtr if par == 0 else Buf(b_csg128.ap[64:65, 192:256], base=b_csg128)
            smb = b_smb if par == 0 else b_smbB
            Btm = b_Btm if par == 0 else b_BtmB
            fhl = b_fhl0 if par == 0 else b_fhl1
            if c == 0:
                prevcol = b_csgc
                prevsl = (slice(None), slice(0, 1))
            else:
                prevcol = b_csg
                prevsl = (slice(None), slice(c * 128 - 1, c * 128))
            step(VE, lambda e, c=c, pc=prevcol, psl=prevsl, cs=cs: e.tensor_scalar(
                cs[:, :], b_csg[:, c * 128:(c + 1) * 128], pc[psl], None,
                ALU.subtract), reads=[b_csg, prevcol], writes=[cs])
            step(VE, lambda e, cs=cs, cshi=cshi: e.tensor_copy(cshi[:, :],
                 cs[:, :]), reads=[cs], writes=[cshi])
            step(VE, lambda e, cs=cs, cshi=cshi, loF=loF: e.tensor_tensor(
                 loF[:, :], cs[:, :], cshi[:, :], ALU.subtract),
                 reads=[cs, cshi], writes=[loF])
            step(VE, lambda e, loF=loF, cslo=cslo: e.tensor_copy(cslo[:, :],
                 loF[:, :]), reads=[loF], writes=[cslo])
            dma(GP, fhl[0:1, :], cshi[0:32, :], reads=[cshi], writes=[fhl])
            dma(GP, fhl[1:2, :], cslo[0:32, :], reads=[cslo], writes=[fhl])
            dma(GP, fhl[32:33, :], cshi[32:64, :], reads=[cshi], writes=[fhl])
            dma(GP, fhl[33:34, :], cslo[32:64, :], reads=[cslo], writes=[fhl])
            step(SC, lambda e, cs=cs, dd=dd: e.activation(dd[:, :], cs[:, :],
                 AF.Exp), reads=[cs], writes=[dd])
            step(SC, lambda e, cs=cs, ds=ds: e.activation(ds[:, :], cs[:, :],
                 AF.Exp, bias=cs[:, 127:128], scale=-1.0), reads=[cs],
                 writes=[ds])
            def tpd(e, dd=dd, ds=ds):
                for i, s4 in enumerate((dd, ds)):
                    ins = e.matmul(PD[:, i * 64:(i + 1) * 64], s4[:, :],
                                   c_id_f[0:H, 0:H], start=(i == 0),
                                   stop=(i == 1))
                return ins
            step(TN, tpd, reads=[dd, ds, c_id_f], writes=[PD])
            step(VE, lambda e, dec4=dec4: e.tensor_copy(dec4[:, 0:128],
                 PD[:, 0:128]), reads=[PD], writes=[dec4])
            step(VE, lambda e, dec4=dec4, decb=decb: e.tensor_copy(decb[:, :],
                 dec4[:, 0:128]), reads=[dec4], writes=[decb])
            step(TN, lambda e, dd=dd: e.matmul(PD[0:1, 384:448], dd[:, 127:128],
                 c_id_f[0:H, 0:H], start=True, stop=True),
                 reads=[dd, c_id_f], writes=[PD])
            step(VE, lambda e, dtr=dtr: e.tensor_copy(dtr[:, :],
                 PD[0:1, 384:448]), reads=[PD], writes=[dtr])
            step(TN, lambda e, dtr=dtr: e.matmul(PD[:, 256:320],
                 c_onesf[0:1, 0:128], dtr[0:1, :], start=True, stop=True),
                 reads=[c_onesf, dtr], writes=[PD])
            step(VE, lambda e, dtot=dtot: e.tensor_copy(dtot[:, :],
                 PD[:, 256:320]), reads=[PD], writes=[dtot])
            step(TN, lambda e, c=c: e.matmul(PD[:, 320:448],
                 b_Bfm[:, c * 128:(c + 1) * 128], b_Cfm[:, c * 128:(c + 1) * 128],
                 start=True, stop=True), reads=[b_Bfm, b_Cfm], writes=[PD])
            step(VE, lambda e, smb=smb: e.tensor_mul(smb[:, :], PD[:, 320:448],
                 c_triu[:, :]), reads=[PD, c_triu], writes=[smb])
            ptrB = PTRa if par == 0 else PTRb
            step(TN, lambda e, c=c, ptrB=ptrB: e.matmul(ptrB[:, 0:128],
                 b_Bfm[:, c * 128:(c + 1) * 128], c_id_bf[:, :],
                 start=True, stop=True),
                 reads=[b_Bfm, c_id_bf], writes=[ptrB])
            step(SC, lambda e, Btm=Btm, ptrB=ptrB: e.copy(Btm[:, :],
                 ptrB[:, 0:128]), reads=[ptrB], writes=[Btm])
            step(VE, lambda e, c=c, decb=decb: e.tensor_tensor(r3(b_dx.ap, PH),
                 r3(vA_x[:, c, :], PH), brep(decb[:, 64:128], PH), ALU.mult),
                 reads=[arA, decb], writes=[b_dx])
            step(VE, lambda e, c=c: e.tensor_tensor(r3(b_dpx.ap, PH),
                 r3(vA_x[:, c, :], PH), brep(c_dp[:, :], PH), ALU.mult),
                 reads=[arA, c_dp], writes=[b_dpx])
            for grp in range(8):
                    pd = BA if grp % 2 == 0 else BB
                    def mmdiff(e, grp=grp, pd=pd, fhl=fhl):
                        for j in range(8):
                            h = grp * 8 + j
                            hr = 0 if h < 32 else 32
                            off = (h % 32) * 128
                            col = slice(j * 128, (j + 1) * 128)
                            first = (j == 0) or (j == 4)
                            last = (j == 3) or (j == 7)
                            e.matmul(pd[:, col], fhl[hr:hr + 2, off:off + 128],
                                     c_nones[hr:hr + 2, 0:128], start=first,
                                     stop=False)
                            e.matmul(pd[:, col], c_ones[hr:hr + 2, 0:128],
                                     fhl[hr:hr + 2, off:off + 128],
                                     start=False, stop=False)
                            ins = e.matmul(pd[:, col], c_id_bf[:, :],
                                           c_negm[:, :], start=False, stop=last)
                        return ins
                    step(TN, mmdiff, reads=[fhl, c_ones, c_nones,
                         c_id_bf, c_negm], writes=[pd])
                    step(SC, lambda e, pd=pd: e.activation(b_exps[:, :],
                         pd[:, 0:1024], AF.Exp), reads=[pd], writes=[b_exps])
                    step(VE, lambda e, smb=smb: e.tensor_tensor(
                         r3(b_mt[:, 0:1024], 128), r3(b_exps.ap, 128),
                         smb.ap.unsqueeze(1).broadcast_to([128, 8, 128]),
                         ALU.mult), reads=[b_exps, smb], writes=[b_mt])
                    step(TN, lambda e, c=c, grp=grp: e.matmul(PYO[:, :],
                         b_Cfm[:, c * 128:(c + 1) * 128],
                         st_bf[:, grp * 512:(grp + 1) * 512],
                         start=True, stop=True),
                         reads=[b_Cfm, st_bf], writes=[PYO])
                    def mmyd(e, c=c, grp=grp):
                        for i in range(8):
                            h = grp * 8 + i
                            ins = e.matmul(PYD[:, i * 64:(i + 1) * 64],
                                           b_mt[:, i * 128:(i + 1) * 128],
                                           vA_x[:, c, h * 64:(h + 1) * 64],
                                           start=(i == 0), stop=(i == 7))
                        return ins
                    step(TN, mmyd, reads=[b_mt, arA], writes=[PYD])
                    step(VE, lambda e, grp=grp, dec4=dec4: e.tensor_tensor(
                         r3(b_u2.ap, PH), r3(PYO.ap, PH),
                         brep(dec4[:, grp * 8:(grp + 1) * 8], PH),
                         ALU.mult), reads=[PYO, dec4], writes=[b_u2])
                    step(VE, lambda e: e.tensor_add(b_u1[:, :], b_u2[:, :],
                         PYD[:, :]), reads=[b_u2, PYD], writes=[b_u1])
                    step(VE, lambda e, grp=grp: e.tensor_add(
                         b_yp[:, grp * 512:(grp + 1) * 512], b_u1[:, :],
                         b_dpx[:, grp * 512:(grp + 1) * 512]),
                         reads=[b_u1, b_dpx], writes=[b_yp])
                    step(TN, lambda e, grp=grp, Btm=Btm: e.matmul(PYO[:, :],
                         Btm[:, :], b_dx[:, grp * 512:(grp + 1) * 512],
                         start=True, stop=True),
                         reads=[Btm, b_dx], writes=[PYO])
                    step(VE, lambda e, grp=grp, dtot=dtot: e.tensor_tensor(
                         r3(b_su.ap, PH),
                         r3(st_bf[:, grp * 512:(grp + 1) * 512], PH),
                         brep(dtot[:, grp * 8:(grp + 1) * 8], PH), ALU.mult),
                         reads=[st_bf, dtot], writes=[b_su])
                    step(VE, lambda e, grp=grp: e.tensor_add(
                         st_bf[:, grp * 512:(grp + 1) * 512], b_su[:, :],
                         PYO[:, :]), reads=[b_su, PYO], writes=[st_bf])
            dma(SC, sc_ypart[ss * SS + c * 128: ss * SS + (c + 1) * 128, :],
                b_yp[:, :], reads=[b_yp], writes=[scr_ypart])
        step(VE, lambda e: e.tensor_copy(b_csgc[:, :], b_csg[:, 1023:1024]),
             reads=[b_csg], writes=[b_csgc])

    for ss in range(NSS):
        pass1_ss(ss)
    drip(len(_drip_pend))
    pjoin()

    step(SC, lambda e: e.activation(b_nc0[:, :], b_csgc[:, :], AF.Exp),
         reads=[b_csgc], writes=[b_nc0])
    step(TN, lambda e: e.matmul(PD[0:1, 0:H], b_nc0[:, :], c_id_f[0:H, 0:H],
         start=True, stop=True), reads=[b_nc0, c_id_f], writes=[PD])
    step(VE, lambda e: e.tensor_copy(b_gst[0:1, 0:H], PD[0:1, 0:H]),
         reads=[PD], writes=[b_gst])
    dma(SY, ag1d_i[:, :], b_gst[0:1, 0:H], reads=[b_gst], writes=[scr_ag])
    dma(SY, ag1s_i[:, :], st_bf[:, :], reads=[st_bf], writes=[scr_ag])
    step(GP, lambda e: e.collective_compute(
        "AllGather", ALU.bypass,
        replica_groups=[[0, 1, 2, 3], [4, 5, 6, 7]],
        ins=[ag1s_i.opt()], outs=[ag1s_o.opt()]),
        reads=[scr_ag], writes=[scr_ag], inc="cc")
    step(GP, lambda e: e.collective_compute(
        "AllGather", ALU.bypass,
        replica_groups=[[0, 1, 2, 3], [4, 5, 6, 7]],
        ins=[ag1d_i.opt()], outs=[ag1d_o.opt()]),
        reads=[scr_ag], writes=[scr_ag], inc="cc")
    for i in range(GROUP):
        dma(SY, vA_x[:, i, :], ag1s_o[i * NST:(i + 1) * NST, :],
            reads=[scr_ag], writes=[arA])
    acc_f = arA.ap[:, 16384:24576].bitcast(F32)
    si_f = arA.ap[:, 24576:32768].bitcast(F32)
    step(GP, lambda e: e.memset(acc_f, 0.0), writes=[arA])
    for i in range(GROUP - 1):
        dma(SY, b_dtr[:, :], ag1d_o[i:i + 1, :], reads=[scr_ag], writes=[b_dtr])
        step(TN, lambda e, i=i: e.matmul(PD[:, 0:H], c_onesf[0:1, 0:128],
             b_dtr[0:1, :], start=True, stop=True),
             reads=[c_onesf, b_dtr], writes=[PD])
        step(VE, lambda e, i=i: e.tensor_scalar(b_dmod[:, :], PD[:, 0:H],
             c_m8[:, i:i + 1], c_m8[:, 3 + i:4 + i], ALU.mult, ALU.add),
             reads=[PD, c_m8], writes=[b_dmod])
        step(VE, lambda e, i=i: e.tensor_scalar(
             si_f, vA_x[:, i, :], c_m8[:, i:i + 1], None, ALU.mult),
             reads=[arA, c_m8], writes=[arA])
        step(VE, lambda e: e.tensor_tensor(
             acc_f.rearrange("p (a b) -> p a b", b=PH),
             acc_f.rearrange("p (a b) -> p a b", b=PH),
             brep(b_dmod[:, :], PH), ALU.mult), reads=[arA, b_dmod],
             writes=[arA])
        step(VE, lambda e: e.tensor_tensor(acc_f, acc_f, si_f, ALU.add),
             reads=[arA], writes=[arA])
    step(VE, lambda e: e.tensor_copy(st_bf[:, :], acc_f),
         reads=[arA], writes=[st_bf])
    dma(GP, dbg_sin[:, :], st_bf[:, :], reads=[st_bf], writes=[scr_ag])

    for m in range(KD):
        wq = b_u if m % 2 == 0 else b_x1
        qq = Buf(Q1.ap[:, 0:2048], base=Q1) if m % 2 == 0 else \
            Buf(Q2.ap[:, 0:2048], base=Q2)
        ab = Buf(Q3.ap[:, 0:2048], base=Q3) if m % 2 == 0 else \
            Buf(Q4.ap[:, 0:2048], base=Q4)
        st = b_st1 if m % 2 == 0 else b_st2
        th = b_m if m % 2 == 0 else b_m2
        dma(SY, wq[:, :], wf1[m * 128:(m + 1) * 128, :], writes=[wq])
        step(SC, lambda e, wq=wq, ab=ab, st=st: e.activation(ab[:, :], wq[:, :],
             AF.Abs, accum_out=st[:, :]), reads=[wq], writes=[ab, st])
        step(VE, lambda e, st=st, th=th: e.tensor_scalar(th[:, :], st[:, :],
             0.7 / D, None, ALU.mult), reads=[st], writes=[th])
        step(VE, lambda e, ab=ab, th=th: e.tensor_scalar(ab[:, :], ab[:, :],
             th[:, :], None, ALU.is_gt), reads=[ab, th], writes=[ab])
        step(SC, lambda e, wq=wq, qq=qq: e.activation(qq[:, :], wq[:, :],
             AF.Sign), reads=[wq], writes=[qq])
        step(VE, lambda e, qq=qq, ab=ab: e.tensor_mul(qq[:, :], qq[:, :],
             ab[:, :]), reads=[qq, ab], writes=[qq])
        for g in range(4):
            ptr = PTRa if g % 2 == 0 else PTRb
            eb = b_e if g % 2 == 0 else b_ecl
            def tpq(e, g=g, ptr=ptr, qq=qq):
                for i in range(4):
                    j = g * 4 + i
                    ins = e.matmul(ptr[:, i * 128:(i + 1) * 128],
                                   qq[:, j * 128:(j + 1) * 128],
                                   c_id_bf[:, :], start=(i == 0), stop=(i == 3))
                return ins
            step(TN, tpq, reads=[qq, c_id_bf], writes=[ptr])
            step(SC if g % 2 == 0 else VE,
                 lambda e, ptr=ptr, eb=eb: (e.copy if ptr is PTRa else e.tensor_copy)(
                 eb[:, 0:512], r3(ptr.ap, 128)), reads=[ptr], writes=[eb])
            dma(SC, sc_qwT[:, m * 2048 + g * 512: m * 2048 + (g + 1) * 512],
                eb[:, 0:512], reads=[eb], writes=[qwms[m]])

    def pass2_ss(ss):
        pjoin()
        for c in range(NCH):
            t0 = ss * SS + c * 128
            dma(GP, b_ypl[:, :], sc_ypart[t0:t0 + 128, :],
                reads=[scr_ypart], writes=[b_ypl])
            dma(GP, b_zl[:, :], sc_z[t0:t0 + 128, :], reads=[scr_z], writes=[b_zl])
            dma(SY, b_cfm2[:, 0:128], sc_c[:, t0:t0 + 128],
                reads=[scr_c], writes=[b_cfm2])
            dma(SY, b_dtg[:, :], sc_csg[:, t0:t0 + 128],
                reads=[scr_csg], writes=[b_dtg])
            step(SC, lambda e: e.activation(b_dtg[:, :], b_dtg[:, :], AF.Exp),
                 reads=[b_dtg], writes=[b_dtg])
            step(TN, lambda e: e.matmul(PD[:, 0:H], b_dtg[:, :],
                 c_id_f[0:H, 0:H], start=True, stop=True),
                 reads=[b_dtg, c_id_f], writes=[PD])
            step(VE, lambda e: e.tensor_copy(b_dcg[:, :], PD[:, 0:H]),
                 reads=[PD], writes=[b_dcg])
            step(SC, lambda e: e.activation(b_sq[:, :], b_zl[:, :], AF.Silu),
                 reads=[b_zl], writes=[b_sq])
            for grp in range(8):
                step(TN, lambda e, grp=grp: e.matmul(PYD[:, :], b_cfm2[:, 0:128],
                     st_bf[:, grp * 512:(grp + 1) * 512], start=True, stop=True),
                     reads=[b_cfm2, st_bf], writes=[PYD])
                step(VE, lambda e, grp=grp: e.tensor_tensor(r3(b_u1.ap, PH),
                     r3(PYD.ap, PH),
                     brep(b_dcg[:, grp * 8:(grp + 1) * 8], PH), ALU.mult),
                     reads=[PYD, b_dcg], writes=[b_u1])
                step(VE, lambda e, grp=grp: e.tensor_add(
                     b_t2[:, grp * 512:(grp + 1) * 512], b_u1[:, :],
                     b_ypl[:, grp * 512:(grp + 1) * 512]),
                     reads=[b_u1, b_ypl], writes=[b_t2])
            if ss == 0 and c == 0:
                dma(GP, dbg_t2[:, :], b_t2[:, :], reads=[b_t2], writes=[scr_ag])
                dma(GP, dbg_sz[:, :], b_sq[:, :], reads=[b_sq], writes=[scr_ag])
            step(VE, lambda e: e.tensor_mul(b_y[:, :], b_t2[:, :], b_sq[:, :]),
                 reads=[b_t2, b_sq], writes=[b_y])
            if ss == 0 and c == 0:
                dma(GP, dbg_y[:, :], b_y[:, :], reads=[b_y], writes=[scr_ag])
            step(SC, lambda e: e.activation(b_sq[:, :], b_y[:, :], AF.Square,
                 accum_out=b_st1[:, :]), reads=[b_y], writes=[b_sq, b_st1])
            step(VE, lambda e: e.tensor_scalar(b_sc1[:, :], b_st1[:, :], 1.0 / DIN,
                 None, ALU.mult), reads=[b_st1], writes=[b_sc1])
            step(SC, lambda e: e.activation(b_sc2[:, :], b_sc1[:, :], AF.Sqrt,
                 bias=c_epsln[:, :]), reads=[b_sc1, c_epsln], writes=[b_sc2])
            step(VE, lambda e: e.reciprocal(b_r[:, :], b_sc2[:, :]),
                 reads=[b_sc2], writes=[b_r])
            step(VE, lambda e: e.tensor_scalar(b_yn[:, :], b_y[:, :],
                 b_r[:, :], None, ALU.mult), reads=[b_y, b_r], writes=[b_yn])
            if ss == 0 and c == 0:
                dma(GP, dbg_yn[:, :], b_yn[:, :], reads=[b_yn], writes=[scr_ag])
            for g in range(8):
                ptr = PTRa if g % 2 == 0 else PTRb
                def tpy(e, g=g, ptr=ptr):
                    for i in range(4):
                        ins = e.matmul(ptr[:, i * 128:(i + 1) * 128],
                                       b_yn[:, (g * 4 + i) * 128:(g * 4 + i + 1) * 128],
                                       c_id_bf[:, :], start=(i == 0), stop=(i == 3))
                    return ins
                step(TN, tpy, reads=[b_yn, c_id_bf], writes=[ptr])
                step(SC if ptr is PTRa else VE, lambda e, g=g, c=c, ptr=ptr: (e.copy if ptr is PTRa else e.tensor_copy)(
                    vA_ynfm[:, g * 4:(g + 1) * 4, c * 128:(c + 1) * 128],
                    r3(ptr.ap, 128)), reads=[ptr], writes=[arA])
        for k in range(32):
            step(VE, lambda e, k=k: e.tensor_scalar(vA_ynfm[:, k, :],
                 vA_ynfm[:, k, :], c_nwc[:, k:k + 1], None, ALU.mult),
                 reads=[arA, c_nwc], writes=[arA])
        for m in range(KD):
            pp = BA if m % 2 == 0 else BB
            sgo = b_sg1 if m % 2 == 0 else b_kv
            g0i = b_g0b if m % 2 == 0 else b_ecl
            dma(SY, b_w0[:, 0:2048],
                m_woutT[:, m * 4096: m * 4096 + 2048],
                reads=[w2cs[m]], writes=[b_w0])
            def mmo1(e, pp=pp):
                for k in range(16):
                    for n0 in (0, 512):
                        ins = e.matmul(pp[:, n0:n0 + 512],
                                       b_w0[:, k * 128:(k + 1) * 128],
                                       vA_ynfm[:, k, n0:n0 + 512],
                                       start=(k == 0), stop=False)
                return ins
            dma(SY, b_w1[:, 0:2048],
                m_woutT[:, m * 4096 + 2048: m * 4096 + 4096],
                reads=[w2cs[m]], writes=[b_w1])
            step(TN, mmo1, reads=[b_w0, arA], writes=[pp])
            def mmo2(e, pp=pp):
                for k in range(16):
                    for n0 in (0, 512):
                        ins = e.matmul(pp[:, n0:n0 + 512],
                                       b_w1[:, k * 128:(k + 1) * 128],
                                       vA_ynfm[:, 16 + k, n0:n0 + 512],
                                       start=False, stop=(k == 15))
                return ins
            step(TN, mmo2, reads=[b_w1, arA], writes=[pp])
            dma(SC, g0i[:, :], sc_g0[m * 128:(m + 1) * 128, ss * SS:(ss + 1) * SS],
                reads=[scr_g0], writes=[g0i])
            step(VE, lambda e, m=m, pp=pp, g0i=g0i: e.tensor_mul(
                 vB_xn1[:, m, :], pp[:, 0:1024], g0i[:, :]),
                 reads=[pp, g0i], writes=[arB])
        pjoin()
        for m in range(KD):
            for g in range(2):
                ptr = PTRa if g % 2 == 0 else PTRb
                def tpsg(e, m=m, g=g, ptr=ptr):
                    for i in range(4):
                        ins = e.matmul(ptr[:, i * 128:(i + 1) * 128],
                                       vB_xn1[:, m, (g * 4 + i) * 128:(g * 4 + i + 1) * 128],
                                       c_id_bf[:, :], start=(i == 0), stop=(i == 3))
                    return ins
                step(TN, tpsg, reads=[arB, c_id_bf], writes=[ptr])
                step(SC if ptr is PTRa else VE, lambda e, m=m, g=g, ptr=ptr: (e.copy if ptr is PTRa else e.tensor_copy)(
                    vA_sgtm[:, g * 4:(g + 1) * 4, m * 128:(m + 1) * 128],
                    r3(ptr.ap, 128)), reads=[ptr], writes=[arA])
        uB = Buf(arA.ap[:, 16384:20480].bitcast(F32), base=arA)
        x1B = Buf(arA.ap[:, 20480:24576].bitcast(F32), base=arA)
        x1cB = Buf(arA.ap[:, 24576:26624], base=arA)
        dmpB = Buf(Q4.ap[:, 0:2048], base=Q4)
        for tt in range(NT):
            par = tt % 2
            ub = b_u if par == 0 else uB
            x1b = b_x1 if par == 0 else x1B
            x1c = b_x1c if par == 0 else x1cB
            dmp = b_dump if par == 0 else dmpB
            st1 = b_st1 if par == 0 else b_st1B
            st2 = b_st2 if par == 0 else b_st2B
            mc = b_m if par == 0 else b_mB
            rc = b_r if par == 0 else b_rB
            s1c = b_sc1 if par == 0 else b_sc1B
            s2c = b_sc2 if par == 0 else b_sc2B
            xnb = b_xn if par == 0 else b_v2
            r0 = ss * SS + tt * 128 + 128
            dma(GP, ub[:, :], u_d[r0:r0 + 128, :], writes=[ub])
            step(VE, lambda e, tt=tt, ub=ub, x1b=x1b: e.tensor_add(x1b[:, :],
                 ub[:, :], vA_sgtm[:, tt, :]), reads=[ub, arA], writes=[x1b])
            step(SC, lambda e, x1b=x1b, x1c=x1c: e.activation(x1c[:, :],
                 x1b[:, :], AF.Copy), reads=[x1b], writes=[x1c])
            dma(SC, sc_x1[ss * SS + tt * 128: ss * SS + (tt + 1) * 128, :],
                x1c[:, :], reads=[x1c], writes=[scr_x1])
            step(SC, lambda e, x1b=x1b, dmp=dmp, st1=st1: e.activation(dmp[:, :],
                 x1b[:, :], AF.Copy, accum_out=st1[:, :]), reads=[x1b],
                 writes=[dmp, st1])
            step(SC, lambda e, x1b=x1b, dmp=dmp, st2=st2: e.activation(dmp[:, :],
                 x1b[:, :], AF.Square, accum_out=st2[:, :]), reads=[x1b],
                 writes=[dmp, st2])
            step(VE, lambda e, st1=st1, mc=mc: e.tensor_scalar(mc[:, :],
                 st1[:, :], 1.0 / D, None, ALU.mult), reads=[st1], writes=[mc])
            step(VE, lambda e, st2=st2, s1c=s1c: e.tensor_scalar(s1c[:, :],
                 st2[:, :], 1.0 / D, None, ALU.mult), reads=[st2], writes=[s1c])
            step(VE, lambda e, mc=mc, s2c=s2c: e.tensor_mul(s2c[:, :], mc[:, :],
                 mc[:, :]), reads=[mc], writes=[s2c])
            step(VE, lambda e, s1c=s1c, s2c=s2c: e.tensor_sub(s1c[:, :],
                 s1c[:, :], s2c[:, :]), reads=[s1c, s2c], writes=[s1c])
            step(SC, lambda e, s1c=s1c, s2c=s2c: e.activation(s2c[:, :],
                 s1c[:, :], AF.Sqrt, bias=c_epsln[:, :]),
                 reads=[s1c, c_epsln], writes=[s2c])
            step(VE, lambda e, s2c=s2c, rc=rc: e.reciprocal(rc[:, :], s2c[:, :]),
                 reads=[s2c], writes=[rc])
            step(VE, lambda e, x1b=x1b, mc=mc, rc=rc, xnb=xnb: e.tensor_scalar(
                 xnb[:, :], x1b[:, :], mc[:, :], rc[:, :], ALU.subtract,
                 ALU.mult), reads=[x1b, mc, rc], writes=[xnb])
            for g in range(4):
                ptr = PTRa if g % 2 == 0 else PTRb
                def tp1(e, g=g, ptr=ptr, xnb=xnb):
                    for i in range(4):
                        ins = e.matmul(ptr[:, i * 128:(i + 1) * 128],
                                       xnb[:, (g * 4 + i) * 128:(g * 4 + i + 1) * 128],
                                       c_id_bf[:, :], start=(i == 0), stop=(i == 3))
                    return ins
                step(TN, tp1, reads=[xnb, c_id_bf], writes=[ptr])
                step(SC if ptr is PTRa else VE, lambda e, g=g, tt=tt, ptr=ptr: (e.copy if ptr is PTRa else e.tensor_copy)(
                    vB_xn1[:, g * 4:(g + 1) * 4, tt * 128:(tt + 1) * 128],
                    r3(ptr.ap, 128)), reads=[ptr], writes=[arB])
        pjoin()
        for j in range(KD):
            for kind in range(3):
                wb = b_w0 if kind % 2 == 0 else b_w1
                pp = (BA, BB, BA)[kind] if j % 2 == 0 else (BB, BA, BB)[kind]
                col = kind * D + j * 128
                dma(SY, wb[:, 0:2048],
                    m_wqkvT[:, (kind * KD + j) * 2048:(kind * KD + j + 1) * 2048],
                    reads=[w3cs[(kind * KD + j) // 2]], writes=[wb])
                def mmq(e, wb=wb, pp=pp):
                    for k in range(KD):
                        for n0 in (0, 512):
                            ins = e.matmul(pp[:, n0:n0 + 512],
                                           wb[:, k * 128:(k + 1) * 128],
                                           vB_xn1[:, k, n0:n0 + 512],
                                           start=(k == 0), stop=(k == KD - 1))
                    return ins
                step(TN, mmq, reads=[wb, arB], writes=[pp])
                if kind < 2:
                    step(SC, lambda e, pp=pp: e.activation(b_e[:, :], pp[:, 0:1024],
                         AF.Exp), reads=[pp], writes=[b_e])
                    step(VE, lambda e: e.tensor_scalar(b_ecl[:, :], b_e[:, :],
                         1.0, None, ALU.min), reads=[b_e], writes=[b_ecl])
                    dstb = b_qphi if kind == 0 else b_kphi
                    step(VE, lambda e, dstb=dstb, pp=pp: e.scalar_tensor_tensor(
                         dstb[:, :], pp[:, 0:1024], 0.0, b_ecl[:, :],
                         ALU.max, ALU.add), reads=[pp, b_ecl], writes=[dstb])
                    if kind == 0:
                        dma(SC, sc_q[j * 128:(j + 1) * 128, ss * SS:(ss + 1) * SS],
                            b_qphi[:, :], reads=[b_qphi], writes=[scr_q])
                else:
                    step(VE, lambda e, pp=pp: e.tensor_tensor(b_kv[:, :],
                         b_kphi[:, :], pp[:, 0:1024], ALU.mult),
                         reads=[b_kphi, pp], writes=[b_kv])
                    step(VE, lambda e, j=j: e.tensor_tensor_scan(b_kvc[:, :],
                         b_kv[:, :], b_kv[:, :], b_ckv[:, j:j + 1], ALU.add,
                         ALU.bypass), reads=[b_kv, b_ckv], writes=[b_kvc])
                    dma(GP, sc_kvc[j * 128:(j + 1) * 128, ss * SS:(ss + 1) * SS],
                        b_kvc[:, :], reads=[b_kvc], writes=[scr_kvc])
                    step(VE, lambda e, j=j: e.tensor_copy(b_ckv[:, j:j + 1],
                         b_kvc[:, 1023:1024]), reads=[b_kvc], writes=[b_ckv])
                    step(VE, lambda e, j=j: e.tensor_tensor_scan(b_kvc[:, :],
                         b_kphi[:, :], b_kphi[:, :], b_ckc[:, j:j + 1], ALU.add,
                         ALU.bypass), reads=[b_kphi, b_ckc], writes=[b_kvc])
                    dma(GP, sc_kc[j * 128:(j + 1) * 128, ss * SS:(ss + 1) * SS],
                        b_kvc[:, :], reads=[b_kvc], writes=[scr_kc])
                    step(VE, lambda e, j=j: e.tensor_copy(b_ckc[:, j:j + 1],
                         b_kvc[:, 1023:1024]), reads=[b_kvc], writes=[b_ckc])
        for m in range(KD):
            wb = b_w0 if m % 2 == 0 else b_w1
            pp = BA if m % 2 == 0 else BB
            g0t = b_g0t if m % 2 == 0 else b_ca1
            g0b = b_g0b if m % 2 == 0 else b_e
            dma(SY, wb[:, 0:2048],
                sc_qwT[:, m * 2048:(m + 1) * 2048],
                reads=[qwms[m]], writes=[wb])
            def mmg1(e, wb=wb, pp=pp):
                for k in range(KD):
                    for n0 in (0, 512):
                        ins = e.matmul(pp[:, n0:n0 + 512],
                                       wb[:, k * 128:(k + 1) * 128],
                                       vB_xn1[:, k, n0:n0 + 512],
                                       start=(k == 0), stop=(k == KD - 1))
                return ins
            step(TN, mmg1, reads=[wb, arB], writes=[pp])
            step(VE, lambda e, m=m, pp=pp, g0t=g0t: e.tensor_scalar(g0t[:, :],
                 pp[:, 0:1024], c_f1b[:, m:m + 1], None, ALU.add),
                 reads=[pp, c_f1b], writes=[g0t])
            step(SC, lambda e, g0t=g0t, g0b=g0b: e.activation(g0b[:, :],
                 g0t[:, :], AF.Silu), reads=[g0t], writes=[g0b])
            dma(SC, sc_g1[m * 128:(m + 1) * 128, ss * SS:(ss + 1) * SS],
                g0b[:, :], reads=[g0b], writes=[scr_g1])
        pjoin()

    for ss in range(NSS):
        pass2_ss(ss)

    dma(SY, ag2_i[0:1, :].rearrange("r (p m) -> (r p) m", m=KD), b_ckc[:, :],
        reads=[b_ckc], writes=[scr_ag])
    dma(SY, ag2_i[1:2, :].rearrange("r (p m) -> (r p) m", m=KD), b_ckv[:, :],
        reads=[b_ckv], writes=[scr_ag])
    step(GP, lambda e: e.collective_compute(
        "AllGather", ALU.bypass,
        replica_groups=[[0, 1, 2, 3], [4, 5, 6, 7]],
        ins=[ag2_i.opt()], outs=[ag2_o.opt()]),
        reads=[scr_ag], writes=[scr_ag], inc="cc")
    step(GP, lambda e: e.memset(b_offk[:, :], 0.0), writes=[b_offk])
    step(GP, lambda e: e.memset(b_offkv[:, :], 0.0), writes=[b_offkv])
    for i in range(GROUP - 1):
        for row, dst in ((2 * i, b_offk), (2 * i + 1, b_offkv)):
            dma(SY, b_gk[:, :],
                ag2_o[row:row + 1, :].rearrange("r (p m) -> (r p) m", m=KD),
                reads=[scr_ag], writes=[b_gk])
            step(VE, lambda e, i=i, dst=dst: e.scalar_tensor_tensor(dst[:, :],
                 b_gk[:, :], c_m8[:, i:i + 1], dst[:, :], ALU.mult, ALU.add),
                 reads=[b_gk, c_m8, dst], writes=[dst])

    def pass3_ss(ss):
        pjoin()
        vao = vB_ao if ss % 2 == 0 else vA_ao
        aobuf = arB if ss % 2 == 0 else arA_hi
        for m in range(KD):
            dma(SY, b_qphi[:, :], sc_q[m * 128:(m + 1) * 128, ss * SS:(ss + 1) * SS],
                reads=[scr_q], writes=[b_qphi])
            dma(GP, b_kphi[:, :], sc_kvc[m * 128:(m + 1) * 128, ss * SS:(ss + 1) * SS],
                reads=[scr_kvc], writes=[b_kphi])
            dma(GP, b_e[:, :], sc_kc[m * 128:(m + 1) * 128, ss * SS:(ss + 1) * SS],
                reads=[scr_kc], writes=[b_e])
            step(VE, lambda e, m=m: e.scalar_tensor_tensor(b_n1[:, :], b_kphi[:, :],
                 b_offkv[:, m:m + 1], b_qphi[:, :], ALU.add, ALU.mult),
                 reads=[b_kphi, b_offkv, b_qphi], writes=[b_n1])
            step(VE, lambda e, m=m: e.scalar_tensor_tensor(b_d1[:, :], b_e[:, :],
                 b_offk[:, m:m + 1], b_qphi[:, :], ALU.add, ALU.mult),
                 reads=[b_e, b_offk, b_qphi], writes=[b_d1])
            step(SC, lambda e: e.activation(b_d1[:, :], b_d1[:, :], AF.Ln,
                 bias=c_epsl1[:, :]), reads=[b_d1, c_epsl1], writes=[b_d1])
            step(SC, lambda e: e.activation(b_d1[:, :], b_d1[:, :], AF.Exp,
                 scale=-1.0), reads=[b_d1], writes=[b_d1])
            step(VE, lambda e, m=m, vao=vao: e.tensor_tensor(vao[:, m, :],
                 b_n1[:, :], b_d1[:, :], ALU.mult),
                 reads=[b_n1, b_d1], writes=[aobuf])
        for m in range(KD):
            wb = b_w0 if m % 2 == 0 else b_w1
            pp = BA if m % 2 == 0 else BB
            dma(SY, wb[:, 0:2048],
                m_wout1T[:, m * 2048:(m + 1) * 2048],
                reads=[w4cs[m // 2]], writes=[wb])
            def mms1(e, wb=wb, pp=pp, vao=vao):
                for k in range(KD):
                    for n0 in (0, 512):
                        ins = e.matmul(pp[:, n0:n0 + 512],
                                       wb[:, k * 128:(k + 1) * 128],
                                       vao[:, k, n0:n0 + 512],
                                       start=(k == 0), stop=(k == KD - 1))
                return ins
            step(TN, mms1, reads=[wb, aobuf], writes=[pp])
            step(VE, lambda e, pp=pp: e.tensor_scalar(b_s1[:, :], pp[:, 0:1024],
                 SCL_OUT1, None, ALU.mult), reads=[pp], writes=[b_s1])
            dma(SC, b_g0b[:, :], sc_g1[m * 128:(m + 1) * 128, ss * SS:(ss + 1) * SS],
                reads=[scr_g1], writes=[b_g0b])
            step(VE, lambda e: e.tensor_mul(b_sg1[:, :], b_s1[:, :], b_g0b[:, :]),
                 reads=[b_s1, b_g0b], writes=[b_sg1])
            for g in range(2):
                ptr = PTRa if g % 2 == 0 else PTRb
                def tps1(e, g=g, ptr=ptr):
                    for i in range(4):
                        ins = e.matmul(ptr[:, i * 128:(i + 1) * 128],
                                       b_sg1[:, (g * 4 + i) * 128:(g * 4 + i + 1) * 128],
                                       c_id_bf[:, :], start=(i == 0), stop=(i == 3))
                    return ins
                step(TN, tps1, reads=[b_sg1, c_id_bf], writes=[ptr])
                step(SC if ptr is PTRa else VE, lambda e, m=m, g=g, ptr=ptr: (e.copy if ptr is PTRa else e.tensor_copy)(
                    vA_sgtm[:, g * 4:(g + 1) * 4, m * 128:(m + 1) * 128],
                    r3(ptr.ap, 128)), reads=[ptr], writes=[arA])
        for tt in range(NT):
            t0 = ss * SS + tt * 128
            dma(GP, b_x1c[:, :], sc_x1[t0:t0 + 128, :], reads=[scr_x1],
                writes=[b_x1c])
            step(VE, lambda e, tt=tt: e.tensor_add(b_x2[:, :], b_x1c[:, :],
                 vA_sgtm[:, tt, :]), reads=[b_x1c, arA], writes=[b_x2])
            dma(SC, out_d[t0:t0 + 128, :], b_x2[:, :], reads=[b_x2],
                writes=[scr_ag])

    for ss in range(NSS):
        pass3_ss(ss)

    fin = {n: cx.semv[n] for n in cx.semv if cx.semv[n] > 0}
    cx.do(SY, lambda e: e.nop(), waits=fin)
    cx.finalize()
    return nc


_NC_CACHE = {}


def _build_cached():
    if "nc" not in _NC_CACHE:
        _NC_CACHE["nc"] = build()
    return _NC_CACHE["nc"]


def kernel(L0_attn_u, L0_attn_W_in, L0_attn_W_conv, L0_attn_b_conv,
           L0_attn_dt_bias, L0_attn_A_log, L0_attn_Dp, L0_attn_norm_w,
           L0_attn_W_out, L0_ffn_gamma, L0_ffn_beta, L0_ffn_weight,
           L0_ffn_bias, L0_ffn_alpha, L1_attn_W_qkv, L1_attn_W_out,
           L1_ffn_weight, L1_ffn_bias):
    f32 = np.float32
    u = np.asarray(L0_attn_u, f32)

    def ct(a):
        return np.ascontiguousarray(np.asarray(a, f32).T)

    def strips(W, nk):
        O = W.shape[0]
        ns = O // 128
        E = W[0:ns * 128].reshape(ns, 128, nk, 128)
        out = np.ascontiguousarray(E.transpose(3, 0, 2, 1).reshape(128, ns * nk * 128))
        if O % 128:
            L_ = W[ns * 128:].reshape(O % 128, nk, 128)
            out = np.concatenate(
                [out, np.ascontiguousarray(
                    L_.transpose(2, 1, 0).reshape(128, nk * (O % 128)))], axis=1)
        return np.ascontiguousarray(out)

    winT = strips(np.asarray(L0_attn_W_in, f32), KD)
    woutT = strips(np.asarray(L0_attn_W_out, f32), 32)
    wqkvT = strips(np.asarray(L1_attn_W_qkv, f32), KD)
    wout1T = strips(np.asarray(L1_attn_W_out, f32), KD)
    wf0 = np.ascontiguousarray(np.asarray(L0_ffn_weight, f32))
    wf1 = np.ascontiguousarray(np.asarray(L1_ffn_weight, f32))

    wconv = np.ascontiguousarray(
        np.asarray(L0_attn_W_conv, f32).reshape(34, 128, 4)
        .transpose(1, 0, 2).reshape(128, 136))
    bconv = np.ascontiguousarray(np.asarray(L0_attn_b_conv, f32).reshape(34, 128).T)
    dtb = np.asarray(L0_attn_dt_bias, f32).reshape(H, 1)
    alog = np.asarray(L0_attn_A_log, f32).reshape(H, 1)
    dpcol = np.ascontiguousarray(
        np.broadcast_to(np.asarray(L0_attn_Dp, f32)[None, :], (128, H)))
    normw = np.ascontiguousarray(np.asarray(L0_attn_norm_w, f32).reshape(32, 128).T)
    gamma = np.ascontiguousarray(np.asarray(L0_ffn_gamma, f32).reshape(KD, 128).T)
    beta = np.ascontiguousarray(np.asarray(L0_ffn_beta, f32).reshape(KD, 128).T)
    f0b = np.ascontiguousarray(np.asarray(L0_ffn_bias, f32).reshape(KD, 128).T)
    f0a = np.ascontiguousarray(np.asarray(L0_ffn_alpha, f32).reshape(KD, 128).T)
    f1b = np.ascontiguousarray(np.asarray(L1_ffn_bias, f32).reshape(KD, 128).T)
    ident = np.eye(128, dtype=f32)
    triu = np.triu(np.ones((128, 128), dtype=f32))
    negm = np.where(triu > 0, np.float32(0.0), np.float32(-1e30)).astype(f32)
    ones = np.ones((1, 128), dtype=f32)
    onesf = np.ones((1, 128), dtype=f32)

    nc = _build_cached()
    in_maps = []
    for core in range(NCORES):
        b = core // GROUP
        r = core % GROUP
        t0 = r * TCORE
        if t0 == 0:
            uslice = np.concatenate(
                [np.zeros((128, D), f32), u[b, 0:TCORE, :]], axis=0)
        else:
            uslice = u[b, t0 - 128:t0 + TCORE, :]
        uslice = np.ascontiguousarray(uslice)
        mask8 = np.zeros((128, 8), f32)
        for i in range(GROUP - 1):
            mask8[:, i] = 1.0 if i < r else 0.0
            mask8[:, 3 + i] = 0.0 if i < r else 1.0
        in_maps.append(dict(
            u=uslice, winT=winT, woutT=woutT, wqkvT=wqkvT, wout1T=wout1T,
            wf0=wf0, wf1=wf1, wconv=wconv, bconv=bconv, dtb=dtb, alog=alog,
            dpcol=dpcol, normw=normw, gamma=gamma, beta=beta, f0b=f0b, negm=negm,
            f0a=f0a, f1b=f1b, ident=ident, triu=triu, mask8=mask8, ones=ones,
            onesf=onesf))

    res = run_bass_kernel_spmd(nc, in_maps, core_ids=list(range(NCORES)))
    out = np.empty((B, L, D), dtype=f32)
    for core in range(NCORES):
        b = core // GROUP
        t0 = (core % GROUP) * TCORE
        out[b, t0:t0 + TCORE, :] = res.results[core]["out"]
    return out


# version 20
# speedup vs baseline: 1.0083x; 1.0083x over previous
import numpy as np
import ml_dtypes
import concourse.bass as bass
import concourse.mybir as mybir
from concourse.bass_utils import run_bass_kernel_spmd

F32 = mybir.dt.float32
BF16 = mybir.dt.bfloat16
AF = mybir.ActivationFunctionType
ALU = mybir.AluOpType
AX = mybir.AxisListType

B, L, D = 2, 8192, 2048
DIN = 4096
H = 64
PH = 64
NST = 128
DIP = 8512
CONVD = DIN + 2 * NST
NCORES = 8
GROUP = 4
TCORE = (B * L) // NCORES
SS = 1024
NSS = TCORE // SS
CH = 128
NCH = SS // CH
NT = SS // 128
KD = D // 128
EPS_LN = 1e-5
EPS_RMS = 1e-5
EPS_L1 = 1e-9
SCL_OUT1 = float(128 ** -0.5)

ENGINES = ("tensor", "vector", "scalar", "gpsimd", "sync")


class Ctx:

    NSLOT = 12

    def __init__(self, nc):
        self.nc = nc
        self.plan = []
        self.sems = {}
        names = ["te", "ve", "se", "ge", "cc"]
        names += ["dy%d" % i for i in range(self.NSLOT)]
        names += ["dg%d" % i for i in range(self.NSLOT)]
        names += ["ds%d" % i for i in range(self.NSLOT)]
        self.semv = {n: 0 for n in names}
        for n in self.semv:
            self.sems[n] = nc.semaphore(n).__enter__()
        self.waited = {e: {n: 0 for n in self.semv} for e in ENGINES}
        self.engine_sem = {"tensor": "te", "vector": "ve", "scalar": "se",
                           "gpsimd": "ge", "sync": None}
        self.slot = {"sync": 0, "gpsimd": 0, "scalar": 0}

    def do(self, eng, fn, waits=None, inc=None, inck=1):
        wd = {}
        if waits:
            it = waits.items() if isinstance(waits, dict) else waits
            for s, v in it:
                if v > wd.get(s, 0):
                    wd[s] = v
        weff = {}
        for s, v in wd.items():
            if v > self.waited[eng][s]:
                weff[s] = v
                self.waited[eng][s] = v
        tok = None
        if inc:
            self.semv[inc] += inck
            tok = (inc, self.semv[inc])
        sems = self.sems

        def run(e, fn=fn, weff=weff, inc=inc, inck=inck):
            for s, v in weff.items():
                e.wait_ge(sems[s], v)
            ins = fn(e)
            if inc is not None:
                assert ins is not None
                ins.then_inc(sems[inc], inck)
        self.plan.append((eng, run))
        return tok

    def tok(self, sem):
        return (sem, self.semv[sem])

    def finalize(self):
        nc = self.nc
        with nc.Block() as block:
            for name in ENGINES:
                def make(name):
                    def body(e):
                        for eng, fn in self.plan:
                            if eng == name:
                                fn(e)
                    return body
                getattr(block, name)(make(name))


class Buf:

    def __init__(self, ap, base=None):
        self.ap_ = ap
        self._st = base._st if base is not None else {"w": None, "r": []}

    def sub(self, key):
        return Buf(self.ap_[key], base=self)

    def __getitem__(self, key):
        return self.ap_[key]

    @property
    def ap(self):
        return self.ap_


def rd(bufs):
    w = []
    for b in bufs:
        if b._st["w"]:
            w.append(b._st["w"])
    return w


def wr(bufs):
    w = []
    for b in bufs:
        if b._st["w"]:
            w.append(b._st["w"])
        w.extend(b._st["r"])
    return w


def mark(tok, reads=(), writes=()):
    for b in reads:
        b._st["r"].append(tok)
    for b in writes:
        b._st["w"] = tok
        b._st["r"] = []


def build(debug_names=(), dump_scratch=False):
    nc = bass.Bass("TRN2", target_bir_lowering=False)
    cx = Ctx(nc)

    def din(name, shape, dt=F32):
        return nc.dram_tensor(name, shape, dt, kind="ExternalInput").ap()

    u_d = din("u", [128 + TCORE, D])
    winT = din("winT", [128, 66 * 2048 + 1024])
    woutT = din("woutT", [128, KD * 4096])
    wqkvT = din("wqkvT", [128, 48 * 2048])
    wout1T = din("wout1T", [128, KD * 2048])
    wf0 = din("wf0", [D, D])
    wf1 = din("wf1", [D, D])
    wconv_d = din("wconv", [128, 34 * 4])
    bconv_d = din("bconv", [128, 34])
    dtb_d = din("dtb", [H, 1])
    alog_d = din("alog", [H, 1])
    dpcol_d = din("dpcol", [128, H])
    normw_d = din("normw", [128, 32])
    gamma_d = din("gamma", [128, KD])
    beta_d = din("beta", [128, KD])
    onesf_d = din("onesf", [1, 128])
    f0b_d = din("f0b", [128, KD])
    f0a_d = din("f0a", [128, KD])
    f1b_d = din("f1b", [128, KD])
    ident_d = din("ident", [128, 128])
    triu_d = din("triu", [128, 128])
    negm_d = din("negm", [128, 128])
    mask8_d = din("mask8", [128, 8])
    ones_d = din("ones", [1, 128])

    out_d = nc.dram_tensor("out", [TCORE, D], F32, kind="ExternalOutput").ap()
    dbg = {}
    for nm, shp in debug_names:
        dbg[nm] = nc.dram_tensor("dbg_" + nm, shp, F32, kind="ExternalOutput").ap()

    _scr_kind = "ExternalOutput" if dump_scratch else "Internal"

    def dsc(name, shape, dt):
        return nc.dram_tensor(name, shape, dt, kind=_scr_kind).ap()

    sc_ypart = dsc("sc_ypart", [TCORE, DIN], BF16)
    sc_z = dsc("sc_z", [TCORE, DIN], BF16)
    sc_c = dsc("sc_c", [NST, TCORE], BF16)
    sc_g0 = dsc("sc_g0", [D, TCORE], BF16)
    sc_csg = dsc("sc_csg", [H, TCORE], F32)
    sc_q = dsc("sc_q", [D, TCORE], BF16)
    sc_kvc = dsc("sc_kvc", [D, TCORE], BF16)
    sc_kc = dsc("sc_kc", [D, TCORE], BF16)
    sc_g1 = dsc("sc_g1", [D, TCORE], BF16)
    sc_x1 = dsc("sc_x1", [TCORE, D], BF16)
    sc_qwT = dsc("sc_qwT", [128, KD * 2048], BF16)
    sc_sw0T = dsc("sc_sw0T", [128, KD * 2048], BF16)
    sc_sg = dsc("sc_sg", [D, TCORE], BF16)
    dbg_sin = dsc("dbg_sin", [NST, DIN], F32)
    dbg_y = dsc("dbg_y", [128, DIN], F32)
    dbg_t2 = dsc("dbg_t2", [128, DIN], F32)
    dbg_sz = dsc("dbg_sz", [128, DIN], F32)
    dbg_yn = dsc("dbg_yn", [128, DIN], F32)
    dbg_sg = dsc("dbg_sg", [128, SS], F32)
    m_woutT = nc.dram_tensor("m_woutT", [128, KD * 4096], BF16).ap()
    m_wqkvT = nc.dram_tensor("m_wqkvT", [128, 48 * 2048], BF16).ap()
    m_wout1T = nc.dram_tensor("m_wout1T", [128, KD * 2048], BF16).ap()
    ag1s_i = nc.dram_tensor("ag1s_i", [NST, DIN], BF16).ap()
    ag1s_o = nc.dram_tensor("ag1s_o", [GROUP * NST, DIN], BF16).ap()
    ag1d_i = nc.dram_tensor("ag1d_i", [1, H], F32).ap()
    ag1d_o = nc.dram_tensor("ag1d_o", [GROUP, H], F32).ap()
    ag2_i = nc.dram_tensor("ag2_i", [2, D], F32).ap()
    ag2_o = nc.dram_tensor("ag2_o", [GROUP * 2, D], F32).ap()

    def sb(name, shape, dt):
        return Buf(nc.alloc_sbuf_tensor(name, shape, dt).ap())

    arA = sb("arA", [128, 32768], BF16)
    arB = sb("arB", [128, 18432], BF16)
    st_bf = sb("state", [128, 4096], BF16)
    c_id_bf = sb("c_id_bf", [128, 128], BF16)
    c_id_f = sb("c_id_f", [64, 64], F32)
    c_triu = sb("c_triu", [128, 128], BF16)
    c_ones = sb("c_ones", [34, 128], BF16)
    c_dp = sb("c_dp", [128, H], BF16)
    c_f0b = sb("c_f0b", [128, KD], F32)
    c_f0a = sb("c_f0a", [128, KD], F32)
    c_f1b = sb("c_f1b", [128, KD], F32)
    c_wcv = sb("c_wcv", [128, 34 * 4], F32)
    c_bcv = sb("c_bcv", [128, 34], F32)
    c_dtb = sb("c_dtb", [H, 1], F32)
    c_A = sb("c_A", [H, 1], F32)
    c_m8 = sb("c_m8", [128, 8], F32)
    c_nwc = sb("c_nwc", [128, 32], F32)
    c_gc = sb("c_gc", [128, KD], F32)
    c_bc = sb("c_bc", [128, KD], F32)
    c_epsln = sb("c_epsln", [128, 1], F32)
    c_epsl1 = sb("c_epsl1", [128, 1], F32)
    c_one = sb("c_one", [128, 1], F32)
    b_u = sb("b_u", [128, D], F32)
    b_x1 = sb("b_x1", [128, D], F32)
    b_xn = sb("b_xn", [128, D], BF16)
    b_v2 = sb("b_v2", [128, D], BF16)
    b_st1 = sb("b_st1", [128, 1], F32)
    b_st2 = sb("b_st2", [128, 1], F32)
    b_m = sb("b_m", [128, 1], F32)
    b_r = sb("b_r", [128, 1], F32)
    b_m2 = sb("b_m2", [128, 1], F32)
    b_r2 = sb("b_r2", [128, 1], F32)
    b_sc1 = sb("b_sc1", [128, 1], F32)
    b_sc2 = sb("b_sc2", [128, 1], F32)
    b_st1B = sb("b_st1B", [128, 1], F32)
    b_st2B = sb("b_st2B", [128, 1], F32)
    b_mB = sb("b_mB", [128, 1], F32)
    b_rB = sb("b_rB", [128, 1], F32)
    b_sc1B = sb("b_sc1B", [128, 1], F32)
    b_sc2B = sb("b_sc2B", [128, 1], F32)
    b_w0 = sb("b_w0", [128, 2048], BF16)
    b_w1 = sb("b_w1", [128, 2048], BF16)
    Q1 = sb("Q1", [128, 4096], BF16)
    Q2 = sb("Q2", [128, 4096], BF16)
    Q3 = sb("Q3", [128, 4096], BF16)
    Q4 = sb("Q4", [128, 4096], BF16)
    b_dx = Q1
    b_ypl = Q1
    b_yn = Q1
    b_dpx = Q2
    b_zl = Q2
    b_y = Q2
    b_yp = Q3
    b_sq = Q3
    b_dump = Q3.sub((slice(None), slice(0, 2048)))
    b_x1c = Buf(Q3.ap[:, 2048:4096])
    b_t2 = Q4
    b_mt = Q4.sub((slice(None), slice(0, 2048)))
    b_qw = Q4.sub((slice(None), slice(0, 2048)))
    E1 = sb("E1", [128, 1152], BF16)
    E2 = sb("E2", [128, 1024], BF16)
    E3 = sb("E3", [128, 1024], BF16)
    E4 = sb("E4", [128, 1024], BF16)
    _k1 = (slice(None), slice(0, 1024))
    b_xbct = E1
    b_sg1 = E1.sub(_k1)
    b_xt = E2
    b_e = E2.sub(_k1)
    b_kv = E2.sub(_k1)
    b_s1 = E2.sub(_k1)
    b_ecl = E3.sub(_k1)
    b_kphi = E3.sub(_k1)
    b_g0b = E3.sub(_k1)
    b_zt = E3.sub((slice(None), slice(0, 512)))
    b_qphi = E4.sub(_k1)
    b_exps = E4.sub(_k1)
    F1 = sb("F1", [128, 1024], F32)
    F2 = sb("F2", [128, 1024], F32)
    b_ca0 = F1
    b_g0t = F1
    b_n1 = F1
    b_ca1 = F2
    b_kvc = F2
    b_d1 = F2
    b_dtraw = Buf(Q1.ap.bitcast(F32)[0:H, 0:1152], base=Q1)
    b_adt = Buf(Q2.ap.bitcast(F32)[0:H, 0:1024], base=Q2)
    b_csg128 = sb("b_csg128", [128, 1024], F32)
    b_csg = Buf(b_csg128.ap[0:H, :], base=b_csg128)
    b_csgc = sb("b_csgc", [H, 1], F32)
    b_cs = sb("b_cs", [H, 128], F32)
    b_e1 = sb("b_e1", [H, 128], F32)
    b_e2 = sb("b_e2", [H, 128], F32)
    b_dd = sb("b_dd", [H, 128], F32)
    b_ds = sb("b_ds", [H, 128], F32)
    b_nc0 = sb("b_nc0", [H, 1], F32)
    b_dec4 = sb("b_dec4", [128, 128], F32)
    b_dtot = sb("b_dtot", [128, H], F32)
    b_smb = sb("b_smb", [128, 128], BF16)
    b_cshi = sb("b_cshi", [H, 128], BF16)
    b_decb = sb("b_decb", [128, 128], BF16)
    b_csB = sb("b_csB", [H, 128], F32)
    b_cshiB = sb("b_cshiB", [H, 128], BF16)
    b_csloB = sb("b_csloB", [H, 128], BF16)
    b_ddB = sb("b_ddB", [H, 128], F32)
    b_dsB = sb("b_dsB", [H, 128], F32)
    b_dec4B = sb("b_dec4B", [128, 128], F32)
    b_decbB = sb("b_decbB", [128, 128], BF16)
    b_dtotB = sb("b_dtotB", [128, H], F32)
    b_smbB = Buf(Q4.ap[:, 2048:2176], base=Q4)
    b_BtmB = Buf(Q4.ap[:, 2176:2304], base=Q4)
    b_e1B = sb("b_e1B", [H, 128], F32)
    b_cslo = sb("b_cslo", [H, 128], BF16)
    b_fhl0 = Buf(b_u.ap.bitcast(BF16)[0:34, :], base=b_u)
    b_fhl1 = Buf(b_x1.ap.bitcast(BF16)[0:34, :], base=b_x1)
    c_nones = sb("c_nones", [34, 128], BF16)
    c_negm = sb("c_negm", [128, 128], BF16)
    b_u1 = Buf(F1.ap[:, 0:512], base=F1)
    b_u2 = Buf(F2.ap[:, 0:512], base=F2)
    b_u3 = b_u1
    b_su = Buf(E4.ap[:, 0:512], base=E4)
    b_Bfm = sb("b_Bfm", [128, 1024], BF16)
    b_Cfm = sb("b_Cfm", [128, 1024], BF16)
    b_Btm = sb("b_Btm", [128, 128], BF16)
    b_cfm2 = b_Bfm
    b_thr = sb("b_thr", [128, 1], F32)
    b_wq = b_u
    b_wcl = b_x1
    b_gst = Buf(F1.ap[0:4, 0:64], base=F1)
    b_dtr = Buf(b_csg128.ap[64:65, 128:192], base=b_csg128)
    b_dmod = Buf(F1.ap[:, 128:192], base=F1)
    b_offk = Buf(E1.ap.bitcast(F32)[:, 512:528], base=E1)
    b_offkv = Buf(E1.ap.bitcast(F32)[:, 528:544], base=E1)
    b_ckv = sb("b_ckv", [128, KD], F32)
    b_ckc = sb("b_ckc", [128, KD], F32)
    b_x2 = b_x1
    b_dtg = b_cs
    b_dcg = Buf(b_dtot.ap[:, 0:H], base=b_dtot)
    b_gk = Buf(b_ckv.ap[:, :], base=b_ckv)
    c_onesf = Buf(b_csg128.ap[64:65, 0:128], base=b_csg128)

    def ps(name, shape, dt):
        return Buf(nc.alloc_psum_tensor(name, shape, dt))

    PBIG = ps("PBIG", [128, 2048], F32)
    PYD = ps("PYD", [128, 512], F32)
    PYO = ps("PYO", [128, 512], F32)
    PTRa = ps("PTRa", [128, 512], F32)
    PTRb = ps("PTRb", [128, 512], F32)

    for _b in (PBIG, PYD, PYO, PTRa, PTRb):
        _b.ap_ = _b.ap_.ap()
    BA = Buf(PBIG.ap[:, 0:1024])
    BB = Buf(PBIG.ap[:, 1024:2048])
    BPD = PYD.sub((slice(None), slice(0, 512)))
    PD = BPD

    def pjoin():
        tok = cx.do(VE, lambda e: e.engine_nop(),
                    waits=wr([BA, BB, BPD, PTRa, PTRb]), inc="ve")
        mark(tok, writes=[BA, BB, BPD, PTRa, PTRb])

    TN, VE, SC, GP, SY = "tensor", "vector", "scalar", "gpsimd", "sync"
    ESEM = {TN: "te", VE: "ve", SC: "se", GP: "ge"}

    def step(eng, fn, reads=(), writes=(), inc=None, inck=1):
        waits = rd(reads) + wr(writes)
        if inc is None:
            inc, inck = ESEM[eng], 1
        tok = cx.do(eng, fn, waits=waits, inc=inc, inck=inck)
        mark(tok, reads=reads, writes=writes)
        return tok

    def dma(q, dst_ap, src_ap, reads=(), writes=()):
        i = cx.slot[q]
        cx.slot[q] = (i + 1) % Ctx.NSLOT
        sem = {"sync": "dy%d", "gpsimd": "dg%d", "scalar": "ds%d"}[q] % i
        extra = {sem: cx.semv[sem]} if cx.semv[sem] else None
        waits = rd(reads) + wr(writes) + (list(extra.items()) if extra else [])
        tok = cx.do(q, lambda e, d=dst_ap, s=src_ap: e.dma_start(out=d, in_=s),
                    waits=waits, inc=sem, inck=16)
        mark(tok, reads=reads, writes=writes)
        return tok

    def brep(ap2d, rep):
        return ap2d.unsqueeze(2).broadcast_to([ap2d.shape[0], ap2d.shape[1], rep])

    def r3(ap2d, inner):
        return ap2d.rearrange("p (a b) -> p a b", b=inner)

    vA_x = arA.ap.rearrange("p (t c) -> p t c", c=DIN)
    vA_sgn = arA.ap[:, 0:16 * SS].rearrange("p (k c) -> p k c", c=SS)
    vA_wz = arA.ap.rearrange("p (k c) -> p k c", c=2048)
    vA_ynfm = arA.ap.rearrange("p (k c) -> p k c", c=SS)
    vA_sgtm = arA.ap[:, 0:8 * D].rearrange("p (t c) -> p t c", c=D)
    vB_xnfm = arB.ap.rearrange("p (k c) -> p k c", c=1152)
    vB_xn1 = arB.ap[:, 0:16 * SS].rearrange("p (k c) -> p k c", c=SS)
    vB_ao = vB_xn1
    vA_ao = arA.ap[:, 16384:32768].rearrange("p (k c) -> p k c", c=SS)
    arA_hi = Buf(None)

    scr_ypart = Buf(sc_ypart)
    scr_z = Buf(sc_z)
    scr_c = Buf(sc_c)
    scr_g0 = Buf(sc_g0)
    scr_csg = Buf(sc_csg)
    scr_q = Buf(sc_q)
    scr_kvc = Buf(sc_kvc)
    scr_kc = Buf(sc_kc)
    scr_g1 = Buf(sc_g1)
    scr_x1 = Buf(sc_x1)
    qwms = [Buf(None) for _ in range(KD)]
    sw0ms = [Buf(None) for _ in range(KD)]
    scr_sg = Buf(sc_sg)
    scr_ag = Buf(None)

    dma(GP, c_id_bf[:, :], ident_d[:, :], writes=[c_id_bf])
    dma(SY, c_id_f[:, :], ident_d[0:64, 0:64], writes=[c_id_f])
    dma(GP, c_triu[:, :], triu_d[:, :], writes=[c_triu])
    dma(GP, c_negm[:, :], negm_d[:, :], writes=[c_negm])
    step(GP, lambda e: e.memset(c_ones[:, :], 1.0), writes=[c_ones])
    dma(GP, c_dp[:, :], dpcol_d[:, :], writes=[c_dp])
    dma(SY, c_f0b[:, :], f0b_d[:, :], writes=[c_f0b])
    dma(SY, c_f0a[:, :], f0a_d[:, :], writes=[c_f0a])
    dma(SY, c_f1b[:, :], f1b_d[:, :], writes=[c_f1b])
    dma(SY, c_wcv[:, :], wconv_d[:, :], writes=[c_wcv])
    dma(SY, c_bcv[:, :], bconv_d[:, :], writes=[c_bcv])
    dma(SY, c_dtb[:, :], dtb_d[:, :], writes=[c_dtb])
    dma(SY, c_m8[:, :], mask8_d[:, :], writes=[c_m8])
    dma(SY, c_nwc[:, :], normw_d[:, :], writes=[c_nwc])
    dma(SY, c_gc[:, :], gamma_d[:, :], writes=[c_gc])
    dma(SY, c_bc[:, :], beta_d[:, :], writes=[c_bc])
    dma(SY, c_onesf[:, :], onesf_d[:, :], writes=[c_onesf])
    dma(SY, c_A[:, :], alog_d[:, :], writes=[c_A])
    step(GP, lambda e: e.memset(c_epsln[:, :], EPS_LN), writes=[c_epsln])
    step(GP, lambda e: e.memset(c_epsl1[:, :], EPS_L1), writes=[c_epsl1])
    step(GP, lambda e: e.memset(c_one[:, :], 1.0), writes=[c_one])
    step(GP, lambda e: e.memset(c_nones[:, :], -1.0), writes=[c_nones])
    step(GP, lambda e: e.memset(b_csgc[:, :], 0.0), writes=[b_csgc])
    step(GP, lambda e: e.memset(b_ckv[:, :], 0.0), writes=[b_ckv])
    step(GP, lambda e: e.memset(b_ckc[:, :], 0.0), writes=[b_ckc])
    step(GP, lambda e: e.memset(st_bf[:, :], 0.0), writes=[st_bf])
    step(SC, lambda e: e.activation(c_A[:, :], c_A[:, :], AF.Exp),
         reads=[c_A], writes=[c_A])
    step(VE, lambda e: e.tensor_scalar(c_A[:, :], c_A[:, :], -1.0, None, ALU.mult),
         reads=[c_A], writes=[c_A])
    WCSTEP = 4096
    w2cs = [Buf(None) for _ in range((KD * 4096) // WCSTEP)]
    w3cs = [Buf(None) for _ in range((48 * 2048) // WCSTEP)]
    w4cs = [Buf(None) for _ in range((KD * 2048) // WCSTEP)]
    _drip_pend = []
    for dst, srcw, cols, cbs in ((m_woutT, woutT, KD * 4096, w2cs),
                                 (m_wqkvT, wqkvT, 48 * 2048, w3cs),
                                 (m_wout1T, wout1T, KD * 2048, w4cs)):
        for ci, c0 in enumerate(range(0, cols, WCSTEP)):
            _drip_pend.append((dst, srcw, c0, cbs[ci]))

    def drip(n):
        for _ in range(min(n, len(_drip_pend))):
            dst, srcw, c0, bb = _drip_pend.pop(0)
            dma(GP, dst[:, c0:c0 + WCSTEP], srcw[:, c0:c0 + WCSTEP],
                writes=[bb])

    def sw0_prep():
        for m in range(KD):
            wq = b_u if m % 2 == 0 else b_x1
            sgn = Buf(Q1.ap[:, 0:2048], base=Q1) if m % 2 == 0 else \
                Buf(Q2.ap[:, 0:2048], base=Q2)
            st = b_st1 if m % 2 == 0 else b_st2
            mm_ = b_m if m % 2 == 0 else b_m2
            dma(SY, wq[:, :], wf0[m * 128:(m + 1) * 128, :], writes=[wq])
            step(SC, lambda e, wq=wq, sgn=sgn, st=st: e.activation(sgn[:, :],
                 wq[:, :], AF.Copy, accum_out=st[:, :]), reads=[wq],
                 writes=[sgn, st])
            step(VE, lambda e, st=st, mm_=mm_: e.tensor_scalar(mm_[:, :], st[:, :],
                 -1.0 / D, None, ALU.mult), reads=[st], writes=[mm_])
            step(SC, lambda e, wq=wq, sgn=sgn, mm_=mm_: e.activation(sgn[:, :],
                 wq[:, :], AF.Sign, bias=mm_[:, :]), reads=[wq, mm_], writes=[sgn])
            for g in range(4):
                ptr = PTRa if g % 2 == 0 else PTRb
                eb = b_e if g % 2 == 0 else b_ecl
                def tpw(e, g=g, ptr=ptr, sgn=sgn):
                    for i in range(4):
                        j = g * 4 + i
                        ins = e.matmul(ptr[:, i * 128:(i + 1) * 128],
                                       sgn[:, j * 128:(j + 1) * 128],
                                       c_id_bf[:, :], start=(i == 0), stop=(i == 3))
                    return ins
                step(TN, tpw, reads=[sgn, c_id_bf], writes=[ptr])
                step(SC if g % 2 == 0 else VE,
                     lambda e, ptr=ptr, eb=eb: (e.copy if ptr is PTRa else e.tensor_copy)(
                     eb[:, 0:512], r3(ptr.ap, 128)), reads=[ptr], writes=[eb])
                dma(SC, sc_sw0T[:, m * 2048 + g * 512: m * 2048 + (g + 1) * 512],
                    eb[:, 0:512], reads=[eb], writes=[sw0ms[m]])

    def pass1_ss(ss):
        for tt in range(-1, NT):
            par = (tt + 1) % 2
            ub = b_u if par == 0 else b_x1
            xnb = b_xn if par == 0 else Buf(Q1.ap[:, 0:2048], base=Q1)
            v2b = b_v2 if par == 0 else Buf(Q2.ap[:, 0:2048], base=Q2)
            dmp = b_dump if par == 0 else Buf(Q4.ap[:, 0:2048], base=Q4)
            st1 = b_st1 if par == 0 else b_st1B
            st2 = b_st2 if par == 0 else b_st2B
            mc = b_m if par == 0 else b_mB
            rc = b_r if par == 0 else b_rB
            s1c = b_sc1 if par == 0 else b_sc1B
            s2c = b_sc2 if par == 0 else b_sc2B
            r0 = ss * SS + (tt + 1) * 128
            dma(SY, ub[:, :], u_d[r0:r0 + 128, :], writes=[ub])
            step(SC, lambda e, ub=ub, dmp=dmp, st1=st1: e.activation(dmp[:, :],
                 ub[:, :], AF.Copy, accum_out=st1[:, :]), reads=[ub],
                 writes=[dmp, st1])
            step(SC, lambda e, ub=ub, dmp=dmp, st2=st2: e.activation(dmp[:, :],
                 ub[:, :], AF.Square, accum_out=st2[:, :]), reads=[ub],
                 writes=[dmp, st2])
            step(VE, lambda e, st1=st1, mc=mc: e.tensor_scalar(mc[:, :],
                 st1[:, :], 1.0 / D, None, ALU.mult), reads=[st1], writes=[mc])
            step(VE, lambda e, st2=st2, s1c=s1c: e.tensor_scalar(s1c[:, :],
                 st2[:, :], 1.0 / D, None, ALU.mult), reads=[st2], writes=[s1c])
            step(VE, lambda e, mc=mc, s2c=s2c: e.tensor_mul(s2c[:, :], mc[:, :],
                 mc[:, :]), reads=[mc], writes=[s2c])
            step(VE, lambda e, s1c=s1c, s2c=s2c: e.tensor_sub(s1c[:, :],
                 s1c[:, :], s2c[:, :]), reads=[s1c, s2c], writes=[s1c])
            step(SC, lambda e, s1c=s1c, s2c=s2c: e.activation(s2c[:, :],
                 s1c[:, :], AF.Sqrt, bias=c_epsln[:, :]),
                 reads=[s1c, c_epsln], writes=[s2c])
            step(VE, lambda e, s2c=s2c, rc=rc: e.reciprocal(rc[:, :], s2c[:, :]),
                 reads=[s2c], writes=[rc])
            step(VE, lambda e, ub=ub, mc=mc, rc=rc, xnb=xnb: e.tensor_scalar(
                 xnb[:, :], ub[:, :], mc[:, :], rc[:, :], ALU.subtract,
                 ALU.mult), reads=[ub, mc, rc], writes=[xnb])
            for g in range(4):
                ptr = PTRa if g % 2 == 0 else PTRb
                def tp(e, g=g, ptr=ptr, xnb=xnb):
                    for i in range(4):
                        ins = e.matmul(ptr[:, i * 128:(i + 1) * 128],
                                       xnb[:, (g * 4 + i) * 128:(g * 4 + i + 1) * 128],
                                       c_id_bf[:, :], start=(i == 0), stop=(i == 3))
                    return ins
                step(TN, tp, reads=[xnb, c_id_bf], writes=[ptr])
                col = (tt + 1) * 128
                step(SC if g % 2 == 0 else VE,
                     lambda e, g=g, col=col, ptr=ptr:
                     (e.copy if ptr is PTRa else e.tensor_copy)(
                     vB_xnfm[:, g * 4:(g + 1) * 4, col:col + 128],
                     r3(ptr.ap, 128)), reads=[ptr], writes=[arB])
            if tt < 0:
                continue
            step(SC, lambda e, xnb=xnb, dmp=dmp, st1=st1: e.activation(dmp[:, :],
                 xnb[:, :], AF.Copy, accum_out=st1[:, :]), reads=[xnb],
                 writes=[dmp, st1])
            step(SC, lambda e, xnb=xnb, dmp=dmp, st2=st2: e.activation(dmp[:, :],
                 xnb[:, :], AF.Square, accum_out=st2[:, :]), reads=[xnb],
                 writes=[dmp, st2])
            step(VE, lambda e, st1=st1, mc=mc: e.tensor_scalar(mc[:, :],
                 st1[:, :], 1.0 / D, None, ALU.mult), reads=[st1], writes=[mc])
            step(VE, lambda e, st2=st2, s1c=s1c: e.tensor_scalar(s1c[:, :],
                 st2[:, :], 1.0 / D, None, ALU.mult), reads=[st2], writes=[s1c])
            step(VE, lambda e, mc=mc, s2c=s2c: e.tensor_mul(s2c[:, :], mc[:, :],
                 mc[:, :]), reads=[mc], writes=[s2c])
            step(VE, lambda e, s1c=s1c, s2c=s2c: e.tensor_sub(s1c[:, :],
                 s1c[:, :], s2c[:, :]), reads=[s1c, s2c], writes=[s1c])
            step(SC, lambda e, s1c=s1c, s2c=s2c: e.activation(s2c[:, :],
                 s1c[:, :], AF.Sqrt, bias=c_epsln[:, :]),
                 reads=[s1c, c_epsln], writes=[s2c])
            step(VE, lambda e, s2c=s2c, rc=rc: e.reciprocal(rc[:, :], s2c[:, :]),
                 reads=[s2c], writes=[rc])
            step(VE, lambda e, xnb=xnb, mc=mc, rc=rc, v2b=v2b: e.tensor_scalar(
                 v2b[:, :], xnb[:, :], mc[:, :], rc[:, :], ALU.subtract,
                 ALU.mult), reads=[xnb, mc, rc], writes=[v2b])
            for g in range(4):
                ptr = PTRa if g % 2 == 0 else PTRb
                def tps(e, g=g, ptr=ptr, v2b=v2b):
                    for i in range(4):
                        ins = e.matmul(ptr[:, i * 128:(i + 1) * 128],
                                       v2b[:, (g * 4 + i) * 128:(g * 4 + i + 1) * 128],
                                       c_id_bf[:, :], start=(i == 0), stop=(i == 3))
                    return ins
                step(TN, tps, reads=[v2b, c_id_bf], writes=[ptr])
                step(SC if g % 2 == 0 else VE,
                     lambda e, g=g, tt=tt, ptr=ptr:
                     (e.copy if ptr is PTRa else e.tensor_copy)(
                     vA_sgn[:, g * 4:(g + 1) * 4, tt * 128:(tt + 1) * 128],
                     r3(ptr.ap, 128)), reads=[ptr], writes=[arA])
        for k in range(KD):
            step(VE, lambda e, k=k: e.tensor_scalar(vA_sgn[:, k, :],
                 vA_sgn[:, k, :], c_gc[:, k:k + 1], c_bc[:, k:k + 1],
                 ALU.mult, ALU.add), reads=[arA, c_gc, c_bc], writes=[arA])
            step(SC, lambda e, k=k: e.activation(vA_sgn[:, k, :], vA_sgn[:, k, :],
                 AF.Sign), reads=[arA], writes=[arA])
        if ss == 0:
            sw0_prep()

        for m in range(KD):
            wb = b_w0 if m % 2 == 0 else b_w1
            pp = BA if m % 2 == 0 else BB
            g0t = b_g0t if m % 2 == 0 else b_ca1
            g0b = b_g0b if m % 2 == 0 else b_e
            dma(SY, wb[:, 0:2048],
                sc_sw0T[:, m * 2048:(m + 1) * 2048],
                reads=[sw0ms[m]], writes=[wb])
            def mmf(e, wb=wb, pp=pp):
                for k in range(KD):
                    for n0 in (0, 512):
                        ins = e.matmul(pp[:, n0:n0 + 512],
                                       wb[:, k * 128:(k + 1) * 128],
                                       vA_sgn[:, k, n0:n0 + 512],
                                       start=(k == 0), stop=(k == KD - 1))
                return ins
            step(TN, mmf, reads=[wb, arA], writes=[pp])
            step(VE, lambda e, m=m, pp=pp, g0t=g0t: e.tensor_scalar(
                g0t[:, :], pp[:, 0:1024], c_f0b[:, m:m + 1], c_f0a[:, m:m + 1],
                ALU.add, ALU.mult), reads=[pp, c_f0b, c_f0a], writes=[g0t])
            step(SC, lambda e, g0t=g0t, g0b=g0b: e.activation(g0b[:, :],
                 g0t[:, :], AF.Silu), reads=[g0t], writes=[g0b])
            dma(SC, sc_g0[m * 128:(m + 1) * 128, ss * SS:(ss + 1) * SS], g0b[:, :],
                reads=[g0b], writes=[scr_g0])
        pjoin()

        zps = [BA[:, 0:512], BA[:, 512:1024], BB[:, 0:512], BB[:, 512:1024]]
        vA_wz4 = arA.ap.rearrange("p (s k c) -> p s k c", k=KD, c=128)
        for rnd in range(2):
            for h in range(2):
                c0 = rnd * 32768 + h * 16384
                dma(GP, arA.ap[:, h * 16384:(h + 1) * 16384],
                    winT[:, c0:c0 + 16384], writes=[arA])
            for tt in range(NT):
                def mmz(e, rnd=rnd, tt=tt):
                    for k in range(KD):
                        for zq in range(4):
                            ins = e.matmul(zps[zq],
                                           vB_xnfm[:, k, (tt + 1) * 128:(tt + 2) * 128],
                                           vA_wz4[:, zq * 4:(zq + 1) * 4, k, :],
                                           start=(k == 0), stop=(k == KD - 1))
                    return ins
                step(TN, mmz, reads=[arA, arB], writes=[BA, BB])
                for zq in range(4):
                    step(VE, lambda e, zq=zq: e.tensor_copy(b_zt[:, :], zps[zq]),
                         reads=[BA, BB], writes=[b_zt])
                    dma(SC, sc_z[ss * SS + tt * 128: ss * SS + (tt + 1) * 128,
                                 rnd * 2048 + zq * 512: rnd * 2048 + (zq + 1) * 512],
                        b_zt[:, :], reads=[b_zt], writes=[scr_z])


        for m in range(35):
            wb = b_w0 if m % 2 == 0 else b_w1
            mw = 64 if m == 34 else 128
            dma(GP, wb[:, 0:KD * mw],
                winT[:, (32 + m) * 2048: (32 + m) * 2048 + KD * mw],
                writes=[wb])
            def mmc(e, wb=wb, mw=mw):
                for k in range(KD):
                    for dst, n0, nl in ((BA, 0, 512), (BA, 512, 512), (BB, 1024, 128)):
                        ins = e.matmul(dst.ap[0:mw, n0 - (1024 if dst is BB else 0):
                                              n0 - (1024 if dst is BB else 0) + nl],
                                       wb[:, k * mw:(k + 1) * mw],
                                       vB_xnfm[:, k, n0:n0 + nl],
                                       start=(k == 0), stop=(k == KD - 1))
                return ins
            step(TN, mmc, reads=[wb, arB], writes=[BA, BB])
            if m == 34:
                step(SC, lambda e: e.activation(b_dtraw[:, 0:1024], BA[0:H, 0:1024],
                     AF.Copy), reads=[BA], writes=[b_dtraw])
                step(SC, lambda e: e.activation(b_dtraw[:, 1024:1152], BB[0:H, 0:128],
                     AF.Copy), reads=[BB], writes=[b_dtraw])
                step(SC, lambda e: e.activation(b_dtraw[:, :], b_dtraw[:, :], AF.Exp,
                     bias=c_dtb[:, :]), reads=[b_dtraw, c_dtb], writes=[b_dtraw])
                step(SC, lambda e: e.activation(b_dtraw[:, :], b_dtraw[:, :], AF.Ln,
                     bias=c_one[0:H, :]), reads=[b_dtraw, c_one], writes=[b_dtraw])
                step(VE, lambda e: e.tensor_scalar(b_adt[:, :], b_dtraw[:, 128:1152],
                     c_A[:, :], None, ALU.mult), reads=[b_dtraw, c_A], writes=[b_adt])
                step(VE, lambda e: e.tensor_tensor_scan(b_csg[:, :], b_adt[:, :],
                     b_adt[:, :], b_csgc[:, 0:1], ALU.add, ALU.bypass),
                     reads=[b_adt, b_csgc], writes=[b_csg])
                dma(SC, sc_csg[:, ss * SS:(ss + 1) * SS], b_csg[:, :],
                    reads=[b_csg], writes=[scr_csg])
                continue
            step(SC, lambda e: e.activation(b_xbct[:, 0:1024], BA[:, 0:1024],
                 AF.Copy), reads=[BA], writes=[b_xbct])
            step(SC, lambda e: e.activation(b_xbct[:, 1024:1152], BB[:, 0:128],
                 AF.Copy), reads=[BB], writes=[b_xbct])
            step(VE, lambda e, m=m: e.tensor_scalar(b_ca0[:, :], b_xbct[:, 125:1149],
                 c_wcv[:, m * 4:m * 4 + 1], None, ALU.mult),
                 reads=[b_xbct, c_wcv], writes=[b_ca0])
            step(VE, lambda e, m=m: e.scalar_tensor_tensor(b_ca1[:, :],
                 b_xbct[:, 126:1150], c_wcv[:, m * 4 + 1:m * 4 + 2], b_ca0[:, :],
                 ALU.mult, ALU.add), reads=[b_xbct, c_wcv, b_ca0], writes=[b_ca1])
            step(VE, lambda e, m=m: e.scalar_tensor_tensor(b_ca0[:, :],
                 b_xbct[:, 127:1151], c_wcv[:, m * 4 + 2:m * 4 + 3], b_ca1[:, :],
                 ALU.mult, ALU.add), reads=[b_xbct, c_wcv, b_ca1], writes=[b_ca0])
            step(VE, lambda e, m=m: e.scalar_tensor_tensor(b_ca1[:, :],
                 b_xbct[:, 128:1152], c_wcv[:, m * 4 + 3:m * 4 + 4], b_ca0[:, :],
                 ALU.mult, ALU.add), reads=[b_xbct, c_wcv, b_ca0], writes=[b_ca1])
            if m < 32:
                step(SC, lambda e, m=m: e.activation(b_xt[:, 0:1024], b_ca1[:, :],
                     AF.Silu, bias=c_bcv[:, m:m + 1]),
                     reads=[b_ca1, c_bcv], writes=[b_xt])
                for g in range(2):
                    ptr = PTRa if g % 2 == 0 else PTRb
                    def tpx(e, g=g, ptr=ptr):
                        for i in range(4):
                            ins = e.matmul(ptr[:, i * 128:(i + 1) * 128],
                                           b_xt[:, (g * 4 + i) * 128:(g * 4 + i + 1) * 128],
                                           c_id_bf[:, :], start=(i == 0), stop=(i == 3))
                        return ins
                    step(TN, tpx, reads=[b_xt, c_id_bf], writes=[ptr])
                    step(SC if ptr is PTRa else VE, lambda e, m=m, g=g, ptr=ptr: (e.copy if ptr is PTRa else e.tensor_copy)(
                        vA_x[:, g * 4:(g + 1) * 4, m * 128:(m + 1) * 128],
                        r3(ptr.ap, 128)), reads=[ptr], writes=[arA])
            else:
                dstb = b_Bfm if m == 32 else b_Cfm
                step(SC, lambda e, m=m, dstb=dstb: e.activation(
                    dstb[:, 0:1024], b_ca1[:, :], AF.Silu, bias=c_bcv[:, m:m + 1]),
                    reads=[b_ca1, c_bcv], writes=[dstb])
                if m == 33:
                    dma(SC, sc_c[:, ss * SS:(ss + 1) * SS], b_Cfm[:, :],
                        reads=[b_Cfm], writes=[scr_c])

        pjoin()
        for c in range(NCH):
            drip(3)
            par = c % 2
            cs = b_cs if par == 0 else b_csB
            cshi = b_cshi if par == 0 else b_cshiB
            cslo = b_cslo if par == 0 else b_csloB
            loF = b_e1 if par == 0 else b_e1B
            dd = b_dd if par == 0 else b_ddB
            ds = b_ds if par == 0 else b_dsB
            dec4 = b_dec4 if par == 0 else b_dec4B
            decb = b_decb if par == 0 else b_decbB
            dtot = b_dtot if par == 0 else b_dtotB
            dtr = b_dtr if par == 0 else Buf(b_csg128.ap[64:65, 192:256], base=b_csg128)
            smb = b_smb if par == 0 else b_smbB
            Btm = b_Btm if par == 0 else b_BtmB
            fhl = b_fhl0 if par == 0 else b_fhl1
            if c == 0:
                prevcol = b_csgc
                prevsl = (slice(None), slice(0, 1))
            else:
                prevcol = b_csg
                prevsl = (slice(None), slice(c * 128 - 1, c * 128))
            step(VE, lambda e, c=c, pc=prevcol, psl=prevsl, cs=cs: e.tensor_scalar(
                cs[:, :], b_csg[:, c * 128:(c + 1) * 128], pc[psl], None,
                ALU.subtract), reads=[b_csg, prevcol], writes=[cs])
            step(VE, lambda e, cs=cs, cshi=cshi: e.tensor_copy(cshi[:, :],
                 cs[:, :]), reads=[cs], writes=[cshi])
            step(VE, lambda e, cs=cs, cshi=cshi, loF=loF: e.tensor_tensor(
                 loF[:, :], cs[:, :], cshi[:, :], ALU.subtract),
                 reads=[cs, cshi], writes=[loF])
            step(VE, lambda e, loF=loF, cslo=cslo: e.tensor_copy(cslo[:, :],
                 loF[:, :]), reads=[loF], writes=[cslo])
            dma(GP, fhl[0:1, :], cshi[0:32, :], reads=[cshi], writes=[fhl])
            dma(GP, fhl[1:2, :], cslo[0:32, :], reads=[cslo], writes=[fhl])
            dma(GP, fhl[32:33, :], cshi[32:64, :], reads=[cshi], writes=[fhl])
            dma(GP, fhl[33:34, :], cslo[32:64, :], reads=[cslo], writes=[fhl])
            step(SC, lambda e, cs=cs, dd=dd: e.activation(dd[:, :], cs[:, :],
                 AF.Exp), reads=[cs], writes=[dd])
            step(SC, lambda e, cs=cs, ds=ds: e.activation(ds[:, :], cs[:, :],
                 AF.Exp, bias=cs[:, 127:128], scale=-1.0), reads=[cs],
                 writes=[ds])
            def tpd(e, dd=dd, ds=ds):
                for i, s4 in enumerate((dd, ds)):
                    ins = e.matmul(PD[:, i * 64:(i + 1) * 64], s4[:, :],
                                   c_id_f[0:H, 0:H], start=(i == 0),
                                   stop=(i == 1))
                return ins
            step(TN, tpd, reads=[dd, ds, c_id_f], writes=[PD])
            step(VE, lambda e, dec4=dec4: e.tensor_copy(dec4[:, 0:128],
                 PD[:, 0:128]), reads=[PD], writes=[dec4])
            step(VE, lambda e, dec4=dec4, decb=decb: e.tensor_copy(decb[:, :],
                 dec4[:, 0:128]), reads=[dec4], writes=[decb])
            step(TN, lambda e, dd=dd: e.matmul(PD[0:1, 384:448], dd[:, 127:128],
                 c_id_f[0:H, 0:H], start=True, stop=True),
                 reads=[dd, c_id_f], writes=[PD])
            step(VE, lambda e, dtr=dtr: e.tensor_copy(dtr[:, :],
                 PD[0:1, 384:448]), reads=[PD], writes=[dtr])
            step(TN, lambda e, dtr=dtr: e.matmul(PD[:, 256:320],
                 c_onesf[0:1, 0:128], dtr[0:1, :], start=True, stop=True),
                 reads=[c_onesf, dtr], writes=[PD])
            step(VE, lambda e, dtot=dtot: e.tensor_copy(dtot[:, :],
                 PD[:, 256:320]), reads=[PD], writes=[dtot])
            step(TN, lambda e, c=c: e.matmul(PD[:, 320:448],
                 b_Bfm[:, c * 128:(c + 1) * 128], b_Cfm[:, c * 128:(c + 1) * 128],
                 start=True, stop=True), reads=[b_Bfm, b_Cfm], writes=[PD])
            step(VE, lambda e, smb=smb: e.tensor_mul(smb[:, :], PD[:, 320:448],
                 c_triu[:, :]), reads=[PD, c_triu], writes=[smb])
            ptrB = PTRa if par == 0 else PTRb
            step(TN, lambda e, c=c, ptrB=ptrB: e.matmul(ptrB[:, 0:128],
                 b_Bfm[:, c * 128:(c + 1) * 128], c_id_bf[:, :],
                 start=True, stop=True),
                 reads=[b_Bfm, c_id_bf], writes=[ptrB])
            step(SC, lambda e, Btm=Btm, ptrB=ptrB: e.copy(Btm[:, :],
                 ptrB[:, 0:128]), reads=[ptrB], writes=[Btm])
            step(VE, lambda e, c=c, decb=decb: e.tensor_tensor(r3(b_dx.ap, PH),
                 r3(vA_x[:, c, :], PH), brep(decb[:, 64:128], PH), ALU.mult),
                 reads=[arA, decb], writes=[b_dx])
            step(VE, lambda e, c=c: e.tensor_tensor(r3(b_dpx.ap, PH),
                 r3(vA_x[:, c, :], PH), brep(c_dp[:, :], PH), ALU.mult),
                 reads=[arA, c_dp], writes=[b_dpx])
            for grp in range(8):
                    pd = BA if grp % 2 == 0 else BB
                    def mmdiff(e, grp=grp, pd=pd, fhl=fhl):
                        for j in range(8):
                            h = grp * 8 + j
                            hr = 0 if h < 32 else 32
                            off = (h % 32) * 128
                            col = slice(j * 128, (j + 1) * 128)
                            first = (j == 0) or (j == 4)
                            last = (j == 3) or (j == 7)
                            e.matmul(pd[:, col], fhl[hr:hr + 2, off:off + 128],
                                     c_nones[hr:hr + 2, 0:128], start=first,
                                     stop=False)
                            e.matmul(pd[:, col], c_ones[hr:hr + 2, 0:128],
                                     fhl[hr:hr + 2, off:off + 128],
                                     start=False, stop=False)
                            ins = e.matmul(pd[:, col], c_id_bf[:, :],
                                           c_negm[:, :], start=False, stop=last)
                        return ins
                    step(TN, mmdiff, reads=[fhl, c_ones, c_nones,
                         c_id_bf, c_negm], writes=[pd])
                    step(SC, lambda e, pd=pd: e.activation(b_exps[:, :],
                         pd[:, 0:1024], AF.Exp), reads=[pd], writes=[b_exps])
                    step(VE, lambda e, smb=smb: e.tensor_tensor(
                         r3(b_mt[:, 0:1024], 128), r3(b_exps.ap, 128),
                         smb.ap.unsqueeze(1).broadcast_to([128, 8, 128]),
                         ALU.mult), reads=[b_exps, smb], writes=[b_mt])
                    step(TN, lambda e, c=c, grp=grp: e.matmul(PYO[:, :],
                         b_Cfm[:, c * 128:(c + 1) * 128],
                         st_bf[:, grp * 512:(grp + 1) * 512],
                         start=True, stop=True),
                         reads=[b_Cfm, st_bf], writes=[PYO])
                    def mmyd(e, c=c, grp=grp):
                        for i in range(8):
                            h = grp * 8 + i
                            ins = e.matmul(PYD[:, i * 64:(i + 1) * 64],
                                           b_mt[:, i * 128:(i + 1) * 128],
                                           vA_x[:, c, h * 64:(h + 1) * 64],
                                           start=(i == 0), stop=(i == 7))
                        return ins
                    step(TN, mmyd, reads=[b_mt, arA], writes=[PYD])
                    step(VE, lambda e, grp=grp, dec4=dec4: e.tensor_tensor(
                         r3(b_u2.ap, PH), r3(PYO.ap, PH),
                         brep(dec4[:, grp * 8:(grp + 1) * 8], PH),
                         ALU.mult), reads=[PYO, dec4], writes=[b_u2])
                    step(VE, lambda e: e.tensor_add(b_u1[:, :], b_u2[:, :],
                         PYD[:, :]), reads=[b_u2, PYD], writes=[b_u1])
                    step(VE, lambda e, grp=grp: e.tensor_add(
                         b_yp[:, grp * 512:(grp + 1) * 512], b_u1[:, :],
                         b_dpx[:, grp * 512:(grp + 1) * 512]),
                         reads=[b_u1, b_dpx], writes=[b_yp])
                    step(TN, lambda e, grp=grp, Btm=Btm: e.matmul(PYO[:, :],
                         Btm[:, :], b_dx[:, grp * 512:(grp + 1) * 512],
                         start=True, stop=True),
                         reads=[Btm, b_dx], writes=[PYO])
                    step(VE, lambda e, grp=grp, dtot=dtot: e.tensor_tensor(
                         r3(b_su.ap, PH),
                         r3(st_bf[:, grp * 512:(grp + 1) * 512], PH),
                         brep(dtot[:, grp * 8:(grp + 1) * 8], PH), ALU.mult),
                         reads=[st_bf, dtot], writes=[b_su])
                    step(VE, lambda e, grp=grp: e.tensor_add(
                         st_bf[:, grp * 512:(grp + 1) * 512], b_su[:, :],
                         PYO[:, :]), reads=[b_su, PYO], writes=[st_bf])
            dma(SC, sc_ypart[ss * SS + c * 128: ss * SS + (c + 1) * 128, :],
                b_yp[:, :], reads=[b_yp], writes=[scr_ypart])
        step(VE, lambda e: e.tensor_copy(b_csgc[:, :], b_csg[:, 1023:1024]),
             reads=[b_csg], writes=[b_csgc])

    for ss in range(NSS):
        pass1_ss(ss)
    drip(len(_drip_pend))
    pjoin()

    step(SC, lambda e: e.activation(b_nc0[:, :], b_csgc[:, :], AF.Exp),
         reads=[b_csgc], writes=[b_nc0])
    step(TN, lambda e: e.matmul(PD[0:1, 0:H], b_nc0[:, :], c_id_f[0:H, 0:H],
         start=True, stop=True), reads=[b_nc0, c_id_f], writes=[PD])
    step(VE, lambda e: e.tensor_copy(b_gst[0:1, 0:H], PD[0:1, 0:H]),
         reads=[PD], writes=[b_gst])
    dma(SY, ag1d_i[:, :], b_gst[0:1, 0:H], reads=[b_gst], writes=[scr_ag])
    dma(SY, ag1s_i[:, :], st_bf[:, :], reads=[st_bf], writes=[scr_ag])
    step(GP, lambda e: e.collective_compute(
        "AllGather", ALU.bypass,
        replica_groups=[[0, 1, 2, 3], [4, 5, 6, 7]],
        ins=[ag1s_i.opt()], outs=[ag1s_o.opt()]),
        reads=[scr_ag], writes=[scr_ag], inc="cc")
    step(GP, lambda e: e.collective_compute(
        "AllGather", ALU.bypass,
        replica_groups=[[0, 1, 2, 3], [4, 5, 6, 7]],
        ins=[ag1d_i.opt()], outs=[ag1d_o.opt()]),
        reads=[scr_ag], writes=[scr_ag], inc="cc")
    for i in range(GROUP):
        dma(SY, vA_x[:, i, :], ag1s_o[i * NST:(i + 1) * NST, :],
            reads=[scr_ag], writes=[arA])
    acc_f = arA.ap[:, 16384:24576].bitcast(F32)
    si_f = arA.ap[:, 24576:32768].bitcast(F32)
    step(GP, lambda e: e.memset(acc_f, 0.0), writes=[arA])
    for i in range(GROUP - 1):
        dma(SY, b_dtr[:, :], ag1d_o[i:i + 1, :], reads=[scr_ag], writes=[b_dtr])
        step(TN, lambda e, i=i: e.matmul(PD[:, 0:H], c_onesf[0:1, 0:128],
             b_dtr[0:1, :], start=True, stop=True),
             reads=[c_onesf, b_dtr], writes=[PD])
        step(VE, lambda e, i=i: e.tensor_scalar(b_dmod[:, :], PD[:, 0:H],
             c_m8[:, i:i + 1], c_m8[:, 3 + i:4 + i], ALU.mult, ALU.add),
             reads=[PD, c_m8], writes=[b_dmod])
        step(VE, lambda e, i=i: e.tensor_scalar(
             si_f, vA_x[:, i, :], c_m8[:, i:i + 1], None, ALU.mult),
             reads=[arA, c_m8], writes=[arA])
        step(VE, lambda e: e.tensor_tensor(
             acc_f.rearrange("p (a b) -> p a b", b=PH),
             acc_f.rearrange("p (a b) -> p a b", b=PH),
             brep(b_dmod[:, :], PH), ALU.mult), reads=[arA, b_dmod],
             writes=[arA])
        step(VE, lambda e: e.tensor_tensor(acc_f, acc_f, si_f, ALU.add),
             reads=[arA], writes=[arA])
    step(VE, lambda e: e.tensor_copy(st_bf[:, :], acc_f),
         reads=[arA], writes=[st_bf])
    dma(GP, dbg_sin[:, :], st_bf[:, :], reads=[st_bf], writes=[scr_ag])

    for m in range(KD):
        wq = b_u if m % 2 == 0 else b_x1
        qq = Buf(Q1.ap[:, 0:2048], base=Q1) if m % 2 == 0 else \
            Buf(Q2.ap[:, 0:2048], base=Q2)
        ab = Buf(Q3.ap[:, 0:2048], base=Q3) if m % 2 == 0 else \
            Buf(Q4.ap[:, 0:2048], base=Q4)
        st = b_st1 if m % 2 == 0 else b_st2
        th = b_m if m % 2 == 0 else b_m2
        dma(SY, wq[:, :], wf1[m * 128:(m + 1) * 128, :], writes=[wq])
        step(SC, lambda e, wq=wq, ab=ab, st=st: e.activation(ab[:, :], wq[:, :],
             AF.Abs, accum_out=st[:, :]), reads=[wq], writes=[ab, st])
        step(VE, lambda e, st=st, th=th: e.tensor_scalar(th[:, :], st[:, :],
             0.7 / D, None, ALU.mult), reads=[st], writes=[th])
        step(VE, lambda e, ab=ab, th=th: e.tensor_scalar(ab[:, :], ab[:, :],
             th[:, :], None, ALU.is_gt), reads=[ab, th], writes=[ab])
        step(SC, lambda e, wq=wq, qq=qq: e.activation(qq[:, :], wq[:, :],
             AF.Sign), reads=[wq], writes=[qq])
        step(VE, lambda e, qq=qq, ab=ab: e.tensor_mul(qq[:, :], qq[:, :],
             ab[:, :]), reads=[qq, ab], writes=[qq])
        for g in range(4):
            ptr = PTRa if g % 2 == 0 else PTRb
            eb = b_e if g % 2 == 0 else b_ecl
            def tpq(e, g=g, ptr=ptr, qq=qq):
                for i in range(4):
                    j = g * 4 + i
                    ins = e.matmul(ptr[:, i * 128:(i + 1) * 128],
                                   qq[:, j * 128:(j + 1) * 128],
                                   c_id_bf[:, :], start=(i == 0), stop=(i == 3))
                return ins
            step(TN, tpq, reads=[qq, c_id_bf], writes=[ptr])
            step(SC if g % 2 == 0 else VE,
                 lambda e, ptr=ptr, eb=eb: (e.copy if ptr is PTRa else e.tensor_copy)(
                 eb[:, 0:512], r3(ptr.ap, 128)), reads=[ptr], writes=[eb])
            dma(SC, sc_qwT[:, m * 2048 + g * 512: m * 2048 + (g + 1) * 512],
                eb[:, 0:512], reads=[eb], writes=[qwms[m]])

    def pass2_ss(ss):
        pjoin()
        for c in range(NCH):
            t0 = ss * SS + c * 128
            dma(GP, b_ypl[:, :], sc_ypart[t0:t0 + 128, :],
                reads=[scr_ypart], writes=[b_ypl])
            dma(GP, b_zl[:, :], sc_z[t0:t0 + 128, :], reads=[scr_z], writes=[b_zl])
            dma(SY, b_cfm2[:, 0:128], sc_c[:, t0:t0 + 128],
                reads=[scr_c], writes=[b_cfm2])
            dma(SY, b_dtg[:, :], sc_csg[:, t0:t0 + 128],
                reads=[scr_csg], writes=[b_dtg])
            step(SC, lambda e: e.activation(b_dtg[:, :], b_dtg[:, :], AF.Exp),
                 reads=[b_dtg], writes=[b_dtg])
            step(TN, lambda e: e.matmul(PD[:, 0:H], b_dtg[:, :],
                 c_id_f[0:H, 0:H], start=True, stop=True),
                 reads=[b_dtg, c_id_f], writes=[PD])
            step(VE, lambda e: e.tensor_copy(b_dcg[:, :], PD[:, 0:H]),
                 reads=[PD], writes=[b_dcg])
            step(SC, lambda e: e.activation(b_sq[:, :], b_zl[:, :], AF.Silu),
                 reads=[b_zl], writes=[b_sq])
            for grp in range(8):
                step(TN, lambda e, grp=grp: e.matmul(PYD[:, :], b_cfm2[:, 0:128],
                     st_bf[:, grp * 512:(grp + 1) * 512], start=True, stop=True),
                     reads=[b_cfm2, st_bf], writes=[PYD])
                step(VE, lambda e, grp=grp: e.tensor_tensor(r3(b_u1.ap, PH),
                     r3(PYD.ap, PH),
                     brep(b_dcg[:, grp * 8:(grp + 1) * 8], PH), ALU.mult),
                     reads=[PYD, b_dcg], writes=[b_u1])
                step(VE, lambda e, grp=grp: e.tensor_add(
                     b_t2[:, grp * 512:(grp + 1) * 512], b_u1[:, :],
                     b_ypl[:, grp * 512:(grp + 1) * 512]),
                     reads=[b_u1, b_ypl], writes=[b_t2])
            if ss == 0 and c == 0:
                dma(GP, dbg_t2[:, :], b_t2[:, :], reads=[b_t2], writes=[scr_ag])
                dma(GP, dbg_sz[:, :], b_sq[:, :], reads=[b_sq], writes=[scr_ag])
            step(VE, lambda e: e.tensor_mul(b_y[:, :], b_t2[:, :], b_sq[:, :]),
                 reads=[b_t2, b_sq], writes=[b_y])
            if ss == 0 and c == 0:
                dma(GP, dbg_y[:, :], b_y[:, :], reads=[b_y], writes=[scr_ag])
            step(SC, lambda e: e.activation(b_sq[:, :], b_y[:, :], AF.Square,
                 accum_out=b_st1[:, :]), reads=[b_y], writes=[b_sq, b_st1])
            step(VE, lambda e: e.tensor_scalar(b_sc1[:, :], b_st1[:, :], 1.0 / DIN,
                 None, ALU.mult), reads=[b_st1], writes=[b_sc1])
            step(SC, lambda e: e.activation(b_sc2[:, :], b_sc1[:, :], AF.Sqrt,
                 bias=c_epsln[:, :]), reads=[b_sc1, c_epsln], writes=[b_sc2])
            step(VE, lambda e: e.reciprocal(b_r[:, :], b_sc2[:, :]),
                 reads=[b_sc2], writes=[b_r])
            step(VE, lambda e: e.tensor_scalar(b_yn[:, :], b_y[:, :],
                 b_r[:, :], None, ALU.mult), reads=[b_y, b_r], writes=[b_yn])
            if ss == 0 and c == 0:
                dma(GP, dbg_yn[:, :], b_yn[:, :], reads=[b_yn], writes=[scr_ag])
            for g in range(8):
                ptr = PTRa if g % 2 == 0 else PTRb
                def tpy(e, g=g, ptr=ptr):
                    for i in range(4):
                        ins = e.matmul(ptr[:, i * 128:(i + 1) * 128],
                                       b_yn[:, (g * 4 + i) * 128:(g * 4 + i + 1) * 128],
                                       c_id_bf[:, :], start=(i == 0), stop=(i == 3))
                    return ins
                step(TN, tpy, reads=[b_yn, c_id_bf], writes=[ptr])
                step(SC if ptr is PTRa else VE, lambda e, g=g, c=c, ptr=ptr: (e.copy if ptr is PTRa else e.tensor_copy)(
                    vA_ynfm[:, g * 4:(g + 1) * 4, c * 128:(c + 1) * 128],
                    r3(ptr.ap, 128)), reads=[ptr], writes=[arA])
        for k in range(32):
            step(VE, lambda e, k=k: e.tensor_scalar(vA_ynfm[:, k, :],
                 vA_ynfm[:, k, :], c_nwc[:, k:k + 1], None, ALU.mult),
                 reads=[arA, c_nwc], writes=[arA])
        for m in range(KD):
            pp = BA if m % 2 == 0 else BB
            sgo = b_sg1 if m % 2 == 0 else b_kv
            g0i = b_g0b if m % 2 == 0 else b_ecl
            dma(SY, b_w0[:, 0:2048],
                m_woutT[:, m * 4096: m * 4096 + 2048],
                reads=[w2cs[m]], writes=[b_w0])
            def mmo1(e, pp=pp):
                for k in range(16):
                    for n0 in (0, 512):
                        ins = e.matmul(pp[:, n0:n0 + 512],
                                       b_w0[:, k * 128:(k + 1) * 128],
                                       vA_ynfm[:, k, n0:n0 + 512],
                                       start=(k == 0), stop=False)
                return ins
            dma(SY, b_w1[:, 0:2048],
                m_woutT[:, m * 4096 + 2048: m * 4096 + 4096],
                reads=[w2cs[m]], writes=[b_w1])
            step(TN, mmo1, reads=[b_w0, arA], writes=[pp])
            def mmo2(e, pp=pp):
                for k in range(16):
                    for n0 in (0, 512):
                        ins = e.matmul(pp[:, n0:n0 + 512],
                                       b_w1[:, k * 128:(k + 1) * 128],
                                       vA_ynfm[:, 16 + k, n0:n0 + 512],
                                       start=False, stop=(k == 15))
                return ins
            step(TN, mmo2, reads=[b_w1, arA], writes=[pp])
            dma(SC, g0i[:, :], sc_g0[m * 128:(m + 1) * 128, ss * SS:(ss + 1) * SS],
                reads=[scr_g0], writes=[g0i])
            step(VE, lambda e, m=m, pp=pp, g0i=g0i: e.tensor_mul(
                 vB_xn1[:, m, :], pp[:, 0:1024], g0i[:, :]),
                 reads=[pp, g0i], writes=[arB])
        pjoin()
        for m in range(KD):
            for g in range(2):
                ptr = PTRa if g % 2 == 0 else PTRb
                def tpsg(e, m=m, g=g, ptr=ptr):
                    for i in range(4):
                        ins = e.matmul(ptr[:, i * 128:(i + 1) * 128],
                                       vB_xn1[:, m, (g * 4 + i) * 128:(g * 4 + i + 1) * 128],
                                       c_id_bf[:, :], start=(i == 0), stop=(i == 3))
                    return ins
                step(TN, tpsg, reads=[arB, c_id_bf], writes=[ptr])
                step(SC if ptr is PTRa else VE, lambda e, m=m, g=g, ptr=ptr: (e.copy if ptr is PTRa else e.tensor_copy)(
                    vA_sgtm[:, g * 4:(g + 1) * 4, m * 128:(m + 1) * 128],
                    r3(ptr.ap, 128)), reads=[ptr], writes=[arA])
        for tt in range(NT):
            r0 = ss * SS + tt * 128 + 128
            dma(GP, b_u[:, :], u_d[r0:r0 + 128, :], writes=[b_u])
            step(VE, lambda e, tt=tt: e.tensor_add(b_x1[:, :], b_u[:, :],
                 vA_sgtm[:, tt, :]), reads=[b_u, arA], writes=[b_x1])
            step(SC, lambda e: e.activation(b_x1c[:, :], b_x1[:, :], AF.Copy),
                 reads=[b_x1], writes=[b_x1c])
            dma(SC, sc_x1[ss * SS + tt * 128: ss * SS + (tt + 1) * 128, :],
                b_x1c[:, :], reads=[b_x1c], writes=[scr_x1])
            step(SC, lambda e: e.activation(b_dump[:, :], b_x1[:, :], AF.Copy,
                 accum_out=b_st1[:, :]), reads=[b_x1], writes=[b_dump, b_st1])
            step(SC, lambda e: e.activation(b_dump[:, :], b_x1[:, :], AF.Square,
                 accum_out=b_st2[:, :]), reads=[b_x1], writes=[b_dump, b_st2])
            step(VE, lambda e: e.tensor_scalar(b_m[:, :], b_st1[:, :], 1.0 / D,
                 None, ALU.mult), reads=[b_st1], writes=[b_m])
            step(VE, lambda e: e.tensor_scalar(b_sc1[:, :], b_st2[:, :], 1.0 / D,
                 None, ALU.mult), reads=[b_st2], writes=[b_sc1])
            step(VE, lambda e: e.tensor_mul(b_sc2[:, :], b_m[:, :], b_m[:, :]),
                 reads=[b_m], writes=[b_sc2])
            step(VE, lambda e: e.tensor_sub(b_sc1[:, :], b_sc1[:, :], b_sc2[:, :]),
                 reads=[b_sc1, b_sc2], writes=[b_sc1])
            step(SC, lambda e: e.activation(b_sc2[:, :], b_sc1[:, :], AF.Sqrt,
                 bias=c_epsln[:, :]), reads=[b_sc1, c_epsln], writes=[b_sc2])
            step(VE, lambda e: e.reciprocal(b_r[:, :], b_sc2[:, :]),
                 reads=[b_sc2], writes=[b_r])
            step(VE, lambda e: e.tensor_scalar(b_xn[:, :], b_x1[:, :], b_m[:, :],
                 b_r[:, :], ALU.subtract, ALU.mult),
                 reads=[b_x1, b_m, b_r], writes=[b_xn])
            for g in range(4):
                ptr = PTRa if g % 2 == 0 else PTRb
                def tp1(e, g=g, ptr=ptr):
                    for i in range(4):
                        ins = e.matmul(ptr[:, i * 128:(i + 1) * 128],
                                       b_xn[:, (g * 4 + i) * 128:(g * 4 + i + 1) * 128],
                                       c_id_bf[:, :], start=(i == 0), stop=(i == 3))
                    return ins
                step(TN, tp1, reads=[b_xn, c_id_bf], writes=[ptr])
                step(SC if ptr is PTRa else VE, lambda e, g=g, tt=tt, ptr=ptr: (e.copy if ptr is PTRa else e.tensor_copy)(
                    vB_xn1[:, g * 4:(g + 1) * 4, tt * 128:(tt + 1) * 128],
                    r3(ptr.ap, 128)), reads=[ptr], writes=[arB])
        pjoin()
        for j in range(KD):
            for kind in range(3):
                wb = b_w0 if kind % 2 == 0 else b_w1
                pp = (BA, BB, BA)[kind] if j % 2 == 0 else (BB, BA, BB)[kind]
                col = kind * D + j * 128
                dma(SY, wb[:, 0:2048],
                    m_wqkvT[:, (kind * KD + j) * 2048:(kind * KD + j + 1) * 2048],
                    reads=[w3cs[(kind * KD + j) // 2]], writes=[wb])
                def mmq(e, wb=wb, pp=pp):
                    for k in range(KD):
                        for n0 in (0, 512):
                            ins = e.matmul(pp[:, n0:n0 + 512],
                                           wb[:, k * 128:(k + 1) * 128],
                                           vB_xn1[:, k, n0:n0 + 512],
                                           start=(k == 0), stop=(k == KD - 1))
                    return ins
                step(TN, mmq, reads=[wb, arB], writes=[pp])
                if kind < 2:
                    step(SC, lambda e, pp=pp: e.activation(b_e[:, :], pp[:, 0:1024],
                         AF.Exp), reads=[pp], writes=[b_e])
                    step(VE, lambda e: e.tensor_scalar(b_ecl[:, :], b_e[:, :],
                         1.0, None, ALU.min), reads=[b_e], writes=[b_ecl])
                    dstb = b_qphi if kind == 0 else b_kphi
                    step(VE, lambda e, dstb=dstb, pp=pp: e.scalar_tensor_tensor(
                         dstb[:, :], pp[:, 0:1024], 0.0, b_ecl[:, :],
                         ALU.max, ALU.add), reads=[pp, b_ecl], writes=[dstb])
                    if kind == 0:
                        dma(SC, sc_q[j * 128:(j + 1) * 128, ss * SS:(ss + 1) * SS],
                            b_qphi[:, :], reads=[b_qphi], writes=[scr_q])
                else:
                    step(VE, lambda e, pp=pp: e.tensor_tensor(b_kv[:, :],
                         b_kphi[:, :], pp[:, 0:1024], ALU.mult),
                         reads=[b_kphi, pp], writes=[b_kv])
                    step(VE, lambda e, j=j: e.tensor_tensor_scan(b_kvc[:, :],
                         b_kv[:, :], b_kv[:, :], b_ckv[:, j:j + 1], ALU.add,
                         ALU.bypass), reads=[b_kv, b_ckv], writes=[b_kvc])
                    dma(GP, sc_kvc[j * 128:(j + 1) * 128, ss * SS:(ss + 1) * SS],
                        b_kvc[:, :], reads=[b_kvc], writes=[scr_kvc])
                    step(VE, lambda e, j=j: e.tensor_copy(b_ckv[:, j:j + 1],
                         b_kvc[:, 1023:1024]), reads=[b_kvc], writes=[b_ckv])
                    step(VE, lambda e, j=j: e.tensor_tensor_scan(b_kvc[:, :],
                         b_kphi[:, :], b_kphi[:, :], b_ckc[:, j:j + 1], ALU.add,
                         ALU.bypass), reads=[b_kphi, b_ckc], writes=[b_kvc])
                    dma(GP, sc_kc[j * 128:(j + 1) * 128, ss * SS:(ss + 1) * SS],
                        b_kvc[:, :], reads=[b_kvc], writes=[scr_kc])
                    step(VE, lambda e, j=j: e.tensor_copy(b_ckc[:, j:j + 1],
                         b_kvc[:, 1023:1024]), reads=[b_kvc], writes=[b_ckc])
        for m in range(KD):
            wb = b_w0 if m % 2 == 0 else b_w1
            pp = BA if m % 2 == 0 else BB
            g0t = b_g0t if m % 2 == 0 else b_ca1
            g0b = b_g0b if m % 2 == 0 else b_e
            dma(SY, wb[:, 0:2048],
                sc_qwT[:, m * 2048:(m + 1) * 2048],
                reads=[qwms[m]], writes=[wb])
            def mmg1(e, wb=wb, pp=pp):
                for k in range(KD):
                    for n0 in (0, 512):
                        ins = e.matmul(pp[:, n0:n0 + 512],
                                       wb[:, k * 128:(k + 1) * 128],
                                       vB_xn1[:, k, n0:n0 + 512],
                                       start=(k == 0), stop=(k == KD - 1))
                return ins
            step(TN, mmg1, reads=[wb, arB], writes=[pp])
            step(VE, lambda e, m=m, pp=pp, g0t=g0t: e.tensor_scalar(g0t[:, :],
                 pp[:, 0:1024], c_f1b[:, m:m + 1], None, ALU.add),
                 reads=[pp, c_f1b], writes=[g0t])
            step(SC, lambda e, g0t=g0t, g0b=g0b: e.activation(g0b[:, :],
                 g0t[:, :], AF.Silu), reads=[g0t], writes=[g0b])
            dma(SC, sc_g1[m * 128:(m + 1) * 128, ss * SS:(ss + 1) * SS],
                g0b[:, :], reads=[g0b], writes=[scr_g1])
        pjoin()

    for ss in range(NSS):
        pass2_ss(ss)

    dma(SY, ag2_i[0:1, :].rearrange("r (p m) -> (r p) m", m=KD), b_ckc[:, :],
        reads=[b_ckc], writes=[scr_ag])
    dma(SY, ag2_i[1:2, :].rearrange("r (p m) -> (r p) m", m=KD), b_ckv[:, :],
        reads=[b_ckv], writes=[scr_ag])
    step(GP, lambda e: e.collective_compute(
        "AllGather", ALU.bypass,
        replica_groups=[[0, 1, 2, 3], [4, 5, 6, 7]],
        ins=[ag2_i.opt()], outs=[ag2_o.opt()]),
        reads=[scr_ag], writes=[scr_ag], inc="cc")
    step(GP, lambda e: e.memset(b_offk[:, :], 0.0), writes=[b_offk])
    step(GP, lambda e: e.memset(b_offkv[:, :], 0.0), writes=[b_offkv])
    for i in range(GROUP - 1):
        for row, dst in ((2 * i, b_offk), (2 * i + 1, b_offkv)):
            dma(SY, b_gk[:, :],
                ag2_o[row:row + 1, :].rearrange("r (p m) -> (r p) m", m=KD),
                reads=[scr_ag], writes=[b_gk])
            step(VE, lambda e, i=i, dst=dst: e.scalar_tensor_tensor(dst[:, :],
                 b_gk[:, :], c_m8[:, i:i + 1], dst[:, :], ALU.mult, ALU.add),
                 reads=[b_gk, c_m8, dst], writes=[dst])

    def pass3_ss(ss):
        pjoin()
        vao = vB_ao if ss % 2 == 0 else vA_ao
        aobuf = arB if ss % 2 == 0 else arA_hi
        for m in range(KD):
            dma(SY, b_qphi[:, :], sc_q[m * 128:(m + 1) * 128, ss * SS:(ss + 1) * SS],
                reads=[scr_q], writes=[b_qphi])
            dma(GP, b_kphi[:, :], sc_kvc[m * 128:(m + 1) * 128, ss * SS:(ss + 1) * SS],
                reads=[scr_kvc], writes=[b_kphi])
            dma(GP, b_e[:, :], sc_kc[m * 128:(m + 1) * 128, ss * SS:(ss + 1) * SS],
                reads=[scr_kc], writes=[b_e])
            step(VE, lambda e, m=m: e.scalar_tensor_tensor(b_n1[:, :], b_kphi[:, :],
                 b_offkv[:, m:m + 1], b_qphi[:, :], ALU.add, ALU.mult),
                 reads=[b_kphi, b_offkv, b_qphi], writes=[b_n1])
            step(VE, lambda e, m=m: e.scalar_tensor_tensor(b_d1[:, :], b_e[:, :],
                 b_offk[:, m:m + 1], b_qphi[:, :], ALU.add, ALU.mult),
                 reads=[b_e, b_offk, b_qphi], writes=[b_d1])
            step(SC, lambda e: e.activation(b_d1[:, :], b_d1[:, :], AF.Ln,
                 bias=c_epsl1[:, :]), reads=[b_d1, c_epsl1], writes=[b_d1])
            step(SC, lambda e: e.activation(b_d1[:, :], b_d1[:, :], AF.Exp,
                 scale=-1.0), reads=[b_d1], writes=[b_d1])
            step(VE, lambda e, m=m, vao=vao: e.tensor_tensor(vao[:, m, :],
                 b_n1[:, :], b_d1[:, :], ALU.mult),
                 reads=[b_n1, b_d1], writes=[aobuf])
        for m in range(KD):
            wb = b_w0 if m % 2 == 0 else b_w1
            pp = BA if m % 2 == 0 else BB
            dma(SY, wb[:, 0:2048],
                m_wout1T[:, m * 2048:(m + 1) * 2048],
                reads=[w4cs[m // 2]], writes=[wb])
            def mms1(e, wb=wb, pp=pp, vao=vao):
                for k in range(KD):
                    for n0 in (0, 512):
                        ins = e.matmul(pp[:, n0:n0 + 512],
                                       wb[:, k * 128:(k + 1) * 128],
                                       vao[:, k, n0:n0 + 512],
                                       start=(k == 0), stop=(k == KD - 1))
                return ins
            step(TN, mms1, reads=[wb, aobuf], writes=[pp])
            step(VE, lambda e, pp=pp: e.tensor_scalar(b_s1[:, :], pp[:, 0:1024],
                 SCL_OUT1, None, ALU.mult), reads=[pp], writes=[b_s1])
            dma(SC, b_g0b[:, :], sc_g1[m * 128:(m + 1) * 128, ss * SS:(ss + 1) * SS],
                reads=[scr_g1], writes=[b_g0b])
            step(VE, lambda e: e.tensor_mul(b_sg1[:, :], b_s1[:, :], b_g0b[:, :]),
                 reads=[b_s1, b_g0b], writes=[b_sg1])
            for g in range(2):
                ptr = PTRa if g % 2 == 0 else PTRb
                def tps1(e, g=g, ptr=ptr):
                    for i in range(4):
                        ins = e.matmul(ptr[:, i * 128:(i + 1) * 128],
                                       b_sg1[:, (g * 4 + i) * 128:(g * 4 + i + 1) * 128],
                                       c_id_bf[:, :], start=(i == 0), stop=(i == 3))
                    return ins
                step(TN, tps1, reads=[b_sg1, c_id_bf], writes=[ptr])
                step(SC if ptr is PTRa else VE, lambda e, m=m, g=g, ptr=ptr: (e.copy if ptr is PTRa else e.tensor_copy)(
                    vA_sgtm[:, g * 4:(g + 1) * 4, m * 128:(m + 1) * 128],
                    r3(ptr.ap, 128)), reads=[ptr], writes=[arA])
        for tt in range(NT):
            t0 = ss * SS + tt * 128
            dma(GP, b_x1c[:, :], sc_x1[t0:t0 + 128, :], reads=[scr_x1],
                writes=[b_x1c])
            step(VE, lambda e, tt=tt: e.tensor_add(b_x2[:, :], b_x1c[:, :],
                 vA_sgtm[:, tt, :]), reads=[b_x1c, arA], writes=[b_x2])
            dma(SC, out_d[t0:t0 + 128, :], b_x2[:, :], reads=[b_x2],
                writes=[scr_ag])

    for ss in range(NSS):
        pass3_ss(ss)

    fin = {n: cx.semv[n] for n in cx.semv if cx.semv[n] > 0}
    cx.do(SY, lambda e: e.nop(), waits=fin)
    cx.finalize()
    return nc


_NC_CACHE = {}


def _build_cached():
    if "nc" not in _NC_CACHE:
        _NC_CACHE["nc"] = build()
    return _NC_CACHE["nc"]


def kernel(L0_attn_u, L0_attn_W_in, L0_attn_W_conv, L0_attn_b_conv,
           L0_attn_dt_bias, L0_attn_A_log, L0_attn_Dp, L0_attn_norm_w,
           L0_attn_W_out, L0_ffn_gamma, L0_ffn_beta, L0_ffn_weight,
           L0_ffn_bias, L0_ffn_alpha, L1_attn_W_qkv, L1_attn_W_out,
           L1_ffn_weight, L1_ffn_bias):
    f32 = np.float32
    u = np.asarray(L0_attn_u, f32)

    def ct(a):
        return np.ascontiguousarray(np.asarray(a, f32).T)

    def strips(W, nk):
        O = W.shape[0]
        ns = O // 128
        E = W[0:ns * 128].reshape(ns, 128, nk, 128)
        out = np.ascontiguousarray(E.transpose(3, 0, 2, 1).reshape(128, ns * nk * 128))
        if O % 128:
            L_ = W[ns * 128:].reshape(O % 128, nk, 128)
            out = np.concatenate(
                [out, np.ascontiguousarray(
                    L_.transpose(2, 1, 0).reshape(128, nk * (O % 128)))], axis=1)
        return np.ascontiguousarray(out)

    winT = strips(np.asarray(L0_attn_W_in, f32), KD)
    woutT = strips(np.asarray(L0_attn_W_out, f32), 32)
    wqkvT = strips(np.asarray(L1_attn_W_qkv, f32), KD)
    wout1T = strips(np.asarray(L1_attn_W_out, f32), KD)
    wf0 = np.ascontiguousarray(np.asarray(L0_ffn_weight, f32))
    wf1 = np.ascontiguousarray(np.asarray(L1_ffn_weight, f32))

    wconv = np.ascontiguousarray(
        np.asarray(L0_attn_W_conv, f32).reshape(34, 128, 4)
        .transpose(1, 0, 2).reshape(128, 136))
    bconv = np.ascontiguousarray(np.asarray(L0_attn_b_conv, f32).reshape(34, 128).T)
    dtb = np.asarray(L0_attn_dt_bias, f32).reshape(H, 1)
    alog = np.asarray(L0_attn_A_log, f32).reshape(H, 1)
    dpcol = np.ascontiguousarray(
        np.broadcast_to(np.asarray(L0_attn_Dp, f32)[None, :], (128, H)))
    normw = np.ascontiguousarray(np.asarray(L0_attn_norm_w, f32).reshape(32, 128).T)
    gamma = np.ascontiguousarray(np.asarray(L0_ffn_gamma, f32).reshape(KD, 128).T)
    beta = np.ascontiguousarray(np.asarray(L0_ffn_beta, f32).reshape(KD, 128).T)
    f0b = np.ascontiguousarray(np.asarray(L0_ffn_bias, f32).reshape(KD, 128).T)
    f0a = np.ascontiguousarray(np.asarray(L0_ffn_alpha, f32).reshape(KD, 128).T)
    f1b = np.ascontiguousarray(np.asarray(L1_ffn_bias, f32).reshape(KD, 128).T)
    ident = np.eye(128, dtype=f32)
    triu = np.triu(np.ones((128, 128), dtype=f32))
    negm = np.where(triu > 0, np.float32(0.0), np.float32(-1e30)).astype(f32)
    ones = np.ones((1, 128), dtype=f32)
    onesf = np.ones((1, 128), dtype=f32)

    nc = _build_cached()
    in_maps = []
    for core in range(NCORES):
        b = core // GROUP
        r = core % GROUP
        t0 = r * TCORE
        if t0 == 0:
            uslice = np.concatenate(
                [np.zeros((128, D), f32), u[b, 0:TCORE, :]], axis=0)
        else:
            uslice = u[b, t0 - 128:t0 + TCORE, :]
        uslice = np.ascontiguousarray(uslice)
        mask8 = np.zeros((128, 8), f32)
        for i in range(GROUP - 1):
            mask8[:, i] = 1.0 if i < r else 0.0
            mask8[:, 3 + i] = 0.0 if i < r else 1.0
        in_maps.append(dict(
            u=uslice, winT=winT, woutT=woutT, wqkvT=wqkvT, wout1T=wout1T,
            wf0=wf0, wf1=wf1, wconv=wconv, bconv=bconv, dtb=dtb, alog=alog,
            dpcol=dpcol, normw=normw, gamma=gamma, beta=beta, f0b=f0b, negm=negm,
            f0a=f0a, f1b=f1b, ident=ident, triu=triu, mask8=mask8, ones=ones,
            onesf=onesf))

    res = run_bass_kernel_spmd(nc, in_maps, core_ids=list(range(NCORES)))
    out = np.empty((B, L, D), dtype=f32)
    for core in range(NCORES):
        b = core // GROUP
        t0 = (core % GROUP) * TCORE
        out[b, t0:t0 + TCORE, :] = res.results[core]["out"]
    return out


# version 21
# speedup vs baseline: 1.0177x; 1.0093x over previous
import numpy as np
import ml_dtypes
import concourse.bass as bass
import concourse.mybir as mybir
from concourse.bass_utils import run_bass_kernel_spmd

F32 = mybir.dt.float32
BF16 = mybir.dt.bfloat16
AF = mybir.ActivationFunctionType
ALU = mybir.AluOpType
AX = mybir.AxisListType

B, L, D = 2, 8192, 2048
DIN = 4096
H = 64
PH = 64
NST = 128
DIP = 8512
CONVD = DIN + 2 * NST
NCORES = 8
GROUP = 4
TCORE = (B * L) // NCORES
SS = 1024
NSS = TCORE // SS
CH = 128
NCH = SS // CH
NT = SS // 128
KD = D // 128
EPS_LN = 1e-5
EPS_RMS = 1e-5
EPS_L1 = 1e-9
SCL_OUT1 = float(128 ** -0.5)

ENGINES = ("tensor", "vector", "scalar", "gpsimd", "sync")


class Ctx:

    NSLOT = 12

    def __init__(self, nc):
        self.nc = nc
        self.plan = []
        self.sems = {}
        names = ["te", "ve", "se", "ge", "cc"]
        names += ["dy%d" % i for i in range(self.NSLOT)]
        names += ["dg%d" % i for i in range(self.NSLOT)]
        names += ["ds%d" % i for i in range(self.NSLOT)]
        self.semv = {n: 0 for n in names}
        for n in self.semv:
            self.sems[n] = nc.semaphore(n).__enter__()
        self.waited = {e: {n: 0 for n in self.semv} for e in ENGINES}
        self.engine_sem = {"tensor": "te", "vector": "ve", "scalar": "se",
                           "gpsimd": "ge", "sync": None}
        self.slot = {"sync": 0, "gpsimd": 0, "scalar": 0}

    def do(self, eng, fn, waits=None, inc=None, inck=1):
        wd = {}
        if waits:
            it = waits.items() if isinstance(waits, dict) else waits
            for s, v in it:
                if v > wd.get(s, 0):
                    wd[s] = v
        weff = {}
        for s, v in wd.items():
            if v > self.waited[eng][s]:
                weff[s] = v
                self.waited[eng][s] = v
        tok = None
        if inc:
            self.semv[inc] += inck
            tok = (inc, self.semv[inc])
        sems = self.sems

        def run(e, fn=fn, weff=weff, inc=inc, inck=inck):
            for s, v in weff.items():
                e.wait_ge(sems[s], v)
            ins = fn(e)
            if inc is not None:
                assert ins is not None
                ins.then_inc(sems[inc], inck)
        self.plan.append((eng, run))
        return tok

    def tok(self, sem):
        return (sem, self.semv[sem])

    def finalize(self):
        nc = self.nc
        with nc.Block() as block:
            for name in ENGINES:
                def make(name):
                    def body(e):
                        for eng, fn in self.plan:
                            if eng == name:
                                fn(e)
                    return body
                getattr(block, name)(make(name))


class Buf:

    def __init__(self, ap, base=None):
        self.ap_ = ap
        self._st = base._st if base is not None else {"w": None, "r": []}

    def sub(self, key):
        return Buf(self.ap_[key], base=self)

    def __getitem__(self, key):
        return self.ap_[key]

    @property
    def ap(self):
        return self.ap_


def rd(bufs):
    w = []
    for b in bufs:
        if b._st["w"]:
            w.append(b._st["w"])
    return w


def wr(bufs):
    w = []
    for b in bufs:
        if b._st["w"]:
            w.append(b._st["w"])
        w.extend(b._st["r"])
    return w


def mark(tok, reads=(), writes=()):
    for b in reads:
        b._st["r"].append(tok)
    for b in writes:
        b._st["w"] = tok
        b._st["r"] = []


def build(debug_names=(), dump_scratch=False):
    nc = bass.Bass("TRN2", target_bir_lowering=False)
    cx = Ctx(nc)

    def din(name, shape, dt=F32):
        return nc.dram_tensor(name, shape, dt, kind="ExternalInput").ap()

    u_d = din("u", [128 + TCORE, D])
    winT = din("winT", [128, 66 * 2048 + 1024])
    woutT = din("woutT", [128, KD * 4096])
    wqkvT = din("wqkvT", [128, 48 * 2048])
    wout1T = din("wout1T", [128, KD * 2048])
    wf0 = din("wf0", [D, D])
    wf1 = din("wf1", [D, D])
    wconv_d = din("wconv", [128, 34 * 4])
    bconv_d = din("bconv", [128, 34])
    dtb_d = din("dtb", [H, 1])
    alog_d = din("alog", [H, 1])
    dpcol_d = din("dpcol", [128, H])
    normw_d = din("normw", [128, 32])
    gamma_d = din("gamma", [128, KD])
    beta_d = din("beta", [128, KD])
    onesf_d = din("onesf", [1, 128])
    f0b_d = din("f0b", [128, KD])
    f0a_d = din("f0a", [128, KD])
    f1b_d = din("f1b", [128, KD])
    ident_d = din("ident", [128, 128])
    triu_d = din("triu", [128, 128])
    negm_d = din("negm", [128, 128])
    mask8_d = din("mask8", [128, 8])
    ones_d = din("ones", [1, 128])

    out_d = nc.dram_tensor("out", [TCORE, D], F32, kind="ExternalOutput").ap()
    dbg = {}
    for nm, shp in debug_names:
        dbg[nm] = nc.dram_tensor("dbg_" + nm, shp, F32, kind="ExternalOutput").ap()

    _scr_kind = "ExternalOutput" if dump_scratch else "Internal"

    def dsc(name, shape, dt):
        return nc.dram_tensor(name, shape, dt, kind=_scr_kind).ap()

    sc_ypart = dsc("sc_ypart", [TCORE, DIN], BF16)
    sc_z = dsc("sc_z", [TCORE, DIN], BF16)
    sc_c = dsc("sc_c", [NST, TCORE], BF16)
    sc_g0 = dsc("sc_g0", [D, TCORE], BF16)
    sc_csg = dsc("sc_csg", [H, TCORE], F32)
    sc_q = dsc("sc_q", [D, TCORE], BF16)
    sc_kvc = dsc("sc_kvc", [D, TCORE], BF16)
    sc_kc = dsc("sc_kc", [D, TCORE], BF16)
    sc_g1 = dsc("sc_g1", [D, TCORE], BF16)
    sc_x1 = dsc("sc_x1", [TCORE, D], BF16)
    sc_qwT = dsc("sc_qwT", [128, KD * 2048], BF16)
    sc_sw0T = dsc("sc_sw0T", [128, KD * 2048], BF16)
    sc_sg = dsc("sc_sg", [D, TCORE], BF16)
    dbg_sin = dsc("dbg_sin", [NST, DIN], F32)
    dbg_y = dsc("dbg_y", [128, DIN], F32)
    dbg_t2 = dsc("dbg_t2", [128, DIN], F32)
    dbg_sz = dsc("dbg_sz", [128, DIN], F32)
    dbg_yn = dsc("dbg_yn", [128, DIN], F32)
    dbg_sg = dsc("dbg_sg", [128, SS], F32)
    m_woutT = nc.dram_tensor("m_woutT", [128, KD * 4096], BF16).ap()
    m_wqkvT = nc.dram_tensor("m_wqkvT", [128, 48 * 2048], BF16).ap()
    m_wout1T = nc.dram_tensor("m_wout1T", [128, KD * 2048], BF16).ap()
    ag1s_i = nc.dram_tensor("ag1s_i", [NST, DIN], BF16).ap()
    ag1s_o = nc.dram_tensor("ag1s_o", [GROUP * NST, DIN], BF16).ap()
    ag1d_i = nc.dram_tensor("ag1d_i", [1, H], F32).ap()
    ag1d_o = nc.dram_tensor("ag1d_o", [GROUP, H], F32).ap()
    ag2_i = nc.dram_tensor("ag2_i", [2, D], F32).ap()
    ag2_o = nc.dram_tensor("ag2_o", [GROUP * 2, D], F32).ap()

    def sb(name, shape, dt):
        return Buf(nc.alloc_sbuf_tensor(name, shape, dt).ap())

    arA = sb("arA", [128, 32768], BF16)
    arB = sb("arB", [128, 18432], BF16)
    st_bf = sb("state", [128, 4096], BF16)
    c_id_bf = sb("c_id_bf", [128, 128], BF16)
    c_id_f = sb("c_id_f", [64, 64], F32)
    c_triu = sb("c_triu", [128, 128], BF16)
    c_ones = sb("c_ones", [34, 128], BF16)
    c_dp = sb("c_dp", [128, H], BF16)
    c_f0b = sb("c_f0b", [128, KD], F32)
    c_f0a = sb("c_f0a", [128, KD], F32)
    c_f1b = sb("c_f1b", [128, KD], F32)
    c_wcv = sb("c_wcv", [128, 34 * 4], F32)
    c_bcv = sb("c_bcv", [128, 34], F32)
    c_dtb = sb("c_dtb", [H, 1], F32)
    c_A = sb("c_A", [H, 1], F32)
    c_m8 = sb("c_m8", [128, 8], F32)
    c_nwc = sb("c_nwc", [128, 32], F32)
    c_gc = sb("c_gc", [128, KD], F32)
    c_bc = sb("c_bc", [128, KD], F32)
    c_epsln = sb("c_epsln", [128, 1], F32)
    c_epsl1 = sb("c_epsl1", [128, 1], F32)
    c_one = sb("c_one", [128, 1], F32)
    b_u = sb("b_u", [128, D], F32)
    b_x1 = sb("b_x1", [128, D], F32)
    b_xn = sb("b_xn", [128, D], BF16)
    b_v2 = sb("b_v2", [128, D], BF16)
    b_st1 = sb("b_st1", [128, 1], F32)
    b_st2 = sb("b_st2", [128, 1], F32)
    b_m = sb("b_m", [128, 1], F32)
    b_r = sb("b_r", [128, 1], F32)
    b_m2 = sb("b_m2", [128, 1], F32)
    b_r2 = sb("b_r2", [128, 1], F32)
    b_sc1 = sb("b_sc1", [128, 1], F32)
    b_sc2 = sb("b_sc2", [128, 1], F32)
    b_st1B = sb("b_st1B", [128, 1], F32)
    b_st2B = sb("b_st2B", [128, 1], F32)
    b_mB = sb("b_mB", [128, 1], F32)
    b_rB = sb("b_rB", [128, 1], F32)
    b_sc1B = sb("b_sc1B", [128, 1], F32)
    b_sc2B = sb("b_sc2B", [128, 1], F32)
    b_w0 = sb("b_w0", [128, 2048], BF16)
    b_w1 = sb("b_w1", [128, 2048], BF16)
    Q1 = sb("Q1", [128, 4096], BF16)
    Q2 = sb("Q2", [128, 4096], BF16)
    Q3 = sb("Q3", [128, 4096], BF16)
    Q4 = sb("Q4", [128, 4096], BF16)
    b_dx = Q1
    b_ypl = Q1
    b_yn = Q1
    b_dpx = Q2
    b_zl = Q2
    b_y = Q2
    b_yp = Q3
    b_sq = Q3
    b_dump = Q3.sub((slice(None), slice(0, 2048)))
    b_x1c = Buf(Q3.ap[:, 2048:4096])
    b_t2 = Q4
    b_mt = Q4.sub((slice(None), slice(0, 2048)))
    b_qw = Q4.sub((slice(None), slice(0, 2048)))
    E1 = sb("E1", [128, 1152], BF16)
    E2 = sb("E2", [128, 1024], BF16)
    E3 = sb("E3", [128, 1024], BF16)
    E4 = sb("E4", [128, 1024], BF16)
    _k1 = (slice(None), slice(0, 1024))
    b_xbct = E1
    b_sg1 = E1.sub(_k1)
    b_xt = E2
    b_e = E2.sub(_k1)
    b_kv = E2.sub(_k1)
    b_s1 = E2.sub(_k1)
    b_ecl = E3.sub(_k1)
    b_kphi = E3.sub(_k1)
    b_g0b = E3.sub(_k1)
    b_zt = E3.sub((slice(None), slice(0, 512)))
    b_qphi = E4.sub(_k1)
    b_exps = E4.sub(_k1)
    F1 = sb("F1", [128, 1024], F32)
    F2 = sb("F2", [128, 1024], F32)
    b_ca0 = F1
    b_g0t = F1
    b_n1 = F1
    b_ca1 = F2
    b_kvc = F2
    b_d1 = F2
    b_dtraw = Buf(Q1.ap.bitcast(F32)[0:H, 0:1152], base=Q1)
    b_adt = Buf(Q2.ap.bitcast(F32)[0:H, 0:1024], base=Q2)
    b_csg128 = sb("b_csg128", [128, 1024], F32)
    b_csg = Buf(b_csg128.ap[0:H, :], base=b_csg128)
    b_csgc = sb("b_csgc", [H, 1], F32)
    b_cs = sb("b_cs", [H, 128], F32)
    b_e1 = sb("b_e1", [H, 128], F32)
    b_e2 = sb("b_e2", [H, 128], F32)
    b_dd = sb("b_dd", [H, 128], F32)
    b_ds = sb("b_ds", [H, 128], F32)
    b_nc0 = sb("b_nc0", [H, 1], F32)
    b_dec4 = sb("b_dec4", [128, 128], F32)
    b_dtot = sb("b_dtot", [128, H], F32)
    b_smb = sb("b_smb", [128, 128], BF16)
    b_cshi = sb("b_cshi", [H, 128], BF16)
    b_decb = sb("b_decb", [128, 128], BF16)
    b_csB = sb("b_csB", [H, 128], F32)
    b_cshiB = sb("b_cshiB", [H, 128], BF16)
    b_csloB = sb("b_csloB", [H, 128], BF16)
    b_ddB = sb("b_ddB", [H, 128], F32)
    b_dsB = sb("b_dsB", [H, 128], F32)
    b_dec4B = sb("b_dec4B", [128, 128], F32)
    b_decbB = sb("b_decbB", [128, 128], BF16)
    b_dtotB = sb("b_dtotB", [128, H], F32)
    b_smbB = Buf(Q4.ap[:, 2048:2176], base=Q4)
    b_BtmB = Buf(Q4.ap[:, 2176:2304], base=Q4)
    b_e1B = sb("b_e1B", [H, 128], F32)
    b_cslo = sb("b_cslo", [H, 128], BF16)
    b_fhl0 = Buf(b_u.ap.bitcast(BF16)[0:34, :], base=b_u)
    b_fhl1 = Buf(b_x1.ap.bitcast(BF16)[0:34, :], base=b_x1)
    c_nones = sb("c_nones", [34, 128], BF16)
    c_negm = sb("c_negm", [128, 128], BF16)
    b_u1 = Buf(F1.ap[:, 0:512], base=F1)
    b_u2 = Buf(F2.ap[:, 0:512], base=F2)
    b_u3 = b_u1
    b_su = Buf(E4.ap[:, 0:512], base=E4)
    b_Bfm = sb("b_Bfm", [128, 1024], BF16)
    b_Cfm = sb("b_Cfm", [128, 1024], BF16)
    b_Btm = sb("b_Btm", [128, 128], BF16)
    b_cfm2 = b_Bfm
    b_thr = sb("b_thr", [128, 1], F32)
    b_wq = b_u
    b_wcl = b_x1
    b_gst = Buf(F1.ap[0:4, 0:64], base=F1)
    b_dtr = Buf(b_csg128.ap[64:65, 128:192], base=b_csg128)
    b_dmod = Buf(F1.ap[:, 128:192], base=F1)
    b_offk = Buf(E1.ap.bitcast(F32)[:, 512:528], base=E1)
    b_offkv = Buf(E1.ap.bitcast(F32)[:, 528:544], base=E1)
    b_ckv = sb("b_ckv", [128, KD], F32)
    b_ckc = sb("b_ckc", [128, KD], F32)
    b_x2 = b_x1
    b_dtg = b_cs
    b_dcg = Buf(b_dtot.ap[:, 0:H], base=b_dtot)
    b_gk = Buf(b_ckv.ap[:, :], base=b_ckv)
    c_onesf = Buf(b_csg128.ap[64:65, 0:128], base=b_csg128)

    def ps(name, shape, dt):
        return Buf(nc.alloc_psum_tensor(name, shape, dt))

    PBIG = ps("PBIG", [128, 2048], F32)
    PYD = ps("PYD", [128, 512], F32)
    PYO = ps("PYO", [128, 512], F32)
    PTRa = ps("PTRa", [128, 512], F32)
    PTRb = ps("PTRb", [128, 512], F32)

    for _b in (PBIG, PYD, PYO, PTRa, PTRb):
        _b.ap_ = _b.ap_.ap()
    BA = Buf(PBIG.ap[:, 0:1024])
    BB = Buf(PBIG.ap[:, 1024:2048])
    BPD = PYD.sub((slice(None), slice(0, 512)))
    PD = BPD

    def pjoin():
        tok = cx.do(VE, lambda e: e.engine_nop(),
                    waits=wr([BA, BB, BPD, PTRa, PTRb]), inc="ve")
        mark(tok, writes=[BA, BB, BPD, PTRa, PTRb])

    TN, VE, SC, GP, SY = "tensor", "vector", "scalar", "gpsimd", "sync"
    ESEM = {TN: "te", VE: "ve", SC: "se", GP: "ge"}

    def step(eng, fn, reads=(), writes=(), inc=None, inck=1):
        waits = rd(reads) + wr(writes)
        if inc is None:
            inc, inck = ESEM[eng], 1
        tok = cx.do(eng, fn, waits=waits, inc=inc, inck=inck)
        mark(tok, reads=reads, writes=writes)
        return tok

    def dma(q, dst_ap, src_ap, reads=(), writes=()):
        i = cx.slot[q]
        cx.slot[q] = (i + 1) % Ctx.NSLOT
        sem = {"sync": "dy%d", "gpsimd": "dg%d", "scalar": "ds%d"}[q] % i
        extra = {sem: cx.semv[sem]} if cx.semv[sem] else None
        waits = rd(reads) + wr(writes) + (list(extra.items()) if extra else [])
        tok = cx.do(q, lambda e, d=dst_ap, s=src_ap: e.dma_start(out=d, in_=s),
                    waits=waits, inc=sem, inck=16)
        mark(tok, reads=reads, writes=writes)
        return tok

    def brep(ap2d, rep):
        return ap2d.unsqueeze(2).broadcast_to([ap2d.shape[0], ap2d.shape[1], rep])

    def r3(ap2d, inner):
        return ap2d.rearrange("p (a b) -> p a b", b=inner)

    vA_x = arA.ap.rearrange("p (t c) -> p t c", c=DIN)
    vA_sgn = arA.ap[:, 0:16 * SS].rearrange("p (k c) -> p k c", c=SS)
    vA_wz = arA.ap.rearrange("p (k c) -> p k c", c=2048)
    vA_ynfm = arA.ap.rearrange("p (k c) -> p k c", c=SS)
    vA_sgtm = arA.ap[:, 0:8 * D].rearrange("p (t c) -> p t c", c=D)
    vB_xnfm = arB.ap.rearrange("p (k c) -> p k c", c=1152)
    vB_xn1 = arB.ap[:, 0:16 * SS].rearrange("p (k c) -> p k c", c=SS)
    vB_ao = vB_xn1
    vA_ao = arA.ap[:, 16384:32768].rearrange("p (k c) -> p k c", c=SS)
    arA_hi = Buf(None)

    scr_ypart = Buf(sc_ypart)
    scr_z = Buf(sc_z)
    scr_c = Buf(sc_c)
    scr_g0 = Buf(sc_g0)
    scr_csg = Buf(sc_csg)
    scr_q = Buf(sc_q)
    scr_kvc = Buf(sc_kvc)
    scr_kc = Buf(sc_kc)
    scr_g1 = Buf(sc_g1)
    scr_x1 = Buf(sc_x1)
    qwms = [Buf(None) for _ in range(KD)]
    sw0ms = [Buf(None) for _ in range(KD)]
    scr_sg = Buf(sc_sg)
    scr_ag = Buf(None)

    dma(GP, c_id_bf[:, :], ident_d[:, :], writes=[c_id_bf])
    dma(SY, c_id_f[:, :], ident_d[0:64, 0:64], writes=[c_id_f])
    dma(GP, c_triu[:, :], triu_d[:, :], writes=[c_triu])
    dma(GP, c_negm[:, :], negm_d[:, :], writes=[c_negm])
    step(GP, lambda e: e.memset(c_ones[:, :], 1.0), writes=[c_ones])
    dma(GP, c_dp[:, :], dpcol_d[:, :], writes=[c_dp])
    dma(SY, c_f0b[:, :], f0b_d[:, :], writes=[c_f0b])
    dma(SY, c_f0a[:, :], f0a_d[:, :], writes=[c_f0a])
    dma(SY, c_f1b[:, :], f1b_d[:, :], writes=[c_f1b])
    dma(SY, c_wcv[:, :], wconv_d[:, :], writes=[c_wcv])
    dma(SY, c_bcv[:, :], bconv_d[:, :], writes=[c_bcv])
    dma(SY, c_dtb[:, :], dtb_d[:, :], writes=[c_dtb])
    dma(SY, c_m8[:, :], mask8_d[:, :], writes=[c_m8])
    dma(SY, c_nwc[:, :], normw_d[:, :], writes=[c_nwc])
    dma(SY, c_gc[:, :], gamma_d[:, :], writes=[c_gc])
    dma(SY, c_bc[:, :], beta_d[:, :], writes=[c_bc])
    dma(SY, c_onesf[:, :], onesf_d[:, :], writes=[c_onesf])
    dma(SY, c_A[:, :], alog_d[:, :], writes=[c_A])
    step(GP, lambda e: e.memset(c_epsln[:, :], EPS_LN), writes=[c_epsln])
    step(GP, lambda e: e.memset(c_epsl1[:, :], EPS_L1), writes=[c_epsl1])
    step(GP, lambda e: e.memset(c_one[:, :], 1.0), writes=[c_one])
    step(GP, lambda e: e.memset(c_nones[:, :], -1.0), writes=[c_nones])
    step(GP, lambda e: e.memset(b_csgc[:, :], 0.0), writes=[b_csgc])
    step(GP, lambda e: e.memset(b_ckv[:, :], 0.0), writes=[b_ckv])
    step(GP, lambda e: e.memset(b_ckc[:, :], 0.0), writes=[b_ckc])
    step(GP, lambda e: e.memset(st_bf[:, :], 0.0), writes=[st_bf])
    step(SC, lambda e: e.activation(c_A[:, :], c_A[:, :], AF.Exp),
         reads=[c_A], writes=[c_A])
    step(VE, lambda e: e.tensor_scalar(c_A[:, :], c_A[:, :], -1.0, None, ALU.mult),
         reads=[c_A], writes=[c_A])
    WCSTEP = 4096
    w2cs = [Buf(None) for _ in range((KD * 4096) // WCSTEP)]
    w3cs = [Buf(None) for _ in range((48 * 2048) // WCSTEP)]
    w4cs = [Buf(None) for _ in range((KD * 2048) // WCSTEP)]
    _drip_pend = []
    for dst, srcw, cols, cbs in ((m_woutT, woutT, KD * 4096, w2cs),
                                 (m_wqkvT, wqkvT, 48 * 2048, w3cs),
                                 (m_wout1T, wout1T, KD * 2048, w4cs)):
        for ci, c0 in enumerate(range(0, cols, WCSTEP)):
            _drip_pend.append((dst, srcw, c0, cbs[ci]))

    def drip(n):
        for _ in range(min(n, len(_drip_pend))):
            dst, srcw, c0, bb = _drip_pend.pop(0)
            dma(GP, dst[:, c0:c0 + WCSTEP], srcw[:, c0:c0 + WCSTEP],
                writes=[bb])

    for m in range(KD):
        wq = b_u if m % 2 == 0 else b_x1
        sgn = Buf(Q1.ap[:, 0:2048], base=Q1) if m % 2 == 0 else \
            Buf(Q2.ap[:, 0:2048], base=Q2)
        st = b_st1 if m % 2 == 0 else b_st2
        mm_ = b_m if m % 2 == 0 else b_m2
        dma(SY, wq[:, :], wf0[m * 128:(m + 1) * 128, :], writes=[wq])
        step(SC, lambda e, wq=wq, sgn=sgn, st=st: e.activation(sgn[:, :],
             wq[:, :], AF.Copy, accum_out=st[:, :]), reads=[wq],
             writes=[sgn, st])
        step(VE, lambda e, st=st, mm_=mm_: e.tensor_scalar(mm_[:, :], st[:, :],
             -1.0 / D, None, ALU.mult), reads=[st], writes=[mm_])
        step(SC, lambda e, wq=wq, sgn=sgn, mm_=mm_: e.activation(sgn[:, :],
             wq[:, :], AF.Sign, bias=mm_[:, :]), reads=[wq, mm_], writes=[sgn])
        for g in range(4):
            ptr = PTRa if g % 2 == 0 else PTRb
            eb = b_e if g % 2 == 0 else b_ecl
            def tpw(e, g=g, ptr=ptr, sgn=sgn):
                for i in range(4):
                    j = g * 4 + i
                    ins = e.matmul(ptr[:, i * 128:(i + 1) * 128],
                                   sgn[:, j * 128:(j + 1) * 128],
                                   c_id_bf[:, :], start=(i == 0), stop=(i == 3))
                return ins
            step(TN, tpw, reads=[sgn, c_id_bf], writes=[ptr])
            step(SC if g % 2 == 0 else VE,
                 lambda e, ptr=ptr, eb=eb: (e.copy if ptr is PTRa else e.tensor_copy)(
                 eb[:, 0:512], r3(ptr.ap, 128)), reads=[ptr], writes=[eb])
            dma(SC, sc_sw0T[:, m * 2048 + g * 512: m * 2048 + (g + 1) * 512],
                eb[:, 0:512], reads=[eb], writes=[sw0ms[m]])

    def pass1_ss(ss):
        for tt in range(-1, NT):
            par = (tt + 1) % 2
            ub = b_u if par == 0 else b_x1
            xnb = b_xn if par == 0 else Buf(Q1.ap[:, 0:2048], base=Q1)
            v2b = b_v2 if par == 0 else Buf(Q2.ap[:, 0:2048], base=Q2)
            dmp = b_dump if par == 0 else Buf(Q4.ap[:, 0:2048], base=Q4)
            st1 = b_st1 if par == 0 else b_st1B
            st2 = b_st2 if par == 0 else b_st2B
            mc = b_m if par == 0 else b_mB
            rc = b_r if par == 0 else b_rB
            s1c = b_sc1 if par == 0 else b_sc1B
            s2c = b_sc2 if par == 0 else b_sc2B
            r0 = ss * SS + (tt + 1) * 128
            dma(SY, ub[:, :], u_d[r0:r0 + 128, :], writes=[ub])
            step(SC, lambda e, ub=ub, dmp=dmp, st1=st1: e.activation(dmp[:, :],
                 ub[:, :], AF.Copy, accum_out=st1[:, :]), reads=[ub],
                 writes=[dmp, st1])
            step(SC, lambda e, ub=ub, dmp=dmp, st2=st2: e.activation(dmp[:, :],
                 ub[:, :], AF.Square, accum_out=st2[:, :]), reads=[ub],
                 writes=[dmp, st2])
            step(VE, lambda e, st1=st1, mc=mc: e.tensor_scalar(mc[:, :],
                 st1[:, :], 1.0 / D, None, ALU.mult), reads=[st1], writes=[mc])
            step(VE, lambda e, st2=st2, s1c=s1c: e.tensor_scalar(s1c[:, :],
                 st2[:, :], 1.0 / D, None, ALU.mult), reads=[st2], writes=[s1c])
            step(VE, lambda e, mc=mc, s2c=s2c: e.tensor_mul(s2c[:, :], mc[:, :],
                 mc[:, :]), reads=[mc], writes=[s2c])
            step(VE, lambda e, s1c=s1c, s2c=s2c: e.tensor_sub(s1c[:, :],
                 s1c[:, :], s2c[:, :]), reads=[s1c, s2c], writes=[s1c])
            step(SC, lambda e, s1c=s1c, s2c=s2c: e.activation(s2c[:, :],
                 s1c[:, :], AF.Sqrt, bias=c_epsln[:, :]),
                 reads=[s1c, c_epsln], writes=[s2c])
            step(VE, lambda e, s2c=s2c, rc=rc: e.reciprocal(rc[:, :], s2c[:, :]),
                 reads=[s2c], writes=[rc])
            step(VE, lambda e, ub=ub, mc=mc, rc=rc, xnb=xnb: e.tensor_scalar(
                 xnb[:, :], ub[:, :], mc[:, :], rc[:, :], ALU.subtract,
                 ALU.mult), reads=[ub, mc, rc], writes=[xnb])
            for g in range(4):
                ptr = PTRa if g % 2 == 0 else PTRb
                def tp(e, g=g, ptr=ptr, xnb=xnb):
                    for i in range(4):
                        ins = e.matmul(ptr[:, i * 128:(i + 1) * 128],
                                       xnb[:, (g * 4 + i) * 128:(g * 4 + i + 1) * 128],
                                       c_id_bf[:, :], start=(i == 0), stop=(i == 3))
                    return ins
                step(TN, tp, reads=[xnb, c_id_bf], writes=[ptr])
                col = (tt + 1) * 128
                step(SC if g % 2 == 0 else VE,
                     lambda e, g=g, col=col, ptr=ptr:
                     (e.copy if ptr is PTRa else e.tensor_copy)(
                     vB_xnfm[:, g * 4:(g + 1) * 4, col:col + 128],
                     r3(ptr.ap, 128)), reads=[ptr], writes=[arB])
            if tt < 0:
                continue
            step(SC, lambda e, xnb=xnb, dmp=dmp, st1=st1: e.activation(dmp[:, :],
                 xnb[:, :], AF.Copy, accum_out=st1[:, :]), reads=[xnb],
                 writes=[dmp, st1])
            step(SC, lambda e, xnb=xnb, dmp=dmp, st2=st2: e.activation(dmp[:, :],
                 xnb[:, :], AF.Square, accum_out=st2[:, :]), reads=[xnb],
                 writes=[dmp, st2])
            step(VE, lambda e, st1=st1, mc=mc: e.tensor_scalar(mc[:, :],
                 st1[:, :], 1.0 / D, None, ALU.mult), reads=[st1], writes=[mc])
            step(VE, lambda e, st2=st2, s1c=s1c: e.tensor_scalar(s1c[:, :],
                 st2[:, :], 1.0 / D, None, ALU.mult), reads=[st2], writes=[s1c])
            step(VE, lambda e, mc=mc, s2c=s2c: e.tensor_mul(s2c[:, :], mc[:, :],
                 mc[:, :]), reads=[mc], writes=[s2c])
            step(VE, lambda e, s1c=s1c, s2c=s2c: e.tensor_sub(s1c[:, :],
                 s1c[:, :], s2c[:, :]), reads=[s1c, s2c], writes=[s1c])
            step(SC, lambda e, s1c=s1c, s2c=s2c: e.activation(s2c[:, :],
                 s1c[:, :], AF.Sqrt, bias=c_epsln[:, :]),
                 reads=[s1c, c_epsln], writes=[s2c])
            step(VE, lambda e, s2c=s2c, rc=rc: e.reciprocal(rc[:, :], s2c[:, :]),
                 reads=[s2c], writes=[rc])
            step(VE, lambda e, xnb=xnb, mc=mc, rc=rc, v2b=v2b: e.tensor_scalar(
                 v2b[:, :], xnb[:, :], mc[:, :], rc[:, :], ALU.subtract,
                 ALU.mult), reads=[xnb, mc, rc], writes=[v2b])
            for g in range(4):
                ptr = PTRa if g % 2 == 0 else PTRb
                def tps(e, g=g, ptr=ptr, v2b=v2b):
                    for i in range(4):
                        ins = e.matmul(ptr[:, i * 128:(i + 1) * 128],
                                       v2b[:, (g * 4 + i) * 128:(g * 4 + i + 1) * 128],
                                       c_id_bf[:, :], start=(i == 0), stop=(i == 3))
                    return ins
                step(TN, tps, reads=[v2b, c_id_bf], writes=[ptr])
                step(SC if g % 2 == 0 else VE,
                     lambda e, g=g, tt=tt, ptr=ptr:
                     (e.copy if ptr is PTRa else e.tensor_copy)(
                     vA_sgn[:, g * 4:(g + 1) * 4, tt * 128:(tt + 1) * 128],
                     r3(ptr.ap, 128)), reads=[ptr], writes=[arA])
        for k in range(KD):
            step(VE, lambda e, k=k: e.tensor_scalar(vA_sgn[:, k, :],
                 vA_sgn[:, k, :], c_gc[:, k:k + 1], c_bc[:, k:k + 1],
                 ALU.mult, ALU.add), reads=[arA, c_gc, c_bc], writes=[arA])
            step(SC, lambda e, k=k: e.activation(vA_sgn[:, k, :], vA_sgn[:, k, :],
                 AF.Sign), reads=[arA], writes=[arA])

        for m in range(KD):
            wb = b_w0 if m % 2 == 0 else b_w1
            pp = BA if m % 2 == 0 else BB
            g0t = b_g0t if m % 2 == 0 else b_ca1
            g0b = b_g0b if m % 2 == 0 else b_e
            dma(SY, wb[:, 0:2048],
                sc_sw0T[:, m * 2048:(m + 1) * 2048],
                reads=[sw0ms[m]], writes=[wb])
            def mmf(e, wb=wb, pp=pp):
                for k in range(KD):
                    for n0 in (0, 512):
                        ins = e.matmul(pp[:, n0:n0 + 512],
                                       wb[:, k * 128:(k + 1) * 128],
                                       vA_sgn[:, k, n0:n0 + 512],
                                       start=(k == 0), stop=(k == KD - 1))
                return ins
            step(TN, mmf, reads=[wb, arA], writes=[pp])
            step(VE, lambda e, m=m, pp=pp, g0t=g0t: e.tensor_scalar(
                g0t[:, :], pp[:, 0:1024], c_f0b[:, m:m + 1], c_f0a[:, m:m + 1],
                ALU.add, ALU.mult), reads=[pp, c_f0b, c_f0a], writes=[g0t])
            step(SC, lambda e, g0t=g0t, g0b=g0b: e.activation(g0b[:, :],
                 g0t[:, :], AF.Silu), reads=[g0t], writes=[g0b])
            dma(SC, sc_g0[m * 128:(m + 1) * 128, ss * SS:(ss + 1) * SS], g0b[:, :],
                reads=[g0b], writes=[scr_g0])
        pjoin()

        zps = [BA[:, 0:512], BA[:, 512:1024], BB[:, 0:512], BB[:, 512:1024]]
        vA_wz4 = arA.ap.rearrange("p (s k c) -> p s k c", k=KD, c=128)
        for rnd in range(2):
            for h in range(2):
                c0 = rnd * 32768 + h * 16384
                dma(GP, arA.ap[:, h * 16384:(h + 1) * 16384],
                    winT[:, c0:c0 + 16384], writes=[arA])
            for tt in range(NT):
                def mmz(e, rnd=rnd, tt=tt):
                    for k in range(KD):
                        for zq in range(4):
                            ins = e.matmul(zps[zq],
                                           vB_xnfm[:, k, (tt + 1) * 128:(tt + 2) * 128],
                                           vA_wz4[:, zq * 4:(zq + 1) * 4, k, :],
                                           start=(k == 0), stop=(k == KD - 1))
                    return ins
                step(TN, mmz, reads=[arA, arB], writes=[BA, BB])
                for zq in range(4):
                    step(VE, lambda e, zq=zq: e.tensor_copy(b_zt[:, :], zps[zq]),
                         reads=[BA, BB], writes=[b_zt])
                    dma(SC, sc_z[ss * SS + tt * 128: ss * SS + (tt + 1) * 128,
                                 rnd * 2048 + zq * 512: rnd * 2048 + (zq + 1) * 512],
                        b_zt[:, :], reads=[b_zt], writes=[scr_z])


        for m in range(35):
            wb = b_w0 if m % 2 == 0 else b_w1
            mw = 64 if m == 34 else 128
            dma(GP, wb[:, 0:KD * mw],
                winT[:, (32 + m) * 2048: (32 + m) * 2048 + KD * mw],
                writes=[wb])
            def mmc(e, wb=wb, mw=mw):
                for k in range(KD):
                    for dst, n0, nl in ((BA, 0, 512), (BA, 512, 512), (BB, 1024, 128)):
                        ins = e.matmul(dst.ap[0:mw, n0 - (1024 if dst is BB else 0):
                                              n0 - (1024 if dst is BB else 0) + nl],
                                       wb[:, k * mw:(k + 1) * mw],
                                       vB_xnfm[:, k, n0:n0 + nl],
                                       start=(k == 0), stop=(k == KD - 1))
                return ins
            step(TN, mmc, reads=[wb, arB], writes=[BA, BB])
            if m == 34:
                step(SC, lambda e: e.activation(b_dtraw[:, 0:1024], BA[0:H, 0:1024],
                     AF.Copy), reads=[BA], writes=[b_dtraw])
                step(SC, lambda e: e.activation(b_dtraw[:, 1024:1152], BB[0:H, 0:128],
                     AF.Copy), reads=[BB], writes=[b_dtraw])
                step(SC, lambda e: e.activation(b_dtraw[:, :], b_dtraw[:, :], AF.Exp,
                     bias=c_dtb[:, :]), reads=[b_dtraw, c_dtb], writes=[b_dtraw])
                step(SC, lambda e: e.activation(b_dtraw[:, :], b_dtraw[:, :], AF.Ln,
                     bias=c_one[0:H, :]), reads=[b_dtraw, c_one], writes=[b_dtraw])
                step(VE, lambda e: e.tensor_scalar(b_adt[:, :], b_dtraw[:, 128:1152],
                     c_A[:, :], None, ALU.mult), reads=[b_dtraw, c_A], writes=[b_adt])
                step(VE, lambda e: e.tensor_tensor_scan(b_csg[:, :], b_adt[:, :],
                     b_adt[:, :], b_csgc[:, 0:1], ALU.add, ALU.bypass),
                     reads=[b_adt, b_csgc], writes=[b_csg])
                dma(SC, sc_csg[:, ss * SS:(ss + 1) * SS], b_csg[:, :],
                    reads=[b_csg], writes=[scr_csg])
                continue
            step(SC, lambda e: e.activation(b_xbct[:, 0:1024], BA[:, 0:1024],
                 AF.Copy), reads=[BA], writes=[b_xbct])
            step(SC, lambda e: e.activation(b_xbct[:, 1024:1152], BB[:, 0:128],
                 AF.Copy), reads=[BB], writes=[b_xbct])
            step(VE, lambda e, m=m: e.tensor_scalar(b_ca0[:, :], b_xbct[:, 125:1149],
                 c_wcv[:, m * 4:m * 4 + 1], None, ALU.mult),
                 reads=[b_xbct, c_wcv], writes=[b_ca0])
            step(VE, lambda e, m=m: e.scalar_tensor_tensor(b_ca1[:, :],
                 b_xbct[:, 126:1150], c_wcv[:, m * 4 + 1:m * 4 + 2], b_ca0[:, :],
                 ALU.mult, ALU.add), reads=[b_xbct, c_wcv, b_ca0], writes=[b_ca1])
            step(VE, lambda e, m=m: e.scalar_tensor_tensor(b_ca0[:, :],
                 b_xbct[:, 127:1151], c_wcv[:, m * 4 + 2:m * 4 + 3], b_ca1[:, :],
                 ALU.mult, ALU.add), reads=[b_xbct, c_wcv, b_ca1], writes=[b_ca0])
            step(VE, lambda e, m=m: e.scalar_tensor_tensor(b_ca1[:, :],
                 b_xbct[:, 128:1152], c_wcv[:, m * 4 + 3:m * 4 + 4], b_ca0[:, :],
                 ALU.mult, ALU.add), reads=[b_xbct, c_wcv, b_ca0], writes=[b_ca1])
            if m < 32:
                step(SC, lambda e, m=m: e.activation(b_xt[:, 0:1024], b_ca1[:, :],
                     AF.Silu, bias=c_bcv[:, m:m + 1]),
                     reads=[b_ca1, c_bcv], writes=[b_xt])
                for g in range(2):
                    ptr = PTRa if g % 2 == 0 else PTRb
                    def tpx(e, g=g, ptr=ptr):
                        for i in range(4):
                            ins = e.matmul(ptr[:, i * 128:(i + 1) * 128],
                                           b_xt[:, (g * 4 + i) * 128:(g * 4 + i + 1) * 128],
                                           c_id_bf[:, :], start=(i == 0), stop=(i == 3))
                        return ins
                    step(TN, tpx, reads=[b_xt, c_id_bf], writes=[ptr])
                    step(SC if ptr is PTRa else VE, lambda e, m=m, g=g, ptr=ptr: (e.copy if ptr is PTRa else e.tensor_copy)(
                        vA_x[:, g * 4:(g + 1) * 4, m * 128:(m + 1) * 128],
                        r3(ptr.ap, 128)), reads=[ptr], writes=[arA])
            else:
                dstb = b_Bfm if m == 32 else b_Cfm
                step(SC, lambda e, m=m, dstb=dstb: e.activation(
                    dstb[:, 0:1024], b_ca1[:, :], AF.Silu, bias=c_bcv[:, m:m + 1]),
                    reads=[b_ca1, c_bcv], writes=[dstb])
                if m == 33:
                    dma(SC, sc_c[:, ss * SS:(ss + 1) * SS], b_Cfm[:, :],
                        reads=[b_Cfm], writes=[scr_c])

        pjoin()
        for c in range(NCH):
            drip(3)
            par = c % 2
            cs = b_cs if par == 0 else b_csB
            cshi = b_cshi if par == 0 else b_cshiB
            cslo = b_cslo if par == 0 else b_csloB
            loF = b_e1 if par == 0 else b_e1B
            dd = b_dd if par == 0 else b_ddB
            ds = b_ds if par == 0 else b_dsB
            dec4 = b_dec4 if par == 0 else b_dec4B
            decb = b_decb if par == 0 else b_decbB
            dtot = b_dtot if par == 0 else b_dtotB
            dtr = b_dtr if par == 0 else Buf(b_csg128.ap[64:65, 192:256], base=b_csg128)
            smb = b_smb if par == 0 else b_smbB
            Btm = b_Btm if par == 0 else b_BtmB
            fhl = b_fhl0 if par == 0 else b_fhl1
            if c == 0:
                prevcol = b_csgc
                prevsl = (slice(None), slice(0, 1))
            else:
                prevcol = b_csg
                prevsl = (slice(None), slice(c * 128 - 1, c * 128))
            step(VE, lambda e, c=c, pc=prevcol, psl=prevsl, cs=cs: e.tensor_scalar(
                cs[:, :], b_csg[:, c * 128:(c + 1) * 128], pc[psl], None,
                ALU.subtract), reads=[b_csg, prevcol], writes=[cs])
            step(VE, lambda e, cs=cs, cshi=cshi: e.tensor_copy(cshi[:, :],
                 cs[:, :]), reads=[cs], writes=[cshi])
            step(VE, lambda e, cs=cs, cshi=cshi, loF=loF: e.tensor_tensor(
                 loF[:, :], cs[:, :], cshi[:, :], ALU.subtract),
                 reads=[cs, cshi], writes=[loF])
            step(VE, lambda e, loF=loF, cslo=cslo: e.tensor_copy(cslo[:, :],
                 loF[:, :]), reads=[loF], writes=[cslo])
            dma(GP, fhl[0:1, :], cshi[0:32, :], reads=[cshi], writes=[fhl])
            dma(GP, fhl[1:2, :], cslo[0:32, :], reads=[cslo], writes=[fhl])
            dma(GP, fhl[32:33, :], cshi[32:64, :], reads=[cshi], writes=[fhl])
            dma(GP, fhl[33:34, :], cslo[32:64, :], reads=[cslo], writes=[fhl])
            step(SC, lambda e, cs=cs, dd=dd: e.activation(dd[:, :], cs[:, :],
                 AF.Exp), reads=[cs], writes=[dd])
            step(SC, lambda e, cs=cs, ds=ds: e.activation(ds[:, :], cs[:, :],
                 AF.Exp, bias=cs[:, 127:128], scale=-1.0), reads=[cs],
                 writes=[ds])
            def tpd(e, dd=dd, ds=ds):
                for i, s4 in enumerate((dd, ds)):
                    ins = e.matmul(PD[:, i * 64:(i + 1) * 64], s4[:, :],
                                   c_id_f[0:H, 0:H], start=(i == 0),
                                   stop=(i == 1))
                return ins
            step(TN, tpd, reads=[dd, ds, c_id_f], writes=[PD])
            step(VE, lambda e, dec4=dec4: e.tensor_copy(dec4[:, 0:128],
                 PD[:, 0:128]), reads=[PD], writes=[dec4])
            step(VE, lambda e, dec4=dec4, decb=decb: e.tensor_copy(decb[:, :],
                 dec4[:, 0:128]), reads=[dec4], writes=[decb])
            step(TN, lambda e, dd=dd: e.matmul(PD[0:1, 384:448], dd[:, 127:128],
                 c_id_f[0:H, 0:H], start=True, stop=True),
                 reads=[dd, c_id_f], writes=[PD])
            step(VE, lambda e, dtr=dtr: e.tensor_copy(dtr[:, :],
                 PD[0:1, 384:448]), reads=[PD], writes=[dtr])
            step(TN, lambda e, dtr=dtr: e.matmul(PD[:, 256:320],
                 c_onesf[0:1, 0:128], dtr[0:1, :], start=True, stop=True),
                 reads=[c_onesf, dtr], writes=[PD])
            step(VE, lambda e, dtot=dtot: e.tensor_copy(dtot[:, :],
                 PD[:, 256:320]), reads=[PD], writes=[dtot])
            step(TN, lambda e, c=c: e.matmul(PD[:, 320:448],
                 b_Bfm[:, c * 128:(c + 1) * 128], b_Cfm[:, c * 128:(c + 1) * 128],
                 start=True, stop=True), reads=[b_Bfm, b_Cfm], writes=[PD])
            step(VE, lambda e, smb=smb: e.tensor_mul(smb[:, :], PD[:, 320:448],
                 c_triu[:, :]), reads=[PD, c_triu], writes=[smb])
            ptrB = PTRa if par == 0 else PTRb
            step(TN, lambda e, c=c, ptrB=ptrB: e.matmul(ptrB[:, 0:128],
                 b_Bfm[:, c * 128:(c + 1) * 128], c_id_bf[:, :],
                 start=True, stop=True),
                 reads=[b_Bfm, c_id_bf], writes=[ptrB])
            step(SC, lambda e, Btm=Btm, ptrB=ptrB: e.copy(Btm[:, :],
                 ptrB[:, 0:128]), reads=[ptrB], writes=[Btm])
            step(VE, lambda e, c=c, decb=decb: e.tensor_tensor(r3(b_dx.ap, PH),
                 r3(vA_x[:, c, :], PH), brep(decb[:, 64:128], PH), ALU.mult),
                 reads=[arA, decb], writes=[b_dx])
            step(VE, lambda e, c=c: e.tensor_tensor(r3(b_dpx.ap, PH),
                 r3(vA_x[:, c, :], PH), brep(c_dp[:, :], PH), ALU.mult),
                 reads=[arA, c_dp], writes=[b_dpx])
            for grp in range(8):
                    pd = BA if grp % 2 == 0 else BB
                    def mmdiff(e, grp=grp, pd=pd, fhl=fhl):
                        for j in range(8):
                            h = grp * 8 + j
                            hr = 0 if h < 32 else 32
                            off = (h % 32) * 128
                            col = slice(j * 128, (j + 1) * 128)
                            first = (j == 0) or (j == 4)
                            last = (j == 3) or (j == 7)
                            e.matmul(pd[:, col], fhl[hr:hr + 2, off:off + 128],
                                     c_nones[hr:hr + 2, 0:128], start=first,
                                     stop=False)
                            e.matmul(pd[:, col], c_ones[hr:hr + 2, 0:128],
                                     fhl[hr:hr + 2, off:off + 128],
                                     start=False, stop=False)
                            ins = e.matmul(pd[:, col], c_id_bf[:, :],
                                           c_negm[:, :], start=False, stop=last)
                        return ins
                    step(TN, mmdiff, reads=[fhl, c_ones, c_nones,
                         c_id_bf, c_negm], writes=[pd])
                    step(SC, lambda e, pd=pd: e.activation(b_exps[:, :],
                         pd[:, 0:1024], AF.Exp), reads=[pd], writes=[b_exps])
                    step(VE, lambda e, smb=smb: e.tensor_tensor(
                         r3(b_mt[:, 0:1024], 128), r3(b_exps.ap, 128),
                         smb.ap.unsqueeze(1).broadcast_to([128, 8, 128]),
                         ALU.mult), reads=[b_exps, smb], writes=[b_mt])
                    step(TN, lambda e, c=c, grp=grp: e.matmul(PYO[:, :],
                         b_Cfm[:, c * 128:(c + 1) * 128],
                         st_bf[:, grp * 512:(grp + 1) * 512],
                         start=True, stop=True),
                         reads=[b_Cfm, st_bf], writes=[PYO])
                    def mmyd(e, c=c, grp=grp):
                        for i in range(8):
                            h = grp * 8 + i
                            ins = e.matmul(PYD[:, i * 64:(i + 1) * 64],
                                           b_mt[:, i * 128:(i + 1) * 128],
                                           vA_x[:, c, h * 64:(h + 1) * 64],
                                           start=(i == 0), stop=(i == 7))
                        return ins
                    step(TN, mmyd, reads=[b_mt, arA], writes=[PYD])
                    step(VE, lambda e, grp=grp, dec4=dec4: e.tensor_tensor(
                         r3(b_u2.ap, PH), r3(PYO.ap, PH),
                         brep(dec4[:, grp * 8:(grp + 1) * 8], PH),
                         ALU.mult), reads=[PYO, dec4], writes=[b_u2])
                    step(VE, lambda e: e.tensor_add(b_u1[:, :], b_u2[:, :],
                         PYD[:, :]), reads=[b_u2, PYD], writes=[b_u1])
                    step(VE, lambda e, grp=grp: e.tensor_add(
                         b_yp[:, grp * 512:(grp + 1) * 512], b_u1[:, :],
                         b_dpx[:, grp * 512:(grp + 1) * 512]),
                         reads=[b_u1, b_dpx], writes=[b_yp])
                    step(TN, lambda e, grp=grp, Btm=Btm: e.matmul(PYO[:, :],
                         Btm[:, :], b_dx[:, grp * 512:(grp + 1) * 512],
                         start=True, stop=True),
                         reads=[Btm, b_dx], writes=[PYO])
                    step(VE, lambda e, grp=grp, dtot=dtot: e.tensor_tensor(
                         r3(b_su.ap, PH),
                         r3(st_bf[:, grp * 512:(grp + 1) * 512], PH),
                         brep(dtot[:, grp * 8:(grp + 1) * 8], PH), ALU.mult),
                         reads=[st_bf, dtot], writes=[b_su])
                    step(VE, lambda e, grp=grp: e.tensor_add(
                         st_bf[:, grp * 512:(grp + 1) * 512], b_su[:, :],
                         PYO[:, :]), reads=[b_su, PYO], writes=[st_bf])
            dma(SC, sc_ypart[ss * SS + c * 128: ss * SS + (c + 1) * 128, :],
                b_yp[:, :], reads=[b_yp], writes=[scr_ypart])
        step(VE, lambda e: e.tensor_copy(b_csgc[:, :], b_csg[:, 1023:1024]),
             reads=[b_csg], writes=[b_csgc])

    for ss in range(NSS):
        pass1_ss(ss)
    drip(len(_drip_pend))
    pjoin()

    step(SC, lambda e: e.activation(b_nc0[:, :], b_csgc[:, :], AF.Exp),
         reads=[b_csgc], writes=[b_nc0])
    step(TN, lambda e: e.matmul(PD[0:1, 0:H], b_nc0[:, :], c_id_f[0:H, 0:H],
         start=True, stop=True), reads=[b_nc0, c_id_f], writes=[PD])
    step(VE, lambda e: e.tensor_copy(b_gst[0:1, 0:H], PD[0:1, 0:H]),
         reads=[PD], writes=[b_gst])
    dma(SY, ag1d_i[:, :], b_gst[0:1, 0:H], reads=[b_gst], writes=[scr_ag])
    dma(SY, ag1s_i[:, :], st_bf[:, :], reads=[st_bf], writes=[scr_ag])
    step(GP, lambda e: e.collective_compute(
        "AllGather", ALU.bypass,
        replica_groups=[[0, 1, 2, 3], [4, 5, 6, 7]],
        ins=[ag1s_i.opt()], outs=[ag1s_o.opt()]),
        reads=[scr_ag], writes=[scr_ag], inc="cc")
    step(GP, lambda e: e.collective_compute(
        "AllGather", ALU.bypass,
        replica_groups=[[0, 1, 2, 3], [4, 5, 6, 7]],
        ins=[ag1d_i.opt()], outs=[ag1d_o.opt()]),
        reads=[scr_ag], writes=[scr_ag], inc="cc")
    for i in range(GROUP):
        dma(SY, vA_x[:, i, :], ag1s_o[i * NST:(i + 1) * NST, :],
            reads=[scr_ag], writes=[arA])
    acc_f = arA.ap[:, 16384:24576].bitcast(F32)
    si_f = arA.ap[:, 24576:32768].bitcast(F32)
    step(GP, lambda e: e.memset(acc_f, 0.0), writes=[arA])
    for i in range(GROUP - 1):
        dma(SY, b_dtr[:, :], ag1d_o[i:i + 1, :], reads=[scr_ag], writes=[b_dtr])
        step(TN, lambda e, i=i: e.matmul(PD[:, 0:H], c_onesf[0:1, 0:128],
             b_dtr[0:1, :], start=True, stop=True),
             reads=[c_onesf, b_dtr], writes=[PD])
        step(VE, lambda e, i=i: e.tensor_scalar(b_dmod[:, :], PD[:, 0:H],
             c_m8[:, i:i + 1], c_m8[:, 3 + i:4 + i], ALU.mult, ALU.add),
             reads=[PD, c_m8], writes=[b_dmod])
        step(VE, lambda e, i=i: e.tensor_scalar(
             si_f, vA_x[:, i, :], c_m8[:, i:i + 1], None, ALU.mult),
             reads=[arA, c_m8], writes=[arA])
        step(VE, lambda e: e.tensor_tensor(
             acc_f.rearrange("p (a b) -> p a b", b=PH),
             acc_f.rearrange("p (a b) -> p a b", b=PH),
             brep(b_dmod[:, :], PH), ALU.mult), reads=[arA, b_dmod],
             writes=[arA])
        step(VE, lambda e: e.tensor_tensor(acc_f, acc_f, si_f, ALU.add),
             reads=[arA], writes=[arA])
    step(VE, lambda e: e.tensor_copy(st_bf[:, :], acc_f),
         reads=[arA], writes=[st_bf])
    dma(GP, dbg_sin[:, :], st_bf[:, :], reads=[st_bf], writes=[scr_ag])

    for m in range(KD):
        wq = b_u if m % 2 == 0 else b_x1
        qq = Buf(Q1.ap[:, 0:2048], base=Q1) if m % 2 == 0 else \
            Buf(Q2.ap[:, 0:2048], base=Q2)
        ab = Buf(Q3.ap[:, 0:2048], base=Q3) if m % 2 == 0 else \
            Buf(Q4.ap[:, 0:2048], base=Q4)
        st = b_st1 if m % 2 == 0 else b_st2
        th = b_m if m % 2 == 0 else b_m2
        dma(SY, wq[:, :], wf1[m * 128:(m + 1) * 128, :], writes=[wq])
        step(SC, lambda e, wq=wq, ab=ab, st=st: e.activation(ab[:, :], wq[:, :],
             AF.Abs, accum_out=st[:, :]), reads=[wq], writes=[ab, st])
        step(VE, lambda e, st=st, th=th: e.tensor_scalar(th[:, :], st[:, :],
             0.7 / D, None, ALU.mult), reads=[st], writes=[th])
        step(VE, lambda e, ab=ab, th=th: e.tensor_scalar(ab[:, :], ab[:, :],
             th[:, :], None, ALU.is_gt), reads=[ab, th], writes=[ab])
        step(SC, lambda e, wq=wq, qq=qq: e.activation(qq[:, :], wq[:, :],
             AF.Sign), reads=[wq], writes=[qq])
        step(VE, lambda e, qq=qq, ab=ab: e.tensor_mul(qq[:, :], qq[:, :],
             ab[:, :]), reads=[qq, ab], writes=[qq])
        for g in range(4):
            ptr = PTRa if g % 2 == 0 else PTRb
            eb = b_e if g % 2 == 0 else b_ecl
            def tpq(e, g=g, ptr=ptr, qq=qq):
                for i in range(4):
                    j = g * 4 + i
                    ins = e.matmul(ptr[:, i * 128:(i + 1) * 128],
                                   qq[:, j * 128:(j + 1) * 128],
                                   c_id_bf[:, :], start=(i == 0), stop=(i == 3))
                return ins
            step(TN, tpq, reads=[qq, c_id_bf], writes=[ptr])
            step(SC if g % 2 == 0 else VE,
                 lambda e, ptr=ptr, eb=eb: (e.copy if ptr is PTRa else e.tensor_copy)(
                 eb[:, 0:512], r3(ptr.ap, 128)), reads=[ptr], writes=[eb])
            dma(SC, sc_qwT[:, m * 2048 + g * 512: m * 2048 + (g + 1) * 512],
                eb[:, 0:512], reads=[eb], writes=[qwms[m]])

    def pass2_ss(ss):
        pjoin()
        for c in range(NCH):
            t0 = ss * SS + c * 128
            dma(GP, b_ypl[:, :], sc_ypart[t0:t0 + 128, :],
                reads=[scr_ypart], writes=[b_ypl])
            dma(GP, b_zl[:, :], sc_z[t0:t0 + 128, :], reads=[scr_z], writes=[b_zl])
            dma(SY, b_cfm2[:, 0:128], sc_c[:, t0:t0 + 128],
                reads=[scr_c], writes=[b_cfm2])
            dma(SY, b_dtg[:, :], sc_csg[:, t0:t0 + 128],
                reads=[scr_csg], writes=[b_dtg])
            step(SC, lambda e: e.activation(b_dtg[:, :], b_dtg[:, :], AF.Exp),
                 reads=[b_dtg], writes=[b_dtg])
            step(TN, lambda e: e.matmul(PD[:, 0:H], b_dtg[:, :],
                 c_id_f[0:H, 0:H], start=True, stop=True),
                 reads=[b_dtg, c_id_f], writes=[PD])
            step(VE, lambda e: e.tensor_copy(b_dcg[:, :], PD[:, 0:H]),
                 reads=[PD], writes=[b_dcg])
            step(SC, lambda e: e.activation(b_sq[:, :], b_zl[:, :], AF.Silu),
                 reads=[b_zl], writes=[b_sq])
            for grp in range(8):
                step(TN, lambda e, grp=grp: e.matmul(PYD[:, :], b_cfm2[:, 0:128],
                     st_bf[:, grp * 512:(grp + 1) * 512], start=True, stop=True),
                     reads=[b_cfm2, st_bf], writes=[PYD])
                step(VE, lambda e, grp=grp: e.tensor_tensor(r3(b_u1.ap, PH),
                     r3(PYD.ap, PH),
                     brep(b_dcg[:, grp * 8:(grp + 1) * 8], PH), ALU.mult),
                     reads=[PYD, b_dcg], writes=[b_u1])
                step(VE, lambda e, grp=grp: e.tensor_add(
                     b_t2[:, grp * 512:(grp + 1) * 512], b_u1[:, :],
                     b_ypl[:, grp * 512:(grp + 1) * 512]),
                     reads=[b_u1, b_ypl], writes=[b_t2])
            if ss == 0 and c == 0:
                dma(GP, dbg_t2[:, :], b_t2[:, :], reads=[b_t2], writes=[scr_ag])
                dma(GP, dbg_sz[:, :], b_sq[:, :], reads=[b_sq], writes=[scr_ag])
            step(VE, lambda e: e.tensor_mul(b_y[:, :], b_t2[:, :], b_sq[:, :]),
                 reads=[b_t2, b_sq], writes=[b_y])
            if ss == 0 and c == 0:
                dma(GP, dbg_y[:, :], b_y[:, :], reads=[b_y], writes=[scr_ag])
            step(SC, lambda e: e.activation(b_sq[:, :], b_y[:, :], AF.Square,
                 accum_out=b_st1[:, :]), reads=[b_y], writes=[b_sq, b_st1])
            step(VE, lambda e: e.tensor_scalar(b_sc1[:, :], b_st1[:, :], 1.0 / DIN,
                 None, ALU.mult), reads=[b_st1], writes=[b_sc1])
            step(SC, lambda e: e.activation(b_sc2[:, :], b_sc1[:, :], AF.Sqrt,
                 bias=c_epsln[:, :]), reads=[b_sc1, c_epsln], writes=[b_sc2])
            step(VE, lambda e: e.reciprocal(b_r[:, :], b_sc2[:, :]),
                 reads=[b_sc2], writes=[b_r])
            step(VE, lambda e: e.tensor_scalar(b_yn[:, :], b_y[:, :],
                 b_r[:, :], None, ALU.mult), reads=[b_y, b_r], writes=[b_yn])
            if ss == 0 and c == 0:
                dma(GP, dbg_yn[:, :], b_yn[:, :], reads=[b_yn], writes=[scr_ag])
            for g in range(8):
                ptr = PTRa if g % 2 == 0 else PTRb
                def tpy(e, g=g, ptr=ptr):
                    for i in range(4):
                        ins = e.matmul(ptr[:, i * 128:(i + 1) * 128],
                                       b_yn[:, (g * 4 + i) * 128:(g * 4 + i + 1) * 128],
                                       c_id_bf[:, :], start=(i == 0), stop=(i == 3))
                    return ins
                step(TN, tpy, reads=[b_yn, c_id_bf], writes=[ptr])
                step(SC if ptr is PTRa else VE, lambda e, g=g, c=c, ptr=ptr: (e.copy if ptr is PTRa else e.tensor_copy)(
                    vA_ynfm[:, g * 4:(g + 1) * 4, c * 128:(c + 1) * 128],
                    r3(ptr.ap, 128)), reads=[ptr], writes=[arA])
        for k in range(32):
            step(VE, lambda e, k=k: e.tensor_scalar(vA_ynfm[:, k, :],
                 vA_ynfm[:, k, :], c_nwc[:, k:k + 1], None, ALU.mult),
                 reads=[arA, c_nwc], writes=[arA])
        for m in range(KD):
            pp = BA if m % 2 == 0 else BB
            sgo = b_sg1 if m % 2 == 0 else b_kv
            g0i = b_g0b if m % 2 == 0 else b_ecl
            dma(SY, b_w0[:, 0:2048],
                m_woutT[:, m * 4096: m * 4096 + 2048],
                reads=[w2cs[m]], writes=[b_w0])
            def mmo1(e, pp=pp):
                for k in range(16):
                    for n0 in (0, 512):
                        ins = e.matmul(pp[:, n0:n0 + 512],
                                       b_w0[:, k * 128:(k + 1) * 128],
                                       vA_ynfm[:, k, n0:n0 + 512],
                                       start=(k == 0), stop=False)
                return ins
            dma(SY, b_w1[:, 0:2048],
                m_woutT[:, m * 4096 + 2048: m * 4096 + 4096],
                reads=[w2cs[m]], writes=[b_w1])
            step(TN, mmo1, reads=[b_w0, arA], writes=[pp])
            def mmo2(e, pp=pp):
                for k in range(16):
                    for n0 in (0, 512):
                        ins = e.matmul(pp[:, n0:n0 + 512],
                                       b_w1[:, k * 128:(k + 1) * 128],
                                       vA_ynfm[:, 16 + k, n0:n0 + 512],
                                       start=False, stop=(k == 15))
                return ins
            step(TN, mmo2, reads=[b_w1, arA], writes=[pp])
            dma(SC, g0i[:, :], sc_g0[m * 128:(m + 1) * 128, ss * SS:(ss + 1) * SS],
                reads=[scr_g0], writes=[g0i])
            step(VE, lambda e, m=m, pp=pp, g0i=g0i: e.tensor_mul(
                 vB_xn1[:, m, :], pp[:, 0:1024], g0i[:, :]),
                 reads=[pp, g0i], writes=[arB])
        pjoin()
        for m in range(KD):
            for g in range(2):
                ptr = PTRa if g % 2 == 0 else PTRb
                def tpsg(e, m=m, g=g, ptr=ptr):
                    for i in range(4):
                        ins = e.matmul(ptr[:, i * 128:(i + 1) * 128],
                                       vB_xn1[:, m, (g * 4 + i) * 128:(g * 4 + i + 1) * 128],
                                       c_id_bf[:, :], start=(i == 0), stop=(i == 3))
                    return ins
                step(TN, tpsg, reads=[arB, c_id_bf], writes=[ptr])
                step(SC if ptr is PTRa else VE, lambda e, m=m, g=g, ptr=ptr: (e.copy if ptr is PTRa else e.tensor_copy)(
                    vA_sgtm[:, g * 4:(g + 1) * 4, m * 128:(m + 1) * 128],
                    r3(ptr.ap, 128)), reads=[ptr], writes=[arA])
        for tt in range(NT):
            r0 = ss * SS + tt * 128 + 128
            dma(GP, b_u[:, :], u_d[r0:r0 + 128, :], writes=[b_u])
            step(VE, lambda e, tt=tt: e.tensor_add(b_x1[:, :], b_u[:, :],
                 vA_sgtm[:, tt, :]), reads=[b_u, arA], writes=[b_x1])
            step(SC, lambda e: e.activation(b_x1c[:, :], b_x1[:, :], AF.Copy),
                 reads=[b_x1], writes=[b_x1c])
            dma(SC, sc_x1[ss * SS + tt * 128: ss * SS + (tt + 1) * 128, :],
                b_x1c[:, :], reads=[b_x1c], writes=[scr_x1])
            step(SC, lambda e: e.activation(b_dump[:, :], b_x1[:, :], AF.Copy,
                 accum_out=b_st1[:, :]), reads=[b_x1], writes=[b_dump, b_st1])
            step(SC, lambda e: e.activation(b_dump[:, :], b_x1[:, :], AF.Square,
                 accum_out=b_st2[:, :]), reads=[b_x1], writes=[b_dump, b_st2])
            step(VE, lambda e: e.tensor_scalar(b_m[:, :], b_st1[:, :], 1.0 / D,
                 None, ALU.mult), reads=[b_st1], writes=[b_m])
            step(VE, lambda e: e.tensor_scalar(b_sc1[:, :], b_st2[:, :], 1.0 / D,
                 None, ALU.mult), reads=[b_st2], writes=[b_sc1])
            step(VE, lambda e: e.tensor_mul(b_sc2[:, :], b_m[:, :], b_m[:, :]),
                 reads=[b_m], writes=[b_sc2])
            step(VE, lambda e: e.tensor_sub(b_sc1[:, :], b_sc1[:, :], b_sc2[:, :]),
                 reads=[b_sc1, b_sc2], writes=[b_sc1])
            step(SC, lambda e: e.activation(b_sc2[:, :], b_sc1[:, :], AF.Sqrt,
                 bias=c_epsln[:, :]), reads=[b_sc1, c_epsln], writes=[b_sc2])
            step(VE, lambda e: e.reciprocal(b_r[:, :], b_sc2[:, :]),
                 reads=[b_sc2], writes=[b_r])
            step(VE, lambda e: e.tensor_scalar(b_xn[:, :], b_x1[:, :], b_m[:, :],
                 b_r[:, :], ALU.subtract, ALU.mult),
                 reads=[b_x1, b_m, b_r], writes=[b_xn])
            for g in range(4):
                ptr = PTRa if g % 2 == 0 else PTRb
                def tp1(e, g=g, ptr=ptr):
                    for i in range(4):
                        ins = e.matmul(ptr[:, i * 128:(i + 1) * 128],
                                       b_xn[:, (g * 4 + i) * 128:(g * 4 + i + 1) * 128],
                                       c_id_bf[:, :], start=(i == 0), stop=(i == 3))
                    return ins
                step(TN, tp1, reads=[b_xn, c_id_bf], writes=[ptr])
                step(SC if ptr is PTRa else VE, lambda e, g=g, tt=tt, ptr=ptr: (e.copy if ptr is PTRa else e.tensor_copy)(
                    vB_xn1[:, g * 4:(g + 1) * 4, tt * 128:(tt + 1) * 128],
                    r3(ptr.ap, 128)), reads=[ptr], writes=[arB])
        pjoin()
        for j in range(KD):
            for kind in range(3):
                wb = b_w0 if kind % 2 == 0 else b_w1
                pp = (BA, BB, BA)[kind] if j % 2 == 0 else (BB, BA, BB)[kind]
                col = kind * D + j * 128
                dma(SY, wb[:, 0:2048],
                    m_wqkvT[:, (kind * KD + j) * 2048:(kind * KD + j + 1) * 2048],
                    reads=[w3cs[(kind * KD + j) // 2]], writes=[wb])
                def mmq(e, wb=wb, pp=pp):
                    for k in range(KD):
                        for n0 in (0, 512):
                            ins = e.matmul(pp[:, n0:n0 + 512],
                                           wb[:, k * 128:(k + 1) * 128],
                                           vB_xn1[:, k, n0:n0 + 512],
                                           start=(k == 0), stop=(k == KD - 1))
                    return ins
                step(TN, mmq, reads=[wb, arB], writes=[pp])
                if kind < 2:
                    step(SC, lambda e, pp=pp: e.activation(b_e[:, :], pp[:, 0:1024],
                         AF.Exp), reads=[pp], writes=[b_e])
                    step(VE, lambda e: e.tensor_scalar(b_ecl[:, :], b_e[:, :],
                         1.0, None, ALU.min), reads=[b_e], writes=[b_ecl])
                    dstb = b_qphi if kind == 0 else b_kphi
                    step(VE, lambda e, dstb=dstb, pp=pp: e.scalar_tensor_tensor(
                         dstb[:, :], pp[:, 0:1024], 0.0, b_ecl[:, :],
                         ALU.max, ALU.add), reads=[pp, b_ecl], writes=[dstb])
                    if kind == 0:
                        dma(SC, sc_q[j * 128:(j + 1) * 128, ss * SS:(ss + 1) * SS],
                            b_qphi[:, :], reads=[b_qphi], writes=[scr_q])
                else:
                    step(VE, lambda e, pp=pp: e.tensor_tensor(b_kv[:, :],
                         b_kphi[:, :], pp[:, 0:1024], ALU.mult),
                         reads=[b_kphi, pp], writes=[b_kv])
                    step(VE, lambda e, j=j: e.tensor_tensor_scan(b_kvc[:, :],
                         b_kv[:, :], b_kv[:, :], b_ckv[:, j:j + 1], ALU.add,
                         ALU.bypass), reads=[b_kv, b_ckv], writes=[b_kvc])
                    dma(GP, sc_kvc[j * 128:(j + 1) * 128, ss * SS:(ss + 1) * SS],
                        b_kvc[:, :], reads=[b_kvc], writes=[scr_kvc])
                    step(VE, lambda e, j=j: e.tensor_copy(b_ckv[:, j:j + 1],
                         b_kvc[:, 1023:1024]), reads=[b_kvc], writes=[b_ckv])
                    step(VE, lambda e, j=j: e.tensor_tensor_scan(b_kvc[:, :],
                         b_kphi[:, :], b_kphi[:, :], b_ckc[:, j:j + 1], ALU.add,
                         ALU.bypass), reads=[b_kphi, b_ckc], writes=[b_kvc])
                    dma(GP, sc_kc[j * 128:(j + 1) * 128, ss * SS:(ss + 1) * SS],
                        b_kvc[:, :], reads=[b_kvc], writes=[scr_kc])
                    step(VE, lambda e, j=j: e.tensor_copy(b_ckc[:, j:j + 1],
                         b_kvc[:, 1023:1024]), reads=[b_kvc], writes=[b_ckc])
        for m in range(KD):
            wb = b_w0 if m % 2 == 0 else b_w1
            pp = BA if m % 2 == 0 else BB
            g0t = b_g0t if m % 2 == 0 else b_ca1
            g0b = b_g0b if m % 2 == 0 else b_e
            dma(SY, wb[:, 0:2048],
                sc_qwT[:, m * 2048:(m + 1) * 2048],
                reads=[qwms[m]], writes=[wb])
            def mmg1(e, wb=wb, pp=pp):
                for k in range(KD):
                    for n0 in (0, 512):
                        ins = e.matmul(pp[:, n0:n0 + 512],
                                       wb[:, k * 128:(k + 1) * 128],
                                       vB_xn1[:, k, n0:n0 + 512],
                                       start=(k == 0), stop=(k == KD - 1))
                return ins
            step(TN, mmg1, reads=[wb, arB], writes=[pp])
            step(VE, lambda e, m=m, pp=pp, g0t=g0t: e.tensor_scalar(g0t[:, :],
                 pp[:, 0:1024], c_f1b[:, m:m + 1], None, ALU.add),
                 reads=[pp, c_f1b], writes=[g0t])
            step(SC, lambda e, g0t=g0t, g0b=g0b: e.activation(g0b[:, :],
                 g0t[:, :], AF.Silu), reads=[g0t], writes=[g0b])
            dma(SC, sc_g1[m * 128:(m + 1) * 128, ss * SS:(ss + 1) * SS],
                g0b[:, :], reads=[g0b], writes=[scr_g1])
        pjoin()

    for ss in range(NSS):
        pass2_ss(ss)

    dma(SY, ag2_i[0:1, :].rearrange("r (p m) -> (r p) m", m=KD), b_ckc[:, :],
        reads=[b_ckc], writes=[scr_ag])
    dma(SY, ag2_i[1:2, :].rearrange("r (p m) -> (r p) m", m=KD), b_ckv[:, :],
        reads=[b_ckv], writes=[scr_ag])
    step(GP, lambda e: e.collective_compute(
        "AllGather", ALU.bypass,
        replica_groups=[[0, 1, 2, 3], [4, 5, 6, 7]],
        ins=[ag2_i.opt()], outs=[ag2_o.opt()]),
        reads=[scr_ag], writes=[scr_ag], inc="cc")
    step(GP, lambda e: e.memset(b_offk[:, :], 0.0), writes=[b_offk])
    step(GP, lambda e: e.memset(b_offkv[:, :], 0.0), writes=[b_offkv])
    for i in range(GROUP - 1):
        for row, dst in ((2 * i, b_offk), (2 * i + 1, b_offkv)):
            dma(SY, b_gk[:, :],
                ag2_o[row:row + 1, :].rearrange("r (p m) -> (r p) m", m=KD),
                reads=[scr_ag], writes=[b_gk])
            step(VE, lambda e, i=i, dst=dst: e.scalar_tensor_tensor(dst[:, :],
                 b_gk[:, :], c_m8[:, i:i + 1], dst[:, :], ALU.mult, ALU.add),
                 reads=[b_gk, c_m8, dst], writes=[dst])

    def pass3_ss(ss):
        pjoin()
        vao = vB_ao if ss % 2 == 0 else vA_ao
        aobuf = arB if ss % 2 == 0 else arA_hi
        n1B = Buf(arA.ap[:, 16384:18432].bitcast(F32))
        d1B = Buf(arA.ap[:, 18432:20480].bitcast(F32))
        qphiB = Buf(arA.ap[:, 20480:21504])
        kphiB = Buf(arA.ap[:, 21504:22528])
        eB = Buf(arA.ap[:, 22528:23552])
        for m in range(KD):
            par1 = (ss == 0 and m % 2 == 1)
            qb = qphiB if par1 else b_qphi
            kb = kphiB if par1 else b_kphi
            ebf = eB if par1 else b_e
            n1 = n1B if par1 else b_n1
            d1 = d1B if par1 else b_d1
            dma(SY, qb[:, :], sc_q[m * 128:(m + 1) * 128, ss * SS:(ss + 1) * SS],
                reads=[scr_q], writes=[qb])
            dma(GP, kb[:, :], sc_kvc[m * 128:(m + 1) * 128, ss * SS:(ss + 1) * SS],
                reads=[scr_kvc], writes=[kb])
            dma(GP, ebf[:, :], sc_kc[m * 128:(m + 1) * 128, ss * SS:(ss + 1) * SS],
                reads=[scr_kc], writes=[ebf])
            step(VE, lambda e, m=m, n1=n1, kb=kb, qb=qb: e.scalar_tensor_tensor(
                 n1[:, :], kb[:, :], b_offkv[:, m:m + 1], qb[:, :],
                 ALU.add, ALU.mult), reads=[kb, b_offkv, qb], writes=[n1])
            step(VE, lambda e, m=m, d1=d1, ebf=ebf, qb=qb: e.scalar_tensor_tensor(
                 d1[:, :], ebf[:, :], b_offk[:, m:m + 1], qb[:, :],
                 ALU.add, ALU.mult), reads=[ebf, b_offk, qb], writes=[d1])
            step(SC, lambda e, d1=d1: e.activation(d1[:, :], d1[:, :], AF.Ln,
                 bias=c_epsl1[:, :]), reads=[d1, c_epsl1], writes=[d1])
            step(SC, lambda e, d1=d1: e.activation(d1[:, :], d1[:, :], AF.Exp,
                 scale=-1.0), reads=[d1], writes=[d1])
            step(VE, lambda e, m=m, vao=vao, n1=n1, d1=d1: e.tensor_tensor(
                 vao[:, m, :], n1[:, :], d1[:, :], ALU.mult),
                 reads=[n1, d1], writes=[aobuf])
        for m in range(KD):
            wb = b_w0 if m % 2 == 0 else b_w1
            pp = BA if m % 2 == 0 else BB
            dma(SY, wb[:, 0:2048],
                m_wout1T[:, m * 2048:(m + 1) * 2048],
                reads=[w4cs[m // 2]], writes=[wb])
            def mms1(e, wb=wb, pp=pp, vao=vao):
                for k in range(KD):
                    for n0 in (0, 512):
                        ins = e.matmul(pp[:, n0:n0 + 512],
                                       wb[:, k * 128:(k + 1) * 128],
                                       vao[:, k, n0:n0 + 512],
                                       start=(k == 0), stop=(k == KD - 1))
                return ins
            step(TN, mms1, reads=[wb, aobuf], writes=[pp])
            step(VE, lambda e, pp=pp: e.tensor_scalar(b_s1[:, :], pp[:, 0:1024],
                 SCL_OUT1, None, ALU.mult), reads=[pp], writes=[b_s1])
            dma(SC, b_g0b[:, :], sc_g1[m * 128:(m + 1) * 128, ss * SS:(ss + 1) * SS],
                reads=[scr_g1], writes=[b_g0b])
            step(VE, lambda e: e.tensor_mul(b_sg1[:, :], b_s1[:, :], b_g0b[:, :]),
                 reads=[b_s1, b_g0b], writes=[b_sg1])
            for g in range(2):
                ptr = PTRa if g % 2 == 0 else PTRb
                def tps1(e, g=g, ptr=ptr):
                    for i in range(4):
                        ins = e.matmul(ptr[:, i * 128:(i + 1) * 128],
                                       b_sg1[:, (g * 4 + i) * 128:(g * 4 + i + 1) * 128],
                                       c_id_bf[:, :], start=(i == 0), stop=(i == 3))
                    return ins
                step(TN, tps1, reads=[b_sg1, c_id_bf], writes=[ptr])
                step(SC if ptr is PTRa else VE, lambda e, m=m, g=g, ptr=ptr: (e.copy if ptr is PTRa else e.tensor_copy)(
                    vA_sgtm[:, g * 4:(g + 1) * 4, m * 128:(m + 1) * 128],
                    r3(ptr.ap, 128)), reads=[ptr], writes=[arA])
        for tt in range(NT):
            t0 = ss * SS + tt * 128
            dma(GP, b_x1c[:, :], sc_x1[t0:t0 + 128, :], reads=[scr_x1],
                writes=[b_x1c])
            step(VE, lambda e, tt=tt: e.tensor_add(b_x2[:, :], b_x1c[:, :],
                 vA_sgtm[:, tt, :]), reads=[b_x1c, arA], writes=[b_x2])
            dma(SC, out_d[t0:t0 + 128, :], b_x2[:, :], reads=[b_x2],
                writes=[scr_ag])

    for ss in range(NSS):
        pass3_ss(ss)

    fin = {n: cx.semv[n] for n in cx.semv if cx.semv[n] > 0}
    cx.do(SY, lambda e: e.nop(), waits=fin)
    cx.finalize()
    return nc


_NC_CACHE = {}


def _build_cached():
    if "nc" not in _NC_CACHE:
        _NC_CACHE["nc"] = build()
    return _NC_CACHE["nc"]


def kernel(L0_attn_u, L0_attn_W_in, L0_attn_W_conv, L0_attn_b_conv,
           L0_attn_dt_bias, L0_attn_A_log, L0_attn_Dp, L0_attn_norm_w,
           L0_attn_W_out, L0_ffn_gamma, L0_ffn_beta, L0_ffn_weight,
           L0_ffn_bias, L0_ffn_alpha, L1_attn_W_qkv, L1_attn_W_out,
           L1_ffn_weight, L1_ffn_bias):
    f32 = np.float32
    u = np.asarray(L0_attn_u, f32)

    def ct(a):
        return np.ascontiguousarray(np.asarray(a, f32).T)

    def strips(W, nk):
        O = W.shape[0]
        ns = O // 128
        E = W[0:ns * 128].reshape(ns, 128, nk, 128)
        out = np.ascontiguousarray(E.transpose(3, 0, 2, 1).reshape(128, ns * nk * 128))
        if O % 128:
            L_ = W[ns * 128:].reshape(O % 128, nk, 128)
            out = np.concatenate(
                [out, np.ascontiguousarray(
                    L_.transpose(2, 1, 0).reshape(128, nk * (O % 128)))], axis=1)
        return np.ascontiguousarray(out)

    winT = strips(np.asarray(L0_attn_W_in, f32), KD)
    woutT = strips(np.asarray(L0_attn_W_out, f32), 32)
    wqkvT = strips(np.asarray(L1_attn_W_qkv, f32), KD)
    wout1T = strips(np.asarray(L1_attn_W_out, f32), KD)
    wf0 = np.ascontiguousarray(np.asarray(L0_ffn_weight, f32))
    wf1 = np.ascontiguousarray(np.asarray(L1_ffn_weight, f32))

    wconv = np.ascontiguousarray(
        np.asarray(L0_attn_W_conv, f32).reshape(34, 128, 4)
        .transpose(1, 0, 2).reshape(128, 136))
    bconv = np.ascontiguousarray(np.asarray(L0_attn_b_conv, f32).reshape(34, 128).T)
    dtb = np.asarray(L0_attn_dt_bias, f32).reshape(H, 1)
    alog = np.asarray(L0_attn_A_log, f32).reshape(H, 1)
    dpcol = np.ascontiguousarray(
        np.broadcast_to(np.asarray(L0_attn_Dp, f32)[None, :], (128, H)))
    normw = np.ascontiguousarray(np.asarray(L0_attn_norm_w, f32).reshape(32, 128).T)
    gamma = np.ascontiguousarray(np.asarray(L0_ffn_gamma, f32).reshape(KD, 128).T)
    beta = np.ascontiguousarray(np.asarray(L0_ffn_beta, f32).reshape(KD, 128).T)
    f0b = np.ascontiguousarray(np.asarray(L0_ffn_bias, f32).reshape(KD, 128).T)
    f0a = np.ascontiguousarray(np.asarray(L0_ffn_alpha, f32).reshape(KD, 128).T)
    f1b = np.ascontiguousarray(np.asarray(L1_ffn_bias, f32).reshape(KD, 128).T)
    ident = np.eye(128, dtype=f32)
    triu = np.triu(np.ones((128, 128), dtype=f32))
    negm = np.where(triu > 0, np.float32(0.0), np.float32(-1e30)).astype(f32)
    ones = np.ones((1, 128), dtype=f32)
    onesf = np.ones((1, 128), dtype=f32)

    nc = _build_cached()
    in_maps = []
    for core in range(NCORES):
        b = core // GROUP
        r = core % GROUP
        t0 = r * TCORE
        if t0 == 0:
            uslice = np.concatenate(
                [np.zeros((128, D), f32), u[b, 0:TCORE, :]], axis=0)
        else:
            uslice = u[b, t0 - 128:t0 + TCORE, :]
        uslice = np.ascontiguousarray(uslice)
        mask8 = np.zeros((128, 8), f32)
        for i in range(GROUP - 1):
            mask8[:, i] = 1.0 if i < r else 0.0
            mask8[:, 3 + i] = 0.0 if i < r else 1.0
        in_maps.append(dict(
            u=uslice, winT=winT, woutT=woutT, wqkvT=wqkvT, wout1T=wout1T,
            wf0=wf0, wf1=wf1, wconv=wconv, bconv=bconv, dtb=dtb, alog=alog,
            dpcol=dpcol, normw=normw, gamma=gamma, beta=beta, f0b=f0b, negm=negm,
            f0a=f0a, f1b=f1b, ident=ident, triu=triu, mask8=mask8, ones=ones,
            onesf=onesf))

    res = run_bass_kernel_spmd(nc, in_maps, core_ids=list(range(NCORES)))
    out = np.empty((B, L, D), dtype=f32)
    for core in range(NCORES):
        b = core // GROUP
        t0 = (core % GROUP) * TCORE
        out[b, t0:t0 + TCORE, :] = res.results[core]["out"]
    return out
